# version 12
# baseline (speedup 1.0000x reference)
import contextlib
import numpy as np
import concourse.bass as bass
import concourse.mybir as mybir

F32 = mybir.dt.float32
BF16 = mybir.dt.bfloat16
AF = mybir.ActivationFunctionType
ALU = mybir.AluOpType
AX = mybir.AxisListType

CH = 512


class Ctx:
    def __init__(self, nc, es):
        self.nc = nc
        self.es = es
        self.sem_es = es
        self.sems = {}
        self.cnt = {}
        self.marks = {}
        self.sb = {}
        self.ps = {}

    def sem(self, name):
        if name not in self.sems:
            self.sems[name] = self.sem_es.enter_context(self.nc.semaphore(name))
            self.cnt[name] = 0
        return self.sems[name]

    def inc(self, instr, name, k=1):
        s = self.sem(name)
        self.cnt[name] += k
        instr.then_inc(s, k)
        return self.cnt[name]

    def n(self, name):
        self.sem(name)
        return self.cnt[name]

    def wait(self, eng, name, val=None):
        s = self.sem(name)
        eng.wait_ge(s, self.cnt[name] if val is None else val)

    def sbuf(self, name, shape, dtype):
        t = self.es.enter_context(self.nc.sbuf_tensor(name, shape, dtype))
        self.sb[name] = t
        return t

    def psum(self, name, shape, dtype=F32):
        t = self.es.enter_context(self.nc.psum_tensor(name, shape, dtype))
        self.ps[name] = t
        return t



def emit_blockA(ctx, block, T, xin, xout, W):
    nc = ctx.nc
    NCH = T // CH
    DT = 8
    ET = 16

    xin_r = xin.rearrange("(t p) s -> p t s", p=128)
    xout_r = xout.rearrange("(t p) s -> p t s", p=128)
    w_in_r = W["a_w_inT"].rearrange("(t p) f -> p t f", p=128)
    w_c_r = W["a_w_cT"].rearrange("(t p) f -> p t f", p=128)
    w_d_r = W["a_w_dT"].rearrange("(t p) f -> p t f", p=128)
    w_out_r = W["a_w_outT"].rearrange("(t p) f -> p t f", p=128)

    x_sb2 = ctx.sbuf("a_x2", [128, 2, DT, CH], F32)
    mean_sb2 = ctx.sbuf("a_mean2", [1, 2, CH], F32)
    var_sb2 = ctx.sbuf("a_var2", [1, 2, CH], F32)
    tmprow2 = ctx.sbuf("a_tmprow2", [1, 2, CH], F32)
    mb_sb2 = ctx.sbuf("a_mb2", [128, 2, CH], F32)
    invb_sb2 = ctx.sbuf("a_invb2", [128, 2, CH], F32)
    tmp_ln = ctx.sbuf("a_tmpln", [128, 2, CH], F32)
    tmp_po = ctx.sbuf("a_tmppo", [128, 2, CH], F32)
    sq_ln = ctx.sbuf("a_sqln", [128, 2, CH], F32)
    ln1_bf2 = ctx.sbuf("a_ln12", [128, 2, DT, CH], BF16)
    NSB = 3
    ws_in = ctx.sbuf("a_ws_in", [128, NSB, DT, 128], BF16)
    ws_c = ctx.sbuf("a_ws_c", [128, 2, ET, 128], BF16)
    ws_d = ctx.sbuf("a_ws_d", [128, 2, ET, 128], BF16)
    w_out_sb = ctx.sbuf("a_w_out_sb", [128, ET, 1024], BF16)
    xp_ext = ctx.sbuf("a_xpext", [128, 2, CH + 3], F32)
    acc_sb = ctx.sbuf("a_acc", [128, 2, CH], F32)
    sq = acc_sb
    carry_sb = ctx.sbuf("a_carry", [128, ET, 3], F32)
    xc_bf = ctx.sbuf("a_xc", [128, ET, CH], BF16)
    sg_bf = ctx.sbuf("a_sg", [128, ET, CH], BF16)
    sd_sb = ctx.sbuf("a_sd", [128, 2, CH], F32)
    u_sb = ctx.sbuf("a_u", [128, 2, CH], F32)
    v_sb = ctx.sbuf("a_v", [128, 2, CH], F32)
    vcarry = ctx.sbuf("a_vcarry", [128, ET], F32)
    tg_sb = ctx.sbuf("a_tg", [128, 2, CH], F32)
    abm = tg_sb[:, 0, 0:256].rearrange("p (a b) -> p a b", b=16)
    bm2 = tg_sb[:, 1, 0:256].rearrange("p (a b) -> p a b", b=16)
    gated_bf = ctx.sbuf("a_gated", [128, ET, CH], BF16)
    ones128 = ctx.sbuf("a_ones128", [128, 1], F32)
    ones1 = ctx.sbuf("a_ones1", [1, 128], F32)
    lng = ctx.sbuf("a_lng", [128, DT], F32)
    lnb = ctx.sbuf("a_lnb", [128, DT], F32)
    binxp = ctx.sbuf("a_binxp", [128, ET], F32)
    bing = ctx.sbuf("a_bing", [128, ET], F32)
    convw = ctx.sbuf("a_convw", [128, ET, 4], F32)
    convb = ctx.sbuf("a_convb", [128, ET], F32)
    cfm = ctx.sbuf("a_cfm", [128, ET], F32)
    ld_fm = ctx.sbuf("a_ldfm", [128, ET], F32)
    a_fm = ctx.sbuf("a_afm", [128, ET], F32)
    sp_t1 = ctx.sbuf("a_spt1", [128, ET], F32)
    sp_t2 = sd_sb[:, 0, 0:ET]
    beta = ctx.sbuf("a_beta", [128, ET], F32)
    wsum = ctx.sbuf("a_wsum", [128, ET], F32)
    corr = u_sb[:, 0, 0:ET * 3].rearrange("p (a b) -> p a b", b=3)
    bc_fm = ctx.sbuf("a_bcfm", [128, ET], F32)
    bd_fm = ctx.sbuf("a_bdfm", [128, ET], F32)
    bout_fm = ctx.sbuf("a_bout", [128, DT], F32)

    psA = [ctx.psum("a_psA0", [128, CH]), ctx.psum("a_psA1", [128, CH])]
    psC = [ctx.psum("a_psC0", [128, CH]), ctx.psum("a_psC1", [128, CH])]
    psS4 = [ctx.psum("a_psS0", [128, CH]), ctx.psum("a_psS1", [128, CH]),
            ctx.psum("a_psS2", [128, CH]), ctx.psum("a_psS3", [128, CH])]
    psB1 = psS4[2]
    psB2 = psS4[3]

    @block.sync
    def _(s):
        for name, dst in [
            ("a_lng_fm", lng), ("a_lnb_fm", lnb), ("a_binxp_fm", binxp),
            ("a_bing_fm", bing), ("a_convw_fm", convw), ("a_convb_fm", convb),
            ("a_A_fm", abm), ("a_Bm_fm", bm2), ("a_ld_fm", ld_fm),
            ("a_bc_fm", bc_fm), ("a_bd_fm", bd_fm), ("a_bout_fm", bout_fm),
        ]:
            ctx.inc(s.dma_start(out=dst[:], in_=W[name]), "ld", 16)
    n_param_ld = ctx.n("ld")

    @block.gpsimd
    def _(g):
        ctx.inc(g.memset(ones128[:], 1.0), "pl")
        ctx.inc(g.memset(ones1[:], 1.0), "pl")
        ctx.inc(g.memset(carry_sb[:], 0.0), "pl")
        ctx.inc(g.memset(vcarry[:], 0.0), "pl")
        g.wait_ge(ctx.sem("wqo0"), ctx.n("wqo0"))
        ctx.inc(g.dma_start(out=w_out_sb[:], in_=w_out_r[:]), "wqo0", 16)
    n_init_pl = ctx.n("pl")
    n_wout_ld = ctx.n("wqo0")

    @block.vector
    def _(v):
        v.wait_ge(ctx.sem("ld"), n_param_ld)
        ctx.inc(v.tensor_mul(abm[:], abm[:], bm2[:]), "dv")
        v.wait_ge(ctx.sem("dv"), ctx.n("dv"))
        ctx.inc(v.tensor_reduce(cfm[:].rearrange("p (a b) -> p a b", b=1), abm[:], AX.X, ALU.add), "dv")
        ctx.inc(v.tensor_reduce(wsum[:].rearrange("p (a b) -> p a b", b=1), convw[:], AX.X, ALU.add), "dv")
        v.wait_ge(ctx.sem("dv"), ctx.n("dv"))
        ctx.inc(v.tensor_mul(beta[:], binxp[:], wsum[:]), "dv")
        v.wait_ge(ctx.sem("dv"), ctx.n("dv"))
        ctx.inc(v.tensor_add(beta[:], beta[:], convb[:]), "dv")
        v.wait_ge(ctx.sem("dv"), ctx.n("dv"))
        ctx.inc(v.tensor_copy(corr[:, :, 2], convw[:, :, 0]), "dv")
        v.wait_ge(ctx.sem("dv"), ctx.n("dv"))
        ctx.inc(v.tensor_add(corr[:, :, 1], convw[:, :, 0], convw[:, :, 1]), "dv")
        v.wait_ge(ctx.sem("dv"), ctx.n("dv"))
        ctx.inc(v.tensor_add(corr[:, :, 0], corr[:, :, 1], convw[:, :, 2]), "dv")
        v.wait_ge(ctx.sem("dv"), ctx.n("dv"))
        for k in range(3):
            ctx.inc(v.tensor_mul(corr[:, :, k], corr[:, :, k], binxp[:]), "dv")
            v.wait_ge(ctx.sem("dv"), ctx.n("dv"))
        ctx.inc(v.tensor_scalar_mul(corr[:], corr[:], -1.0), "dv")

    @block.scalar
    def _(sc):
        sc.wait_ge(ctx.sem("ld"), n_param_ld)
        ctx.inc(sc.activation(sp_t1[:], ld_fm[:], AF.Abs), "ac")
        sc.wait_ge(ctx.sem("ac"), ctx.n("ac"))
        ctx.inc(sc.activation(sp_t1[:], sp_t1[:], AF.Exp, scale=-1.0), "ac")
        sc.wait_ge(ctx.sem("ac"), ctx.n("ac"))
        ctx.inc(sc.activation(sp_t1[:], sp_t1[:], AF.Ln, bias=1.0), "ac")
    n_sp_ac = ctx.n("ac")

    @block.vector
    def _(v):
        v.wait_ge(ctx.sem("ac"), n_sp_ac)
        ctx.inc(v.tensor_scalar_max(sp_t2[:], ld_fm[:], 0.0), "dv")
        v.wait_ge(ctx.sem("dv"), ctx.n("dv"))
        ctx.inc(v.tensor_add(sp_t2[:], sp_t2[:], sp_t1[:]), "dv")
    n_sp_dv = ctx.n("dv")

    @block.scalar
    def _(sc):
        sc.wait_ge(ctx.sem("dv"), n_sp_dv)
        ctx.inc(sc.activation(a_fm[:], sp_t2[:], AF.Exp, scale=-1.0), "ac")
    n_pre_ac = ctx.n("ac")
    n_pre_dv = ctx.n("dv")

    @block.tensor
    def _(t):
        t.wait_ge(ctx.sem("pl"), n_init_pl)
        t.wait_ge(ctx.sem("dv"), n_pre_dv)
        t.wait_ge(ctx.sem("ac"), n_pre_ac)

    slotA = [0, 0, 0]
    slotCD = [0, 0, 0]
    slotO = [0, 0]

    def emit_lnphase(c):
        pp = c % 2
        s0 = c * CH
        xs = x_sb2[:, pp]
        pS, pQ = psS4[2 * pp], psS4[2 * pp + 1]

        @block.sync
        def _(s):
            if c >= 2:
                s.wait_ge(ctx.sem("st2"), ctx.marks[("a_st2", c - 2)])
            s.wait_ge(ctx.sem("ld"), ctx.n("ld"))
            ctx.inc(s.dma_start(out=xs[:], in_=xin_r[:, :, s0:s0 + CH]), "ld", 16)
        n_xld = ctx.n("ld")

        pl_b = ctx.n("pl")

        @block.gpsimd
        def _(g):
            g.wait_ge(ctx.sem("ld"), n_xld)
            g.wait_ge(ctx.sem("pe"), ctx.marks.get("a_sqln_free", 0))
            for dt in range(2):
                ctx.inc(g.tensor_mul(sq_ln[:, dt, :], xs[:, dt, :], xs[:, dt, :]), "pl")
        for dt in range(DT):
            n_sqrdy = pl_b + 2 if dt < 2 else ctx.n("pl")

            @block.tensor
            def _(t):
                t.wait_ge(ctx.sem("pl"), n_sqrdy)
                if dt == 0:
                    t.wait_ge(ctx.sem("dv"), ctx.marks.get(("a_psS_rd", pp), 0))
                t.matmul(pS[0:1, :], ones128[:], xs[:, dt, :], start=(dt == 0), stop=(dt == DT - 1))
                ins = t.matmul(pQ[0:1, :], ones128[:], sq_ln[:, dt % 2, :], start=(dt == 0), stop=(dt == DT - 1))
                ctx.inc(ins, "pe")
            n_pe_dt = ctx.n("pe")
            if dt + 2 < DT:
                @block.gpsimd
                def _(g):
                    g.wait_ge(ctx.sem("pe"), n_pe_dt)
                    ctx.inc(g.tensor_mul(sq_ln[:, dt % 2, :], xs[:, dt + 2, :], xs[:, dt + 2, :]), "pl")
        n_stmm = ctx.n("pe")
        ctx.marks["a_sqln_free"] = n_stmm

        @block.vector
        def _(v):
            v.wait_ge(ctx.sem("pe"), n_stmm)
            ctx.inc(v.tensor_scalar(mean_sb2[:, pp, :], pS[0:1, :], 1.0 / 1024.0, None, ALU.mult), "dv")
            v.wait_ge(ctx.sem("dv"), ctx.n("dv"))
            ctx.inc(v.tensor_scalar(var_sb2[:, pp, :], pQ[0:1, :], 1.0 / 1024.0, 1e-5, ALU.mult, ALU.add), "dv")
            v.wait_ge(ctx.sem("dv"), ctx.n("dv"))
            ctx.inc(v.tensor_mul(tmprow2[:, pp, :], mean_sb2[:, pp, :], mean_sb2[:, pp, :]), "dv")
            v.wait_ge(ctx.sem("dv"), ctx.n("dv"))
            ctx.inc(v.tensor_sub(var_sb2[:, pp, :], var_sb2[:, pp, :], tmprow2[:, pp, :]), "dv")
        n_var = ctx.n("dv")

        @block.scalar
        def _(sc):
            sc.wait_ge(ctx.sem("dv"), n_var)
            ctx.inc(sc.activation(var_sb2[:, pp, :], var_sb2[:, pp, :], AF.Ln), "ac")
            sc.wait_ge(ctx.sem("ac"), ctx.n("ac"))
            ctx.inc(sc.activation(var_sb2[:, pp, :], var_sb2[:, pp, :], AF.Exp, scale=-0.5), "ac")
        n_inv = ctx.n("ac")

        @block.tensor
        def _(t):
            t.wait_ge(ctx.sem("ac"), n_inv)
            t.matmul(pS[:], ones1[:], mean_sb2[:, pp, :], start=True, stop=True)
            ins = t.matmul(pQ[:], ones1[:], var_sb2[:, pp, :], start=True, stop=True)
            ctx.inc(ins, "pe")
        n_bc = ctx.n("pe")

        @block.vector
        def _(v):
            v.wait_ge(ctx.sem("pe"), n_bc)
            ctx.inc(v.tensor_copy(mb_sb2[:, pp, :], pS[:]), "dv")
            ctx.inc(v.tensor_copy(invb_sb2[:, pp, :], pQ[:]), "dv")
        n_bcc = ctx.n("dv")
        ctx.marks[("a_psS_rd", pp)] = n_bcc

        dv_base = ctx.n("dv")

        @block.gpsimd
        def _(g):
            g.wait_ge(ctx.sem("dv"), n_bcc)
            for dt in range(DT):
                if dt >= 2:
                    g.wait_ge(ctx.sem("dv"), dv_base + 2 * (dt - 2) + 2)
                ctx.inc(g.tensor_sub(tmp_ln[:, dt % 2, :], xs[:, dt, :], mb_sb2[:, pp, :]), "pl")
                ctx.marks[("a_lnsub", dt)] = ctx.n("pl")

        @block.vector
        def _(v):
            for dt in range(DT):
                v.wait_ge(ctx.sem("pl"), ctx.marks[("a_lnsub", dt)])
                ctx.inc(v.scalar_tensor_tensor(tmp_ln[:, dt % 2, :], tmp_ln[:, dt % 2, :], lng[:, dt:dt + 1], invb_sb2[:, pp, :], ALU.mult, ALU.mult), "dv")
                v.wait_ge(ctx.sem("dv"), ctx.n("dv"))
                ctx.inc(v.tensor_scalar(ln1_bf2[:, pp, dt, :], tmp_ln[:, dt % 2, :], lnb[:, dt:dt + 1], None, ALU.add), "dv")
        ctx.marks[("a_ln1_done", c)] = ctx.n("dv")

    def emit_xproj(c):
        pp = c % 2
        ln1_bf = ln1_bf2[:, pp]
        ac_base_x = ctx.n("ac")
        dv_base_x = ctx.n("dv")
        PSX = [psA[0], psA[1], psC[0], psC[1]]
        per_x = 7 + (3 if c == 0 else 0)

        def dv_after_conv(ff):
            return dv_base_x + min(ff + 1, 16) * per_x

        for f in range(32):
            par = f % 4
            slot = f % NSB
            wsem = f"wqa{slot}"

            @block.gpsimd
            def _(g):
                g.wait_ge(ctx.sem("pe"), slotA[slot])
                ctx.inc(g.dma_start(out=ws_in[:, slot, :, :], in_=w_in_r[:, :, f * 128:(f + 1) * 128]), wsem, 16)
            n_wq = ctx.n(wsem)

            @block.tensor
            def _(t):
                t.wait_ge(ctx.sem(wsem), n_wq)
                if f == 0:
                    t.wait_ge(ctx.sem("dv"), ctx.marks[("a_ln1_done", c)])
                if f >= 4:
                    t.wait_ge(ctx.sem("ac"), ac_base_x + (f - 4) + 1)
                else:
                    sem_name, val = ctx.marks.get(("psfree", id(PSX[par])), ("pe", 0))
                    t.wait_ge(ctx.sem(sem_name), val)
                for dt in range(DT):
                    ins = t.matmul(PSX[par][:], ws_in[:, slot, dt, :], ln1_bf[:, dt, :], start=(dt == 0), stop=(dt == DT - 1))
                ctx.inc(ins, "pe")
            ctx.marks[("a_xmm", f)] = ctx.n("pe")
            slotA[slot] = ctx.n("pe")

            if f < 16:
                @block.vector
                def _(v):
                    v.wait_ge(ctx.sem("pe"), ctx.marks[("a_xmm", f)])
                    if f >= 2:
                        v.wait_ge(ctx.sem("ac"), ac_base_x + (f - 2) + 1)
                    if f < 2:
                        v.wait_ge(ctx.sem("pe"), ctx.marks.get("a_postmm", 0))
                    ctx.inc(v.tensor_copy(xp_ext[:, f % 2, 3:CH + 3], PSX[par][:]), "dv")
                    v.wait_ge(ctx.sem("dv"), ctx.n("dv"))
                    ctx.inc(v.tensor_copy(xp_ext[:, f % 2, 0:3], carry_sb[:, f, :]), "dv")
                    v.wait_ge(ctx.sem("dv"), ctx.n("dv"))
                    ctx.inc(v.tensor_copy(carry_sb[:, f, :], xp_ext[:, f % 2, CH:CH + 3]), "dv")
                    v.wait_ge(ctx.sem("dv"), ctx.n("dv"))
                    ctx.inc(v.tensor_scalar(acc_sb[:, f % 2, :], xp_ext[:, f % 2, 3:CH + 3], convw[:, f, 3:4], None, ALU.mult), "dv")
                    for k, sl in [(2, (2, CH + 2)), (1, (1, CH + 1)), (0, (0, CH))]:
                        v.wait_ge(ctx.sem("dv"), ctx.n("dv"))
                        ctx.inc(v.scalar_tensor_tensor(acc_sb[:, f % 2, :], xp_ext[:, f % 2, sl[0]:sl[1]], convw[:, f, k:k + 1], acc_sb[:, f % 2, :], ALU.mult, ALU.add), "dv")
                    if c == 0:
                        for k in range(3):
                            v.wait_ge(ctx.sem("dv"), ctx.n("dv"))
                            ctx.inc(v.tensor_add(acc_sb[:, f % 2, k:k + 1], acc_sb[:, f % 2, k:k + 1], corr[:, f, k:k + 1]), "dv")
                ctx.marks[("psfree", id(PSX[par]))] = ("dv", dv_base_x + f * per_x + 1)

        @block.scalar
        def _(sc):
            for f in range(16):
                sc.wait_ge(ctx.sem("dv"), dv_after_conv(f))
                ctx.inc(sc.activation(xc_bf[:, f, :], acc_sb[:, f % 2, :], AF.Silu, bias=beta[:, f:f + 1]), "ac")
            for f in range(16, 32):
                sc.wait_ge(ctx.sem("pe"), ctx.marks[("a_xmm", f)])
                ctx.inc(sc.activation(sg_bf[:, f - 16, :], PSX[f % 4][:], AF.Sigmoid, bias=bing[:, f - 16:f - 16 + 1]), "ac")
                ctx.marks[("psfree", id(PSX[f % 4]))] = ("ac", ctx.n("ac"))
        return ctx.n("ac")

    def emit_uphase(c, n_xproj_ac):
        pp = c % 2
        PSU_C = [psA[0], psA[1], psS4[2 * pp]]
        PSU_D = [psC[0], psC[1], psS4[2 * pp + 1]]
        dv_base_u = ctx.n("dv")
        for g_i in range(ET):
            par = g_i % 3
            slot = g_i % 2
            wsc = f"wqc{slot}"
            wsd = f"wqd{slot}"

            @block.gpsimd
            def _(g):
                g.wait_ge(ctx.sem("pe"), slotCD[slot])
                ctx.inc(g.dma_start(out=ws_c[:, slot, :, :], in_=w_c_r[:, :, g_i * 128:(g_i + 1) * 128]), wsc, 16)
                ctx.inc(g.dma_start(out=ws_d[:, slot, :, :], in_=w_d_r[:, :, g_i * 128:(g_i + 1) * 128]), wsd, 16)
            n_wq = ctx.n(wsc)
            n_wqd = ctx.n(wsd)

            @block.tensor
            def _(t):
                t.wait_ge(ctx.sem(wsc), n_wq)
                t.wait_ge(ctx.sem(wsd), n_wqd)
                if g_i == 0:
                    t.wait_ge(ctx.sem("ac"), n_xproj_ac)
                if g_i < 2:
                    sem_name, val = ctx.marks.get(("psfree", id(PSU_C[par])), ("pe", 0))
                    t.wait_ge(ctx.sem(sem_name), val)
                    sem_name, val = ctx.marks.get(("psfree", id(PSU_D[par])), ("pe", 0))
                    t.wait_ge(ctx.sem(sem_name), val)
                elif g_i == 2:
                    t.wait_ge(ctx.sem("dv"), ctx.marks.get(("a_psS_rd", pp), 0))
                else:
                    t.wait_ge(ctx.sem("ac"), ctx.marks.get(("a_usd", g_i - 3), 0))
                    t.wait_ge(ctx.sem("dv"), dv_base_u + 4 * (g_i - 3) + 1)
                for e in range(ET):
                    ins = t.matmul(PSU_C[par][:], ws_c[:, slot, e, :], xc_bf[:, e, :], start=(e == 0), stop=(e == ET - 1))
                ctx.inc(ins, "pe")
                for e in range(ET):
                    ins = t.matmul(PSU_D[par][:], ws_d[:, slot, e, :], xc_bf[:, e, :], start=(e == 0), stop=(e == ET - 1))
                ctx.inc(ins, "pe")
            n_mm = ctx.n("pe")
            slotCD[slot] = n_mm

            @block.scalar
            def _(sc):
                sc.wait_ge(ctx.sem("pe"), n_mm)
                if g_i >= 2:
                    sc.wait_ge(ctx.sem("dv"), dv_base_u + 4 * (g_i - 2) + 1)
                ctx.inc(sc.activation(sd_sb[:, g_i % 2, :], PSU_D[par][:], AF.Sigmoid, bias=bd_fm[:, g_i:g_i + 1]), "ac")
            n_sd = ctx.n("ac")
            ctx.marks[("a_usd", g_i)] = n_sd

            @block.vector
            def _(v):
                v.wait_ge(ctx.sem("ac"), n_sd)
                ctx.inc(v.scalar_tensor_tensor(u_sb[:, g_i % 2, :], PSU_C[par][:], bc_fm[:, g_i:g_i + 1], sd_sb[:, g_i % 2, :], ALU.add, ALU.mult), "dv")
                v.wait_ge(ctx.sem("dv"), ctx.n("dv"))
                ctx.inc(v.tensor_tensor_scan(v_sb[:, g_i % 2, :], a_fm[:, g_i:g_i + 1].broadcast_to([128, CH]), u_sb[:, g_i % 2, :], vcarry[:, g_i:g_i + 1], ALU.mult, ALU.add), "dv")
                v.wait_ge(ctx.sem("dv"), ctx.n("dv"))
                ctx.inc(v.tensor_copy(vcarry[:, g_i:g_i + 1], v_sb[:, g_i % 2, CH - 1:CH]), "dv")
                ctx.inc(v.scalar_tensor_tensor(gated_bf[:, g_i, :], v_sb[:, g_i % 2, :], cfm[:, g_i:g_i + 1], sg_bf[:, g_i, :], ALU.mult, ALU.mult), "dv")
        return ctx.n("dv")

    def emit_outpost(c, n_gated):
        pp = c % 2
        s0 = c * CH
        xs = x_sb2[:, pp]
        pS, pQ = psS4[2 * pp], psS4[2 * pp + 1]

        @block.gpsimd
        def _(g):
            g.wait_ge(ctx.sem("dv"), n_gated)
            for dt in range(DT):
                ctx.inc(g.tensor_scalar_mul(xs[:, dt, :], xs[:, dt, :], 2.0), "pl")
        n_dbl = ctx.n("pl")

        for dt in range(DT):
            par = dt % 2

            @block.tensor
            def _(t):
                if dt == 0:
                    t.wait_ge(ctx.sem("dv"), n_gated)
                    if c == 0:
                        t.wait_ge(ctx.sem("wqo0"), n_wout_ld)
                if dt >= 2:
                    t.wait_ge(ctx.sem("dv"), ctx.marks[("a_resid", dt - 2)])
                else:
                    sem_name, val = ctx.marks.get(("psfree", id(psA[par])), ("pe", 0))
                    t.wait_ge(ctx.sem(sem_name), val)
                for g_i in range(ET):
                    ins = t.matmul(psA[par][:], w_out_sb[:, g_i, dt * 128:(dt + 1) * 128], gated_bf[:, g_i, :], start=(g_i == 0), stop=(g_i == ET - 1))
                ctx.inc(ins, "pe")
            n_mm = ctx.n("pe")

            @block.vector
            def _(v):
                v.wait_ge(ctx.sem("pe"), n_mm)
                v.wait_ge(ctx.sem("pl"), n_dbl)
                ctx.inc(v.scalar_tensor_tensor(xs[:, dt, :], psA[par][:], bout_fm[:, dt:dt + 1], xs[:, dt, :], ALU.add, ALU.add), "dv")
            ctx.marks[("a_resid", dt)] = ctx.n("dv")
            ctx.marks[("psfree", id(psA[par]))] = ("dv", ctx.n("dv"))
        n_resid = ctx.n("dv")

        pl_b = ctx.n("pl")

        @block.gpsimd
        def _(g):
            g.wait_ge(ctx.sem("dv"), n_resid)
            ctx.wait(g, "ac")
            for dt in range(2):
                ctx.inc(g.tensor_mul(sq[:, dt, :], xs[:, dt, :], xs[:, dt, :]), "pl")
        for dt in range(DT):
            n_sqrdy = pl_b + 2 if dt < 2 else ctx.n("pl")

            @block.tensor
            def _(t):
                t.wait_ge(ctx.sem("pl"), n_sqrdy)
                if dt == 0:
                    t.wait_ge(ctx.sem("dv"), ctx.marks.get(("a_psS_rd", pp), 0))
                t.matmul(pS[0:1, :], ones128[:], xs[:, dt, :], start=(dt == 0), stop=(dt == DT - 1))
                ins = t.matmul(pQ[0:1, :], ones128[:], sq[:, dt % 2, :], start=(dt == 0), stop=(dt == DT - 1))
                ctx.inc(ins, "pe")
            n_pe_dt = ctx.n("pe")
            if dt + 2 < DT:
                @block.gpsimd
                def _(g):
                    g.wait_ge(ctx.sem("pe"), n_pe_dt)
                    ctx.inc(g.tensor_mul(sq[:, dt % 2, :], xs[:, dt + 2, :], xs[:, dt + 2, :]), "pl")
        n_stmm = ctx.n("pe")
        ctx.marks["a_postmm"] = n_stmm

        @block.vector
        def _(v):
            v.wait_ge(ctx.sem("pe"), n_stmm)
            ctx.inc(v.tensor_scalar(mean_sb2[:, pp, :], pS[0:1, :], 1.0 / 1024.0, None, ALU.mult), "dv")
            v.wait_ge(ctx.sem("dv"), ctx.n("dv"))
            ctx.inc(v.tensor_scalar(var_sb2[:, pp, :], pQ[0:1, :], 1.0 / 1024.0, 1e-5, ALU.mult, ALU.add), "dv")
            v.wait_ge(ctx.sem("dv"), ctx.n("dv"))
            ctx.inc(v.tensor_mul(tmprow2[:, pp, :], mean_sb2[:, pp, :], mean_sb2[:, pp, :]), "dv")
            v.wait_ge(ctx.sem("dv"), ctx.n("dv"))
            ctx.inc(v.tensor_sub(var_sb2[:, pp, :], var_sb2[:, pp, :], tmprow2[:, pp, :]), "dv")
        n_var = ctx.n("dv")

        @block.scalar
        def _(sc):
            sc.wait_ge(ctx.sem("dv"), n_var)
            ctx.inc(sc.activation(var_sb2[:, pp, :], var_sb2[:, pp, :], AF.Ln), "ac")
            sc.wait_ge(ctx.sem("ac"), ctx.n("ac"))
            ctx.inc(sc.activation(var_sb2[:, pp, :], var_sb2[:, pp, :], AF.Exp, scale=-0.5), "ac")
        n_inv = ctx.n("ac")

        @block.tensor
        def _(t):
            t.wait_ge(ctx.sem("ac"), n_inv)
            t.matmul(pS[:], ones1[:], mean_sb2[:, pp, :], start=True, stop=True)
            ins = t.matmul(pQ[:], ones1[:], var_sb2[:, pp, :], start=True, stop=True)
            ctx.inc(ins, "pe")
        n_bc = ctx.n("pe")

        @block.vector
        def _(v):
            v.wait_ge(ctx.sem("pe"), n_bc)
            ctx.inc(v.tensor_copy(mb_sb2[:, pp, :], pS[:]), "dv")
            ctx.inc(v.tensor_copy(invb_sb2[:, pp, :], pQ[:]), "dv")
        n_bcc = ctx.n("dv")
        ctx.marks[("a_psS_rd", pp)] = n_bcc

        dv_base = ctx.n("dv")

        @block.gpsimd
        def _(g):
            g.wait_ge(ctx.sem("dv"), n_bcc)
            for dt in range(DT):
                if dt >= 2:
                    g.wait_ge(ctx.sem("dv"), dv_base + (dt - 2) + 1)
                ctx.inc(g.tensor_sub(tmp_po[:, dt % 2, :], xs[:, dt, :], mb_sb2[:, pp, :]), "pl")
                ctx.marks[("a_posub", dt)] = ctx.n("pl")

        @block.vector
        def _(v):
            for dt in range(DT):
                v.wait_ge(ctx.sem("pl"), ctx.marks[("a_posub", dt)])
                ctx.inc(v.tensor_mul(xs[:, dt, :], tmp_po[:, dt % 2, :], invb_sb2[:, pp, :]), "dv")
        n_post = ctx.n("dv")

        @block.sync
        def _(s):
            s.wait_ge(ctx.sem("dv"), n_post)
            s.wait_ge(ctx.sem("st2"), ctx.n("st2"))
            ctx.inc(s.dma_start(out=xout_r[:, :, s0:s0 + CH], in_=xs[:]), "st2", 16)
        ctx.marks[("a_st2", c)] = ctx.n("st2")

    emit_lnphase(0)
    for c in range(NCH):
        n_xp = emit_xproj(c)
        if c + 1 < NCH:
            emit_lnphase(c + 1)
        n_g = emit_uphase(c, n_xp)
        emit_outpost(c, n_g)

    return ctx.n("st2")


def emit_stats(ctx, block, src_sb, sq, psSum, psSq, ones128, n_ld_wait, DT, extra_pe_wait):
    nc = ctx.nc
    for dt in range(DT):
        @block.scalar
        def _(sc):
            if dt == 0:
                if n_ld_wait is not None:
                    sc.wait_ge(ctx.sem("ld"), n_ld_wait)
                if extra_pe_wait is not None:
                    sc.wait_ge(ctx.sem(extra_pe_wait[0]), extra_pe_wait[1])
            if dt >= 2:
                ctx.wait(sc, "pe")
            ctx.inc(sc.square(sq[:, dt % 2, :], src_sb[:, dt, :]), "ac")
        n_sq = ctx.n("ac")

        @block.tensor
        def _(t):
            t.wait_ge(ctx.sem("ac"), n_sq)
            if dt == 0:
                ctx.wait(t, "dv")
            t.matmul(psSum[0:1, :], ones128[:], src_sb[:, dt, :], start=(dt == 0), stop=(dt == DT - 1))
            ins = t.matmul(psSq[0:1, :], ones128[:], sq[:, dt % 2, :], start=(dt == 0), stop=(dt == DT - 1))
            ctx.inc(ins, "pe")


def emit_stats_final(ctx, block, psSum, psSq, mean_sb, var_sb, tmprow, denom, eps):
    n_pe = ctx.n("pe")

    @block.vector
    def _(v):
        v.wait_ge(ctx.sem("pe"), n_pe)
        ctx.inc(v.tensor_scalar(mean_sb[:], psSum[0:1, :], 1.0 / denom, None, ALU.mult), "dv")
        v.wait_ge(ctx.sem("dv"), ctx.n("dv"))
        ctx.inc(v.tensor_scalar(var_sb[:], psSq[0:1, :], 1.0 / denom, eps, ALU.mult, ALU.add), "dv")
        v.wait_ge(ctx.sem("dv"), ctx.n("dv"))
        ctx.inc(v.tensor_mul(tmprow[:], mean_sb[:], mean_sb[:]), "dv")
        v.wait_ge(ctx.sem("dv"), ctx.n("dv"))
        ctx.inc(v.tensor_sub(var_sb[:], var_sb[:], tmprow[:]), "dv")
    n_var = ctx.n("dv")

    @block.scalar
    def _(sc):
        sc.wait_ge(ctx.sem("dv"), n_var)
        ctx.inc(sc.activation(var_sb[:], var_sb[:], AF.Ln), "ac")
        sc.wait_ge(ctx.sem("ac"), ctx.n("ac"))
        ctx.inc(sc.activation(tmprow[:], var_sb[:], AF.Exp, scale=-0.5), "ac")
    n_std = ctx.n("ac")


def emit_bcast2(ctx, block, mean_sb, inv_sb, mb_sb, invb_sb, psB1, psB2, ones1):
    n_inv = ctx.n("dv")
    n_inv_ac = ctx.n("ac")

    @block.tensor
    def _(t):
        t.wait_ge(ctx.sem("dv"), n_inv)
        t.wait_ge(ctx.sem("ac"), n_inv_ac)
        t.matmul(psB1[:], ones1[:], mean_sb[:], start=True, stop=True)
        ins = t.matmul(psB2[:], ones1[:], inv_sb[:], start=True, stop=True)
        ctx.inc(ins, "pe")
    n_bc = ctx.n("pe")

    @block.vector
    def _(v):
        v.wait_ge(ctx.sem("pe"), n_bc)
        ctx.inc(v.tensor_copy(mb_sb[:], psB1[:]), "dv")
        ctx.inc(v.tensor_copy(invb_sb[:], psB2[:]), "dv")



def blockA_host_inputs(p):
    def fm(vec, ntiles):
        return np.ascontiguousarray(vec.reshape(ntiles, 128).T)

    def fm2(mat, ntiles, k):
        return np.ascontiguousarray(mat.reshape(ntiles, 128, k).transpose(1, 0, 2))

    out = {
        "a_w_inT": np.ascontiguousarray(p["W_in"].T),
        "a_w_cT": np.ascontiguousarray(p["W_c"].T),
        "a_w_dT": np.ascontiguousarray(p["W_d"].T),
        "a_w_outT": np.ascontiguousarray(p["W_out"].T),
        "a_lng_fm": fm(p["ln_g"], 8),
        "a_lnb_fm": fm(p["ln_b"], 8),
        "a_binxp_fm": fm(p["b_in"][:2048], 16),
        "a_bing_fm": fm(p["b_in"][2048:], 16),
        "a_convw_fm": fm2(p["conv_w"][:, 0, :], 16, 4),
        "a_convb_fm": fm(p["conv_b"], 16),
        "a_A_fm": fm2(p["A"], 16, 16),
        "a_Bm_fm": fm2(p["Bm"], 16, 16),
        "a_ld_fm": fm(p["log_delta"], 16),
        "a_bc_fm": fm(p["b_c"], 16),
        "a_bd_fm": fm(p["b_d"], 16),
        "a_bout_fm": fm(p["b_out"], 8),
    }
    return out


def declare_blockA_drams(nc, T):
    F = mybir.dt.float32
    shapes = {
        "a_w_inT": [1024, 4096], "a_w_cT": [2048, 2048], "a_w_dT": [2048, 2048],
        "a_w_outT": [2048, 1024],
        "a_lng_fm": [128, 8], "a_lnb_fm": [128, 8],
        "a_binxp_fm": [128, 16], "a_bing_fm": [128, 16],
        "a_convw_fm": [128, 16, 4], "a_convb_fm": [128, 16],
        "a_A_fm": [128, 16, 16], "a_Bm_fm": [128, 16, 16],
        "a_ld_fm": [128, 16], "a_bc_fm": [128, 16], "a_bd_fm": [128, 16],
        "a_bout_fm": [128, 8],
    }
    return {k: nc.dram_tensor(k, v, F, kind="ExternalInput").ap() for k, v in shapes.items()}




def emit_barrier(ctx, block, dep_sem, dep_val):
    s = ctx.sem(dep_sem)

    @block.sync
    def _(e):
        e.wait_ge(s, dep_val)

    @block.gpsimd
    def _(e):
        e.wait_ge(s, dep_val)

    @block.tensor
    def _(e):
        e.wait_ge(s, dep_val)

    @block.vector
    def _(e):
        e.wait_ge(s, dep_val)

    @block.scalar
    def _(e):
        e.wait_ge(s, dep_val)




def emit_postln_pipelined(ctx, block, T, xin_r, xout_r, acc_r, n_acc_done, pref,
                          xb, tb, sq2, tmp2, mb2, ivb2, mrow2, vrow2, trow2,
                          psS, ones128, ones1):
    NCH = T // CH
    DT = 8

    for c in range(NCH):
        s0 = c * CH
        pp = c % 2
        psSum, psSq = psS[2 * pp], psS[2 * pp + 1]

        @block.sync
        def _(s):
            s.wait_ge(ctx.sem("ld"), ctx.n("ld"))
            s.wait_ge(ctx.sem("st"), n_acc_done)
            if c >= 2:
                s.wait_ge(ctx.sem("st2"), ctx.marks[(pref, "st2", c - 2)])
            ctx.inc(s.dma_start(out=xb[pp][:], in_=xin_r[:, :, s0:s0 + CH]), "ld", 16)
        n_x1 = ctx.n("ld")

        @block.gpsimd
        def _(g):
            g.wait_ge(ctx.sem("wq"), ctx.n("wq"))
            g.wait_ge(ctx.sem("st"), n_acc_done)
            if c >= 2:
                g.wait_ge(ctx.sem("dv"), ctx.marks[(pref, "resid", c - 2)])
            ctx.inc(g.dma_start(out=tb[pp][:], in_=acc_r[:, :, s0:s0 + CH]), "wq", 16)
        n_accld = ctx.n("wq")

        @block.vector
        def _(v):
            v.wait_ge(ctx.sem("ld"), n_x1)
            v.wait_ge(ctx.sem("wq"), n_accld)
            if c >= 2:
                v.wait_ge(ctx.sem("pe"), ctx.marks[(pref, "stmm", c - 2)])
            for dt in range(DT):
                ctx.inc(v.tensor_add(xb[pp][:, dt, :], xb[pp][:, dt, :], tb[pp][:, dt, :]), "dv")
        n_resid = ctx.n("dv")
        ctx.marks[(pref, "resid", c)] = n_resid

        pl_b = ctx.n("pl")

        @block.gpsimd
        def _(g):
            g.wait_ge(ctx.sem("dv"), n_resid)
            for dt in range(2):
                ctx.inc(g.tensor_mul(sq2[pp][:, dt, :], xb[pp][:, dt, :], xb[pp][:, dt, :]), "pl")
        for dt in range(DT):
            n_sqrdy = pl_b + 2 if dt < 2 else ctx.n("pl")

            @block.tensor
            def _(t):
                t.wait_ge(ctx.sem("pl"), n_sqrdy)
                if dt == 0:
                    t.wait_ge(ctx.sem("dv"), ctx.marks.get((pref, "statrd", c - 2), 0) if c >= 2 else 0)
                t.matmul(psSum[0:1, :], ones128[:], xb[pp][:, dt, :], start=(dt == 0), stop=(dt == DT - 1))
                ins = t.matmul(psSq[0:1, :], ones128[:], sq2[pp][:, dt % 2, :], start=(dt == 0), stop=(dt == DT - 1))
                ctx.inc(ins, "pe")
            n_pe_dt = ctx.n("pe")
            if dt + 2 < DT:
                @block.gpsimd
                def _(g):
                    g.wait_ge(ctx.sem("pe"), n_pe_dt)
                    ctx.inc(g.tensor_mul(sq2[pp][:, dt % 2, :], xb[pp][:, dt + 2, :], xb[pp][:, dt + 2, :]), "pl")
        n_stmm = ctx.n("pe")
        ctx.marks[(pref, "stmm", c)] = n_stmm

        @block.vector
        def _(v):
            v.wait_ge(ctx.sem("pe"), n_stmm)
            ctx.inc(v.tensor_scalar(mrow2[pp][:], psSum[0:1, :], 1.0 / 1024.0, None, ALU.mult), "dv")
            v.wait_ge(ctx.sem("dv"), ctx.n("dv"))
            ctx.inc(v.tensor_scalar(vrow2[pp][:], psSq[0:1, :], 1.0 / 1024.0, 1e-5, ALU.mult, ALU.add), "dv")
            v.wait_ge(ctx.sem("dv"), ctx.n("dv"))
            ctx.inc(v.tensor_mul(trow2[pp][:], mrow2[pp][:], mrow2[pp][:]), "dv")
            v.wait_ge(ctx.sem("dv"), ctx.n("dv"))
            ctx.inc(v.tensor_sub(vrow2[pp][:], vrow2[pp][:], trow2[pp][:]), "dv")
        n_var = ctx.n("dv")
        ctx.marks[(pref, "statrd", c)] = n_var

        @block.scalar
        def _(sc):
            sc.wait_ge(ctx.sem("dv"), n_var)
            ctx.inc(sc.activation(vrow2[pp][:], vrow2[pp][:], AF.Ln), "ac")
            sc.wait_ge(ctx.sem("ac"), ctx.n("ac"))
            ctx.inc(sc.activation(vrow2[pp][:], vrow2[pp][:], AF.Exp, scale=-0.5), "ac")
        n_inv = ctx.n("ac")

        @block.tensor
        def _(t):
            t.wait_ge(ctx.sem("ac"), n_inv)
            if c >= 2:
                t.wait_ge(ctx.sem("dv"), ctx.marks[(pref, "bcrd", c - 2)])
            t.matmul(psSum[:], ones1[:], mrow2[pp][:], start=True, stop=True)
            ins = t.matmul(psSq[:], ones1[:], vrow2[pp][:], start=True, stop=True)
            ctx.inc(ins, "pe")
        n_bc = ctx.n("pe")

        @block.vector
        def _(v):
            v.wait_ge(ctx.sem("pe"), n_bc)
            ctx.inc(v.tensor_copy(mb2[pp][:], psSum[:]), "dv")
            ctx.inc(v.tensor_copy(ivb2[pp][:], psSq[:]), "dv")
        n_bcc = ctx.n("dv")
        ctx.marks[(pref, "bcrd", c)] = n_bcc

        dv_b = ctx.n("dv")

        @block.gpsimd
        def _(g):
            g.wait_ge(ctx.sem("dv"), n_bcc)
            for dt in range(DT):
                if dt >= 2:
                    g.wait_ge(ctx.sem("dv"), dv_b + (dt - 2) + 1)
                ctx.inc(g.tensor_sub(tmp2[pp][:, dt % 2, :], xb[pp][:, dt, :], mb2[pp][:]), "pl")
                ctx.marks[(pref, "plsub", dt)] = ctx.n("pl")

        @block.vector
        def _(v):
            for dt in range(DT):
                v.wait_ge(ctx.sem("pl"), ctx.marks[(pref, "plsub", dt)])
                ctx.inc(v.tensor_mul(xb[pp][:, dt, :], tmp2[pp][:, dt % 2, :], ivb2[pp][:]), "dv")
        n_post = ctx.n("dv")

        @block.sync
        def _(s):
            s.wait_ge(ctx.sem("dv"), n_post)
            s.wait_ge(ctx.sem("st2"), ctx.n("st2"))
            ctx.inc(s.dma_start(out=xout_r[:, :, s0:s0 + CH], in_=xb[pp][:]), "st2", 16)
        ctx.marks[(pref, "st2", c)] = ctx.n("st2")

    return ctx.n("st2")



def emit_blockB(ctx, block, T, xin, xout, W, acc_hbm):
    nc = ctx.nc
    NCH = T // CH
    DT = 8
    FT = 16
    NE = 8

    xin_r = xin.rearrange("(t p) s -> p t s", p=128)
    xout_r = xout.rearrange("(t p) s -> p t s", p=128)
    acc_r = acc_hbm.rearrange("(t p) s -> p t s", p=128)
    gw_r = W["b_gwT"].rearrange("(t p) e -> p t e", p=128)
    w1_r = W["b_w1T"].rearrange("E (t p) f -> E p t f", p=128)
    w3_r = W["b_w3T"].rearrange("E (t p) f -> E p t f", p=128)
    w2_r = W["b_w2T"].rearrange("E (t p) f -> E p t f", p=128)

    xf_bf = ctx.sbuf("b_xf", [128, DT, T], BF16)
    gwt = ctx.sbuf("b_gwtsb", [128, DT, 8], F32)
    ident = ctx.sbuf("b_identsb", [128, 128], F32)
    sel = ctx.sbuf("b_selsb", [8, NE * 128], BF16)
    e_sb = ctx.sbuf("b_e", [128, 8], F32)
    nm1 = ctx.sbuf("b_nm1", [128, 1], F32)
    ssum = ctx.sbuf("b_ssum", [128, 1], F32)
    rsum = ctx.sbuf("b_rsum", [128, 1], F32)
    probs = ctx.sbuf("b_probs", [128, 8], F32)
    m1p = ctx.sbuf("b_m1p", [128, 1], F32)
    oh0 = ctx.sbuf("b_oh0", [128, 8], F32)
    tnot = ctx.sbuf("b_tnot", [128, 8], F32)
    probs2 = ctx.sbuf("b_probs2", [128, 8], F32)
    m2p = ctx.sbuf("b_m2p", [128, 1], F32)
    oh1 = ctx.sbuf("b_oh1", [128, 8], F32)
    den = ctx.sbuf("b_den", [128, 1], F32)
    rden = ctx.sbuf("b_rden", [128, 1], F32)
    w0c = ctx.sbuf("b_w0c", [128, 1], F32)
    w1c = ctx.sbuf("b_w1c", [128, 1], F32)
    comb = ctx.sbuf("b_comb", [128, 8], F32)
    combT = ctx.sbuf("b_combT", [8, T], BF16)
    w1s = ctx.sbuf("b_w1s", [128, DT, 2048], BF16)
    w3s = ctx.sbuf("b_w3s", [128, DT, 2048], BF16)
    w2s = ctx.sbuf("b_w2s", [128, FT, 1024], BF16)
    s1_sb = ctx.sbuf("b_s1", [128, 2, CH], BF16)
    h_bf = ctx.sbuf("b_h", [128, FT, CH], BF16)
    x1_sb = h_bf[:].bitcast(F32).rearrange("p a b -> p (a b)").rearrange("p (t s) -> p t s", t=DT)
    cb_sb = ctx.sbuf("b_cb", [128, CH], F32)
    t_sb = ctx.sbuf("b_t", [128, DT, CH], F32)
    sq = w3s[:, 0, :].bitcast(F32).rearrange("p (a b) -> p a b", b=CH)
    tmp1 = w3s[:, 1, :].bitcast(F32).rearrange("p (a b) -> p a b", b=CH)
    mb_sb = w1s[:, 0, 0:1024].bitcast(F32)
    invb_sb = w1s[:, 1, 0:1024].bitcast(F32)
    mean_sb = w2s[0:1, 0, :].bitcast(F32)[:, 0:CH]
    var_sb = w2s[0:1, 1, :].bitcast(F32)[:, 0:CH]
    tmprow = w2s[0:1, 2, :].bitcast(F32)[:, 0:CH]
    ones128 = ctx.sbuf("b_ones128", [128, 1], F32)
    ones1 = ctx.sbuf("b_ones1", [1, 128], F32)

    psA = [ctx.psum("b_psA0", [128, CH]), ctx.psum("b_psA1", [128, CH])]
    psC = [ctx.psum("b_psC0", [128, CH]), ctx.psum("b_psC1", [128, CH])]
    psO = [ctx.psum("b_psO0", [128, CH]), ctx.psum("b_psO1", [128, CH])]
    psB = ctx.psum("b_psB", [128, CH])
    psG = ctx.psum("b_psG", [128, 512])
    psT = psG

    @block.sync
    def _(s):
        s.wait_ge(ctx.sem("ld"), ctx.n("ld"))
        ctx.inc(s.dma_start(out=gwt[:], in_=gw_r[:]), "ld", 16)
        ctx.inc(s.dma_start(out=ident[:], in_=W["ident"]), "ld", 16)
    n_param_ld = ctx.n("ld")

    @block.gpsimd
    def _(g):
        g.wait_ge(ctx.sem("wq"), ctx.n("wq"))
        ctx.inc(g.dma_start(out=sel[:], in_=W["b_sel"]), "wq", 16)
        ctx.inc(g.memset(ones128[:], 1.0), "pl")
        ctx.inc(g.memset(ones1[:], 1.0), "pl")
    n_init_pl = ctx.n("pl")
    n_init_wq = ctx.n("wq")

    for c in range(NCH):
        s0 = c * CH

        @block.sync
        def _(s):
            s.wait_ge(ctx.sem("ld"), ctx.n("ld"))
            if c >= 1:
                ctx.wait(s, "pe")
            ctx.inc(s.dma_start(out=x1_sb[:], in_=xin_r[:, :, s0:s0 + CH]), "ld", 16)
        n_xld = ctx.n("ld")

        @block.gpsimd
        def _(g):
            g.wait_ge(ctx.sem("wq"), ctx.n("wq"))
            ctx.inc(g.dma_start(out=xf_bf[:, :, s0:s0 + CH], in_=xin_r[:, :, s0:s0 + CH]), "wq", 16)
        n_xfld = ctx.n("wq")

        for st in range(4):
            @block.tensor
            def _(t):
                t.wait_ge(ctx.sem("ld"), n_xld)
                if c == 0 and st == 0:
                    t.wait_ge(ctx.sem("pl"), n_init_pl)
                    t.wait_ge(ctx.sem("ld"), n_param_ld)
                if c >= 1 or st >= 1:
                    ctx.wait(t, "dv")
                for dt in range(DT):
                    ins = t.matmul(psG[:, st * 8:(st + 1) * 8], x1_sb[:, dt, st * 128:(st + 1) * 128], gwt[:, dt, :], start=(dt == 0), stop=(dt == DT - 1))
                ctx.inc(ins, "pe")
            n_gmm = ctx.n("pe")

            @block.vector
            def _(v):
                v.wait_ge(ctx.sem("pe"), n_gmm)
                ctx.inc(v.tensor_reduce(nm1[:].rearrange("p (a b) -> p a b", b=1), psG[:, st * 8:(st + 1) * 8].rearrange("p (a b) -> p a b", b=8), AX.X, ALU.max, negate=True), "dv")
            n_nm1 = ctx.n("dv")

            @block.scalar
            def _(sc):
                sc.wait_ge(ctx.sem("dv"), n_nm1)
                ctx.inc(sc.activation(e_sb[:], psG[:, st * 8:(st + 1) * 8], AF.Exp, bias=nm1[:, 0:1]), "ac")
            n_exp = ctx.n("ac")

            @block.vector
            def _(v):
                v.wait_ge(ctx.sem("ac"), n_exp)
                ctx.inc(v.tensor_reduce(ssum[:].rearrange("p (a b) -> p a b", b=1), e_sb[:].rearrange("p (a b) -> p a b", b=8), AX.X, ALU.add), "dv")
                v.wait_ge(ctx.sem("dv"), ctx.n("dv"))
                ctx.inc(v.reciprocal(rsum[:], ssum[:]), "dv")
                v.wait_ge(ctx.sem("dv"), ctx.n("dv"))
                ctx.inc(v.tensor_scalar(probs[:], e_sb[:], rsum[:, 0:1], None, ALU.mult), "dv")
                v.wait_ge(ctx.sem("dv"), ctx.n("dv"))
                ctx.inc(v.tensor_reduce(m1p[:].rearrange("p (a b) -> p a b", b=1), probs[:].rearrange("p (a b) -> p a b", b=8), AX.X, ALU.max), "dv")
                v.wait_ge(ctx.sem("dv"), ctx.n("dv"))
                ctx.inc(v.tensor_scalar(oh0[:], probs[:], m1p[:, 0:1], None, ALU.is_ge), "dv")
                v.wait_ge(ctx.sem("dv"), ctx.n("dv"))
                ctx.inc(v.tensor_scalar(tnot[:], oh0[:], -1.0, 1.0, ALU.mult, ALU.add), "dv")
                v.wait_ge(ctx.sem("dv"), ctx.n("dv"))
                ctx.inc(v.tensor_mul(probs2[:], probs[:], tnot[:]), "dv")
                v.wait_ge(ctx.sem("dv"), ctx.n("dv"))
                ctx.inc(v.tensor_reduce(m2p[:].rearrange("p (a b) -> p a b", b=1), probs2[:].rearrange("p (a b) -> p a b", b=8), AX.X, ALU.max), "dv")
                v.wait_ge(ctx.sem("dv"), ctx.n("dv"))
                ctx.inc(v.tensor_scalar(oh1[:], probs2[:], m2p[:, 0:1], None, ALU.is_ge), "dv")
                v.wait_ge(ctx.sem("dv"), ctx.n("dv"))
                ctx.inc(v.tensor_add(den[:], m1p[:], m2p[:]), "dv")
                v.wait_ge(ctx.sem("dv"), ctx.n("dv"))
                ctx.inc(v.reciprocal(rden[:], den[:]), "dv")
                v.wait_ge(ctx.sem("dv"), ctx.n("dv"))
                ctx.inc(v.tensor_mul(w0c[:], m1p[:], rden[:]), "dv")
                ctx.inc(v.tensor_mul(w1c[:], m2p[:], rden[:]), "dv")
                v.wait_ge(ctx.sem("dv"), ctx.n("dv"))
                ctx.inc(v.tensor_scalar(comb[:], oh0[:], w0c[:, 0:1], None, ALU.mult), "dv")
                v.wait_ge(ctx.sem("dv"), ctx.n("dv"))
                ctx.inc(v.scalar_tensor_tensor(comb[:], oh1[:], w1c[:, 0:1], comb[:], ALU.mult, ALU.add), "dv")
            n_comb = ctx.n("dv")

            @block.tensor
            def _(t):
                t.wait_ge(ctx.sem("dv"), n_comb)
                ctx.inc(t.transpose(psT[0:8, 128:256], comb[:], ident[:]), "pe")
            n_tp = ctx.n("pe")

            @block.vector
            def _(v):
                v.wait_ge(ctx.sem("pe"), n_tp)
                ctx.inc(v.tensor_copy(combT[:, s0 + st * 128:s0 + (st + 1) * 128], psT[0:8, 128:256]), "dv")
    n_gate_dv = ctx.n("dv")
    n_gate_wq = ctx.n("wq")

    w13_free = 0
    w2_free = 0
    evacA_b = [("pe", 0), ("pe", 0)]
    evacC_b = [("pe", 0), ("pe", 0)]
    w2rd = 0
    tsb_rd = 0

    wqb1_0 = ctx.n("wqb1")
    wqb3_0 = ctx.n("wqb3")
    wqb2_0 = ctx.n("wqb2")

    @block.gpsimd
    def _(g):
        ctx.inc(g.dma_start(out=w1s[:], in_=w1_r[0]), "wqb1", 16)
        ctx.inc(g.dma_start(out=w3s[:], in_=w3_r[0]), "wqb3", 16)
        ctx.inc(g.dma_start(out=w2s[:], in_=w2_r[0]), "wqb2", 16)

    for e in range(NE):
        n_w13 = wqb1_0 + 16 * (e + 1)
        n_w2l = wqb2_0 + 16 * (e + 1)

        for c in range(NCH):
            s0 = c * CH
            ac_base = ctx.n("ac")
            dv_base = ctx.n("dv")
            for f in range(FT):
                par = f % 2

                @block.tensor
                def _(t):
                    if f == 0 and c == 0:
                        t.wait_ge(ctx.sem("wqb1"), n_w13)
                        t.wait_ge(ctx.sem("wqb3"), n_w13)
                        if e == 0:
                            t.wait_ge(ctx.sem("wq"), n_gate_wq)
                    if f >= 2:
                        t.wait_ge(ctx.sem("ac"), ac_base + (f - 2) + 1)
                        t.wait_ge(ctx.sem("dv"), dv_base + (f - 2) + 1)
                    elif c > 0 or e > 0:
                        sem_name, val = evacA_b[par]
                        t.wait_ge(ctx.sem(sem_name), val)
                        sem_name, val = evacC_b[par]
                        t.wait_ge(ctx.sem(sem_name), val)
                    for dt in range(DT):
                        t.matmul(psA[par][:], w1s[:, dt, f * 128:(f + 1) * 128], xf_bf[:, dt, s0:s0 + CH], start=(dt == 0), stop=(dt == DT - 1))
                    for dt in range(DT):
                        ins = t.matmul(psC[par][:], w3s[:, dt, f * 128:(f + 1) * 128], xf_bf[:, dt, s0:s0 + CH], start=(dt == 0), stop=(dt == DT - 1))
                    ctx.inc(ins, "pe")
                ctx.marks[("b_hmm", f)] = ctx.n("pe")
            n_h_pe = ctx.n("pe")

            @block.scalar
            def _(sc):
                for f in range(FT):
                    sc.wait_ge(ctx.sem("pe"), ctx.marks[("b_hmm", f)])
                    if c > 0 or e > 0:
                        sc.wait_ge(ctx.sem("pe"), w2rd)
                    ctx.inc(sc.activation(s1_sb[:, f % 2, :], psA[f % 2][:], AF.Silu), "ac")
                    ctx.marks[("b_silu", f)] = ctx.n("ac")

            @block.vector
            def _(v):
                for f in range(FT):
                    v.wait_ge(ctx.sem("ac"), ctx.marks[("b_silu", f)])
                    ctx.inc(v.tensor_mul(h_bf[:, f, :], s1_sb[:, f % 2, :], psC[f % 2][:]), "dv")
                    if f >= FT - 2:
                        which = f % 2
                        evacC_b[which] = ("dv", ctx.n("dv"))
                        evacA_b[which] = ("ac", ctx.marks[("b_silu", f)])
            n_h = ctx.n("dv")

            if c == NCH - 1 and e < NE - 1:
                @block.gpsimd
                def _(g):
                    g.wait_ge(ctx.sem("pe"), n_h_pe)
                    ctx.inc(g.dma_start(out=w1s[:], in_=w1_r[e + 1]), "wqb1", 16)
                    ctx.inc(g.dma_start(out=w3s[:], in_=w3_r[e + 1]), "wqb3", 16)

            @block.tensor
            def _(t):
                t.wait_ge(ctx.sem("dv"), n_gate_dv)
                t.wait_ge(ctx.sem("dv"), ctx.marks.get("b_cb_rd", 0))
                ctx.inc(t.matmul(psB[:], sel[:, e * 128:(e + 1) * 128], combT[:, s0:s0 + CH], start=True, stop=True), "pe")
            n_cb = ctx.n("pe")

            @block.vector
            def _(v):
                v.wait_ge(ctx.sem("pe"), n_cb)
                ctx.inc(v.tensor_copy(cb_sb[:], psB[:]), "dv")
            n_cbc = ctx.n("dv")
            ctx.marks["b_cb_rd"] = None

            dv_scale_base = None
            for dt in range(DT):
                par = dt % 2

                @block.tensor
                def _(t):
                    if dt == 0 and c == 0:
                        t.wait_ge(ctx.sem("wqb2"), n_w2l)
                    if dt == 0:
                        t.wait_ge(ctx.sem("dv"), n_h)
                    if dt >= 2:
                        t.wait_ge(ctx.sem("dv"), ctx.marks[("b_scale", dt - 2)])
                    elif c > 0 or e > 0:
                        t.wait_ge(ctx.sem("dv"), ctx.marks[("b_scale_prev", dt)])
                    for f in range(FT):
                        ins = t.matmul(psO[par][:], w2s[:, f, dt * 128:(dt + 1) * 128], h_bf[:, f, :], start=(f == 0), stop=(f == FT - 1))
                    ctx.inc(ins, "pe")
                n_mm2 = ctx.n("pe")

                @block.vector
                def _(v):
                    v.wait_ge(ctx.sem("pe"), n_mm2)
                    v.wait_ge(ctx.sem("dv"), n_cbc)
                    if e > 0 or c > 0:
                        v.wait_ge(ctx.sem("st"), tsb_rd)
                    ctx.inc(v.tensor_mul(t_sb[:, dt, :], psO[par][:], cb_sb[:]), "dv")
                ctx.marks[("b_scale", dt)] = ctx.n("dv")
            for dt in range(DT):
                ctx.marks[("b_scale_prev", dt)] = ctx.marks[("b_scale", dt)]
            w2rd = ctx.n("pe")
            ctx.marks["b_cb_rd"] = ctx.n("dv")
            n_scale = ctx.n("dv")

            @block.gpsimd
            def _(g):
                g.wait_ge(ctx.sem("dv"), n_scale)
                g.wait_ge(ctx.sem("st"), ctx.n("st"))
                op = ALU.bypass if e == 0 else ALU.add
                ctx.inc(g.dma_start(out=acc_r[:, :, s0:s0 + CH], in_=t_sb[:], accum_op=op), "st", 16)
            tsb_rd = ctx.n("st")

        if e < NE - 1:
            @block.gpsimd
            def _(g):
                g.wait_ge(ctx.sem("pe"), w2rd)
                ctx.inc(g.dma_start(out=w2s[:], in_=w2_r[e + 1]), "wqb2", 16)
    n_acc_done = ctx.n("st")
    xb = [x1_sb, w1s[:, 0:4, :].bitcast(F32).rearrange("p a (x b) -> p (a x) b", b=CH)]
    tb = [t_sb, w1s[:, 4:8, :].bitcast(F32).rearrange("p a (x b) -> p (a x) b", b=CH)]
    sq2 = [w3s[:, 0, :].bitcast(F32).rearrange("p (a b) -> p a b", b=CH),
           w3s[:, 1, :].bitcast(F32).rearrange("p (a b) -> p a b", b=CH)]
    tmp2 = [w3s[:, 2, :].bitcast(F32).rearrange("p (a b) -> p a b", b=CH),
            w3s[:, 3, :].bitcast(F32).rearrange("p (a b) -> p a b", b=CH)]
    mb2 = [w3s[:, 4, 0:1024].bitcast(F32), w3s[:, 4, 1024:2048].bitcast(F32)]
    ivb2 = [w3s[:, 5, 0:1024].bitcast(F32), w3s[:, 5, 1024:2048].bitcast(F32)]
    mrow2 = [w3s[0:1, 6, 0:1024].bitcast(F32), w3s[0:1, 6, 1024:2048].bitcast(F32)]
    vrow2 = [w3s[0:1, 7, 0:1024].bitcast(F32), w3s[0:1, 7, 1024:2048].bitcast(F32)]
    trow2 = [w2s[0:1, 0, :].bitcast(F32), w2s[0:1, 1, :].bitcast(F32)]
    emit_postln_pipelined(ctx, block, T, xin_r, xout_r, acc_r, n_acc_done, "Bp",
                          xb, tb, sq2, tmp2, mb2, ivb2, mrow2, vrow2, trow2,
                          [psA[0], psA[1], psO[0], psO[1]], ones128, ones1)
    return ctx.n("st2")


def blockB_host_inputs(p):
    out = {
        "b_gwT": np.ascontiguousarray(p["gate_w"].T),
        "b_w1T": np.ascontiguousarray(p["w1"].transpose(0, 2, 1)),
        "b_w3T": np.ascontiguousarray(p["w3"].transpose(0, 2, 1)),
        "b_w2T": np.ascontiguousarray(p["w2"].transpose(0, 2, 1)),
        "ident": np.eye(128, dtype=np.float32),
    }
    sel = np.zeros((8, 8 * 128), np.float32)
    for e in range(8):
        sel[e, e * 128:(e + 1) * 128] = 1.0
    out["b_sel"] = sel
    return out


def declare_blockB_drams(nc, T):
    F = mybir.dt.float32
    shapes = {
        "b_gwT": [1024, 8], "b_w1T": [8, 1024, 2048], "b_w3T": [8, 1024, 2048],
        "b_w2T": [8, 2048, 1024], "ident": [128, 128], "b_sel": [8, 8 * 128],
    }
    return {k: nc.dram_tensor(k, v, F, kind="ExternalInput").ap() for k, v in shapes.items()}



def emit_blockC(ctx, block, T, xin, xout, W, kvscr, ksscr):
    nc = ctx.nc
    NCH = T // CH
    DT = 8
    NH = 16

    xin_r = xin.rearrange("(t p) s -> p t s", p=128)
    xout_r = xout.rearrange("(t p) s -> p t s", p=128)
    wq_r = W["c_wqT"].rearrange("(t p) f -> p t f", p=128)
    wk_r = W["c_wkT"].rearrange("(t p) f -> p t f", p=128)
    wv_r = W["c_wvT"].rearrange("(t p) f -> p t f", p=128)

    x_sb = ctx.sbuf("c_x", [128, DT, CH], F32)
    xbf = ctx.sbuf("c_xbf", [128, DT, CH], BF16)
    wqs = ctx.sbuf("c_wqs", [128, DT, 1024], BF16)
    wks = ctx.sbuf("c_wks", [128, DT, 1024], BF16)
    wvs = ctx.sbuf("c_wvs", [128, DT, 1024], BF16)
    q_bf = ctx.sbuf("c_q", [128, DT, T], BF16)
    sq_sb = ctx.sbuf("c_sq", [128, 2, CH], F32)
    rb_sb = ctx.sbuf("c_rb", [128, 2, CH], F32)
    t_sb = ctx.sbuf("c_t", [128, 2, CH], F32)
    nrm2 = ctx.sbuf("c_nrm2", [2, CH], F32)
    nrm_all = ctx.sbuf("c_nrmall", [2, DT, CH], F32)
    k_bf = ctx.sbuf("c_k", [128, 2, 1024], BF16)
    v_bf = ctx.sbuf("c_v", [128, 2, 1024], BF16)
    sk_sb = ctx.sbuf("c_sk", [128, 1024], F32)
    ssk = ctx.sbuf("c_ssk", [128, 16], F32)
    rk = ctx.sbuf("c_rk", [128, 16], F32)
    kv_sb = ctx.sbuf("c_kv", [64, 1024], BF16)
    kvacc = ctx.sbuf("c_kvacc", [64, 1024], F32)
    kv2_sb = ctx.sbuf("c_kv2", [128, 8, 64], BF16)
    ks_row = ctx.sbuf("c_ksrow", [1, 1024], F32)
    ksfm = ctx.sbuf("c_ksfm", [128, 8], BF16)
    ks2 = ctx.sbuf("c_ks2", [128, 8, 2], BF16)
    rn_bf_all = ctx.sbuf("c_rnbfall", [2, DT, CH], BF16)
    qn_all = ctx.sbuf("c_qnall", [128, DT, CH], BF16)
    indA = ctx.sbuf("c_indA_sb", [128, 2], F32)
    ident_c = ctx.sbuf("c_ident_sb", [128, 128], F32)
    indT2 = ctx.sbuf("c_indT2_sb", [2, 128], F32)
    indT2b = ctx.sbuf("c_indT2b_sb", [2, 128], BF16)
    maskA = ctx.sbuf("c_maskA_sb", [128, 1], F32)
    maskB = ctx.sbuf("c_maskB_sb", [128, 1], F32)
    ones128 = ctx.sbuf("c_ones128", [128, 1], F32)
    ones128b = ctx.sbuf("c_ones128b", [128, 1], BF16)
    ones1 = ctx.sbuf("c_ones1", [1, 128], F32)
    sq = wks[:, 0:2, :].bitcast(F32).rearrange("p a (x b) -> p (a x) b", b=CH)[:, 0:2, :]
    tmp1 = wks[:, 2:4, :].bitcast(F32).rearrange("p a (x b) -> p (a x) b", b=CH)[:, 0:2, :]
    mb_sb = wks[:, 4, :].bitcast(F32)[:, 0:CH]
    invb_sb = wks[:, 5, :].bitcast(F32)[:, 0:CH]
    mean_sb = wvs[0:1, 0, :].bitcast(F32)[:, 0:CH]
    var_sb = wvs[0:1, 1, :].bitcast(F32)[:, 0:CH]
    tmprow = wvs[0:1, 2, :].bitcast(F32)[:, 0:CH]

    P = [ctx.psum("c_P0", [128, CH]), ctx.psum("c_P1", [128, CH]),
         ctx.psum("c_P2", [128, CH]), ctx.psum("c_P3", [128, CH])]
    psKV = ctx.psum("c_psKV", [64, 512])
    psKS = ctx.psum("c_psKS", [1, 1024])
    psN = ctx.psum("c_psN", [2, CH])

    @block.sync
    def _(s):
        s.wait_ge(ctx.sem("ld"), ctx.n("ld"))
        for name, dst in [("c_indA", indA), ("c_indT2", indT2), ("c_ident", ident_c)]:
            ctx.inc(s.dma_start(out=dst[:], in_=W[name]), "ld", 16)
    n_param_ld = ctx.n("ld")

    @block.gpsimd
    def _(g):
        g.wait_ge(ctx.sem("wq"), ctx.n("wq"))
        ctx.inc(g.dma_start(out=wqs[:], in_=wq_r[:]), "wq", 16)
        g.wait_ge(ctx.sem("wq"), ctx.n("wq"))
        ctx.inc(g.dma_start(out=wks[:], in_=wk_r[:]), "wq", 16)
        g.wait_ge(ctx.sem("wq"), ctx.n("wq"))
        ctx.inc(g.dma_start(out=wvs[:], in_=wv_r[:]), "wq", 16)
        g.wait_ge(ctx.sem("wq"), ctx.n("wq"))
        ctx.inc(g.dma_start(out=indT2b[:], in_=W["c_indT2"]), "wq", 16)
        g.wait_ge(ctx.sem("wq"), ctx.n("wq"))
        ctx.inc(g.dma_start(out=maskA[:], in_=W["c_maskA"]), "wq", 16)
        g.wait_ge(ctx.sem("wq"), ctx.n("wq"))
        ctx.inc(g.dma_start(out=maskB[:], in_=W["c_maskB"]), "wq", 16)
        ctx.inc(g.memset(ones128[:], 1.0), "pl")
        ctx.inc(g.memset(ones128b[:], 1.0), "pl")
        ctx.inc(g.memset(ones1[:], 1.0), "pl")
    n_init_pl = ctx.n("pl")
    n_wload = ctx.n("wq")

    for c in range(NCH):
        s0 = c * CH

        @block.gpsimd
        def _(g):
            g.wait_ge(ctx.sem("wq"), ctx.n("wq"))
            if c >= 1:
                ctx.wait(g, "pe")
            ctx.inc(g.dma_start(out=xbf[:], in_=xin_r[:, :, s0:s0 + CH]), "wq", 16)
        n_xbfld = ctx.n("wq")

        dv_base_q = ctx.n("dv")
        ac_base_q = ctx.n("ac")
        def emit_qgroup(qf):
            par3 = qf % 3

            @block.tensor
            def _(t):
                t.wait_ge(ctx.sem("wq"), n_xbfld)
                if c == 0 and qf == 0:
                    t.wait_ge(ctx.sem("wq"), n_wload)
                    t.wait_ge(ctx.sem("pl"), n_init_pl)
                    t.wait_ge(ctx.sem("ld"), n_param_ld)
                if qf >= 3:
                    t.wait_ge(ctx.sem("dv"), dv_base_q + 4 * (qf - 3) + 4)
                else:
                    sem_name, val = ctx.marks.get(("c_P", par3), ("pe", 0))
                    t.wait_ge(ctx.sem(sem_name), val)
                for dt in range(DT):
                    ins = t.matmul(P[par3][:], wqs[:, dt, qf * 128:(qf + 1) * 128], xbf[:, dt, :], start=(dt == 0), stop=(dt == DT - 1))
                ctx.inc(ins, "pe")
            ctx.marks[("c_qmm", qf)] = ctx.n("pe")


        def emit_qchain(qf):
            par3 = qf % 3

            @block.scalar
            def _(sc):
                sc.wait_ge(ctx.sem("pe"), ctx.marks[("c_qmm", qf)])
                if qf >= 2:
                    sc.wait_ge(ctx.sem("pe"), ctx.marks[("c_ind", qf - 2)])
                ctx.inc(sc.square(sq_sb[:, qf % 2, :], P[par3][:]), "ac")
            n_sq = ctx.n("ac")

            @block.tensor
            def _(t):
                t.wait_ge(ctx.sem("ac"), n_sq)
                if qf >= 1:
                    t.wait_ge(ctx.sem("dv"), dv_base_q + 4 * (qf - 1) + 1)
                ctx.inc(t.matmul(psN[0:2, :], indA[:], sq_sb[:, qf % 2, :], start=True, stop=True), "pe")
            n_ind = ctx.n("pe")
            ctx.marks[("c_ind", qf)] = n_ind

            @block.vector
            def _(v):
                v.wait_ge(ctx.sem("pe"), n_ind)
                ctx.inc(v.tensor_copy(nrm_all[:, qf, :], psN[0:2, :]), "dv")
                v.wait_ge(ctx.sem("dv"), ctx.n("dv"))
                ctx.inc(v.tensor_scalar_max(nrm_all[:, qf, :], nrm_all[:, qf, :], 1e-24), "dv")
            n_mx = ctx.n("dv")

            @block.scalar
            def _(sc):
                sc.wait_ge(ctx.sem("dv"), n_mx)
                ctx.inc(sc.activation(nrm_all[:, qf, :], nrm_all[:, qf, :], AF.Ln), "ac")
                sc.wait_ge(ctx.sem("ac"), ctx.n("ac"))
                ctx.inc(sc.activation(nrm_all[:, qf, :], nrm_all[:, qf, :], AF.Exp, scale=-0.5), "ac")
            n_rsq = ctx.n("ac")

            @block.tensor
            def _(t):
                t.wait_ge(ctx.sem("ac"), n_rsq)
                if qf >= 1:
                    t.wait_ge(ctx.sem("dv"), dv_base_q + 4 * (qf - 1) + 3)
                ctx.inc(t.matmul(P[3][:], indT2[:], nrm_all[:, qf, :], start=True, stop=True), "pe")
            n_bc = ctx.n("pe")

            @block.vector
            def _(v):
                v.wait_ge(ctx.sem("pe"), n_bc)
                ctx.inc(v.tensor_copy(rb_sb[:, qf % 2, :], P[3][:]), "dv")
                v.wait_ge(ctx.sem("dv"), ctx.n("dv"))
                if qf >= 2:
                    v.wait_ge(ctx.sem("ac"), ac_base_q + 5 * (qf - 2) + 5)
                ctx.inc(v.tensor_mul(t_sb[:, qf % 2, :], P[par3][:], rb_sb[:, qf % 2, :]), "dv")
            n_ql2 = ctx.n("dv")
            ctx.marks[("c_P", par3)] = ("dv", n_ql2)

            @block.scalar
            def _(sc):
                sc.wait_ge(ctx.sem("dv"), n_ql2)
                ctx.inc(sc.activation(t_sb[:, qf % 2, :], t_sb[:, qf % 2, :], AF.Exp), "ac")
                sc.wait_ge(ctx.sem("ac"), ctx.n("ac"))
                ctx.inc(sc.activation(q_bf[:, qf, s0:s0 + CH], t_sb[:, qf % 2, :], AF.Ln, bias=1.0), "ac")

        for qf in range(DT):
            emit_qgroup(qf)
            if qf >= 2:
                emit_qchain(qf - 2)
        emit_qchain(6)
        emit_qchain(7)
        n_qdone = ctx.n("ac")

        for st in range(4):
            par = st % 2
            for half in range(2):
                @block.tensor
                def _(t):
                    sem_name, val = ctx.marks.get(("c_P", par), ("pe", 0))
                    t.wait_ge(ctx.sem(sem_name), val)
                    sem_name, val = ctx.marks.get(("c_P", 2 + par), ("pe", 0))
                    t.wait_ge(ctx.sem(sem_name), val)
                    if st == 0 and half == 0:
                        t.wait_ge(ctx.sem("ac"), n_qdone)
                    for dt in range(DT):
                        t.matmul(P[par][:], xbf[:, dt, st * 128:(st + 1) * 128], wks[:, dt, half * 512:(half + 1) * 512], start=(dt == 0), stop=(dt == DT - 1))
                    for dt in range(DT):
                        ins = t.matmul(P[2 + par][:], xbf[:, dt, st * 128:(st + 1) * 128], wvs[:, dt, half * 512:(half + 1) * 512], start=(dt == 0), stop=(dt == DT - 1))
                    ctx.inc(ins, "pe")
                n_kvmm = ctx.n("pe")

                @block.scalar
                def _(sc):
                    sc.wait_ge(ctx.sem("pe"), n_kvmm)
                    ctx.inc(sc.square(sk_sb[:, half * 512:(half + 1) * 512], P[par][:]), "ac")
                n_sqk = ctx.n("ac")

                @block.vector
                def _(v):
                    v.wait_ge(ctx.sem("ac"), n_sqk)
                    ctx.inc(v.tensor_reduce(ssk[:, half * 8:(half + 1) * 8].rearrange("p (a b) -> p a b", b=1), sk_sb[:, half * 512:(half + 1) * 512].rearrange("p (a b) -> p a b", b=64), AX.X, ALU.add), "dv")
                n_ssk = ctx.n("dv")

                @block.vector
                def _(v):
                    v.wait_ge(ctx.sem("dv"), n_ssk)
                    ctx.inc(v.tensor_scalar_max(rk[:, half * 8:(half + 1) * 8], ssk[:, half * 8:(half + 1) * 8], 1e-24), "dv")
                n_rkmx = ctx.n("dv")

                @block.scalar
                def _(sc):
                    sc.wait_ge(ctx.sem("dv"), n_rkmx)
                    ctx.inc(sc.activation(rk[:, half * 8:(half + 1) * 8], rk[:, half * 8:(half + 1) * 8], AF.Ln), "ac")
                    sc.wait_ge(ctx.sem("ac"), ctx.n("ac"))
                    ctx.inc(sc.activation(rk[:, half * 8:(half + 1) * 8], rk[:, half * 8:(half + 1) * 8], AF.Exp, scale=-0.5), "ac")
                n_rk2 = ctx.n("ac")

                @block.vector
                def _(v):
                    v.wait_ge(ctx.sem("ac"), n_rk2)
                    ctx.inc(v.tensor_tensor(sk_sb[:, half * 512:(half + 1) * 512].rearrange("p (a b) -> p a b", b=64), P[par][:].rearrange("p (a b) -> p a b", b=64), rk[:, half * 8:(half + 1) * 8].rearrange("p (a b) -> p a b", b=1).broadcast_to([128, 8, 64]), ALU.mult), "dv")
                n_kn = ctx.n("dv")

                @block.scalar
                def _(sc):
                    sc.wait_ge(ctx.sem("dv"), n_kn)
                    ctx.inc(sc.activation(sk_sb[:, half * 512:(half + 1) * 512], sk_sb[:, half * 512:(half + 1) * 512], AF.Exp), "ac")
                    sc.wait_ge(ctx.sem("ac"), ctx.n("ac"))
                    ctx.inc(sc.activation(k_bf[:, par, half * 512:(half + 1) * 512], sk_sb[:, half * 512:(half + 1) * 512], AF.Ln, bias=1.0), "ac")

                @block.vector
                def _(v):
                    v.wait_ge(ctx.sem("pe"), n_kvmm)
                    ctx.inc(v.tensor_copy(v_bf[:, par, half * 512:(half + 1) * 512], P[2 + par][:]), "dv")
                ctx.marks[("c_P", 2 + par)] = ("dv", ctx.n("dv"))

            gst = c * 4 + st
            n_kv_ac = ctx.n("ac")
            n_kv_dv = ctx.n("dv")

            for hh in range(2):
                @block.tensor
                def _(t):
                    t.wait_ge(ctx.sem("ac"), n_kv_ac)
                    t.wait_ge(ctx.sem("dv"), n_kv_dv)
                    if gst > 0 or hh > 0:
                        t.wait_ge(ctx.sem("dv"), ctx.marks.get("c_kvacc", 0))
                    for h in range(hh * 8, hh * 8 + 8):
                        ins = t.matmul(psKV[0:64, (h - hh * 8) * 64:(h - hh * 8 + 1) * 64], k_bf[:, par, h * 64:(h + 1) * 64], v_bf[:, par, h * 64:(h + 1) * 64], start=True, stop=True)
                    if hh == 1:
                        for half in range(2):
                            ins = t.matmul(psKS[0:1, half * 512:(half + 1) * 512], ones128b[:], k_bf[:, par, half * 512:(half + 1) * 512], start=(gst == 0), stop=(gst == 4 * NCH - 1))
                    ctx.inc(ins, "pe")
                n_kvpe = ctx.n("pe")

                @block.vector
                def _(v):
                    v.wait_ge(ctx.sem("pe"), n_kvpe)
                    if gst == 0:
                        ctx.inc(v.tensor_copy(kvacc[:, hh * 512:(hh + 1) * 512], psKV[:]), "dv")
                    else:
                        ctx.inc(v.tensor_add(kvacc[:, hh * 512:(hh + 1) * 512], kvacc[:, hh * 512:(hh + 1) * 512], psKV[:]), "dv")
                ctx.marks["c_kvacc"] = ctx.n("dv")
    n_pass1_pe = ctx.n("pe")

    n_pass1_dv = ctx.n("dv")

    @block.vector
    def _(v):
        v.wait_ge(ctx.sem("pe"), n_pass1_pe)
        v.wait_ge(ctx.sem("dv"), n_pass1_dv)
        ctx.inc(v.tensor_scalar(kv_sb[:], kvacc[:], 0.125, None, ALU.mult), "dv")
        ctx.inc(v.tensor_scalar(ks_row[:], psKS[0:1, :], 1.0, 1e-6, ALU.mult, ALU.add), "dv")
    n_fin = ctx.n("dv")

    @block.sync
    def _(s):
        s.wait_ge(ctx.sem("dv"), n_fin)
        s.wait_ge(ctx.sem("ld"), ctx.n("ld"))
        ctx.inc(s.dma_start(out=kvscr.transpose([1, 2, 0, 3]), in_=kv_sb[:].rearrange("dh (j two c) -> dh j two c", two=2, c=64)), "ld", 16)
        s.wait_ge(ctx.sem("ld"), ctx.n("ld"))
        ctx.inc(s.dma_start(out=kv2_sb[:], in_=kvscr.rearrange("two dh j c -> (two dh) j c")), "ld", 16)
    n_kv2 = ctx.n("ld")

    @block.tensor
    def _(t):
        t.wait_ge(ctx.sem("dv"), n_fin)
        for t_i in range(8):
            ins = t.transpose(P[0][:, t_i:t_i + 1], ks_row[0:1, t_i * 128:(t_i + 1) * 128], ident_c[0:1, 0:1])
        ctx.inc(ins, "pe")
    n_ksfm_pe = ctx.n("pe")

    @block.vector
    def _(v):
        v.wait_ge(ctx.sem("pe"), n_ksfm_pe)
        ctx.inc(v.tensor_copy(ksfm[:], P[0][:, 0:8]), "dv")
    n_ksfm = ctx.n("dv")

    @block.vector
    def _(v):
        v.wait_ge(ctx.sem("dv"), n_ksfm)
        for t_i in range(8):
            ctx.inc(v.tensor_scalar(ks2[:, t_i, 0:1], ksfm[:, t_i:t_i + 1], maskA[:, 0:1], None, ALU.mult), "dv")
            ctx.inc(v.tensor_scalar(ks2[:, t_i, 1:2], ksfm[:, t_i:t_i + 1], maskB[:, 0:1], None, ALU.mult), "dv")
    n_ks2 = ctx.n("dv")

    st2_0 = ctx.n("st2")
    for c in range(NCH):
        s0 = c * CH

        @block.sync
        def _(s):
            s.wait_ge(ctx.sem("ld"), ctx.n("ld"))
            if c >= 1:
                s.wait_ge(ctx.sem("st2"), st2_0 + 16 * c)
            ctx.inc(s.dma_start(out=x_sb[:], in_=xin_r[:, :, s0:s0 + CH]), "ld", 16)
        n_xld = ctx.n("ld")

        dv_base_p2 = ctx.n("dv")
        for t_i in range(DT):
            @block.tensor
            def _(t):
                if c == 0 and t_i == 0:
                    t.wait_ge(ctx.sem("dv"), n_ks2)
                    t.wait_ge(ctx.sem("ld"), n_kv2)
                if t_i >= 1:
                    t.wait_ge(ctx.sem("dv"), dv_base_p2 + 3 * (t_i - 1) + 1)
                else:
                    ctx.wait(t, "dv")
                ctx.inc(t.matmul(psN[0:2, :], ks2[:, t_i, :], q_bf[:, t_i, s0:s0 + CH], start=True, stop=True), "pe")
            n_nmm = ctx.n("pe")

            @block.vector
            def _(v):
                v.wait_ge(ctx.sem("pe"), n_nmm)
                if t_i >= 1:
                    v.wait_ge(ctx.sem("dv"), dv_base_p2 + 3 * (t_i - 1) + 3)
                ctx.inc(v.tensor_copy(nrm2[:], psN[0:2, :]), "dv")
                v.wait_ge(ctx.sem("dv"), ctx.n("dv"))
                ctx.inc(v.reciprocal(nrm2[:], nrm2[:]), "dv")
                v.wait_ge(ctx.sem("dv"), ctx.n("dv"))
                ctx.inc(v.tensor_copy(rn_bf_all[:, t_i, :], nrm2[:]), "dv")

        for t_i in range(DT):
            par = t_i % 2

            @block.tensor
            def _(t):
                t.wait_ge(ctx.sem("dv"), dv_base_p2 + 3 * t_i + 3)
                if t_i >= 2:
                    t.wait_ge(ctx.sem("dv"), ctx.marks[("c2_qn", t_i - 2)])
                ctx.inc(t.matmul(P[2 + par][:], indT2b[:], rn_bf_all[:, t_i, :], start=True, stop=True), "pe")
            n_rb = ctx.n("pe")

            @block.vector
            def _(v):
                v.wait_ge(ctx.sem("pe"), n_rb)
                ctx.inc(v.tensor_mul(qn_all[:, t_i, :], q_bf[:, t_i, s0:s0 + CH], P[2 + par][:]), "dv")
            ctx.marks[("c2_qn", t_i)] = ctx.n("dv")

        for t_i in range(DT):
            par = t_i % 2

            @block.tensor
            def _(t):
                t.wait_ge(ctx.sem("dv"), ctx.marks[("c2_qn", t_i)])
                if t_i >= 2:
                    t.wait_ge(ctx.sem("dv"), ctx.marks[("c2_res", t_i - 2)])
                elif c > 0:
                    t.wait_ge(ctx.sem("dv"), ctx.marks.get(("c2_res_prev", t_i), 0))
                for h2 in range(2):
                    b0 = h2 * 64
                    ins = t.matmul(P[par][b0:b0 + 64, :], kv2_sb[b0:b0 + 64, t_i, :], qn_all[b0:b0 + 64, t_i, :], start=True, stop=True)
                ctx.inc(ins, "pe")
            n_omm = ctx.n("pe")

            @block.vector
            def _(v):
                v.wait_ge(ctx.sem("pe"), n_omm)
                v.wait_ge(ctx.sem("ld"), n_xld)
                ctx.inc(v.tensor_add(x_sb[:, t_i, :], x_sb[:, t_i, :], P[par][:]), "dv")
            ctx.marks[("c2_res", t_i)] = ctx.n("dv")
        for t_i in range(DT):
            ctx.marks[("c2_res_prev", t_i)] = ctx.marks[("c2_res", t_i)]
        n_resid = ctx.n("dv")

        emit_stats(ctx, block, x_sb, sq, P[0], P[1], ones128, None, DT,
                   extra_pe_wait=("dv", n_resid))
        emit_stats_final(ctx, block, P[0], P[1], mean_sb, var_sb, tmprow, 1024.0, 1e-5)
        emit_bcast2(ctx, block, mean_sb, tmprow, mb_sb, invb_sb, P[2], P[3], ones1)
        n_bc = ctx.n("dv")
        for dt in range(DT):
            @block.gpsimd
            def _(g):
                g.wait_ge(ctx.sem("dv"), n_bc)
                if dt >= 2:
                    ctx.wait(g, "dv")
                ctx.inc(g.tensor_sub(tmp1[:, dt % 2, :], x_sb[:, dt, :], mb_sb[:]), "pl")
            n_pl_sub = ctx.n("pl")

            @block.vector
            def _(v):
                v.wait_ge(ctx.sem("pl"), n_pl_sub)
                ctx.inc(v.tensor_mul(x_sb[:, dt, :], tmp1[:, dt % 2, :], invb_sb[:]), "dv")
        n_post = ctx.n("dv")

        @block.sync
        def _(s):
            s.wait_ge(ctx.sem("dv"), n_post)
            s.wait_ge(ctx.sem("st2"), ctx.n("st2"))
            ctx.inc(s.dma_start(out=xout_r[:, :, s0:s0 + CH], in_=x_sb[:]), "st2", 16)

    return ctx.n("st2")


def blockC_host_inputs(p):
    W_qkv = p["W_qkv"]
    indA = np.zeros((128, 2), np.float32)
    indA[0:64, 0] = 1.0
    indA[64:128, 1] = 1.0
    indT2 = np.ascontiguousarray(indA.T)
    return {
        "c_ident": np.eye(128, dtype=np.float32),
        "c_wqT": np.ascontiguousarray(W_qkv[0:1024].T),
        "c_wkT": np.ascontiguousarray(W_qkv[1024:2048].T),
        "c_wvT": np.ascontiguousarray(W_qkv[2048:3072].T),
        "c_indA": indA,
        "c_indT2": indT2,
        "c_maskA": indA[:, 0:1].copy(),
        "c_maskB": indA[:, 1:2].copy(),
    }


def declare_blockC_drams(nc, T):
    F = mybir.dt.float32
    shapes = {
        "c_ident": [128, 128],
        "c_wqT": [1024, 1024], "c_wkT": [1024, 1024], "c_wvT": [1024, 1024],
        "c_indA": [128, 2], "c_indT2": [2, 128], "c_maskA": [128, 1], "c_maskB": [128, 1],
    }
    return {k: nc.dram_tensor(k, v, F, kind="ExternalInput").ap() for k, v in shapes.items()}



def emit_blockD(ctx, block, T, xin, xout, W, acc_hbm, ar_in, ar_out, use_cc=True):
    nc = ctx.nc
    NCH = T // CH
    DT = 8
    FT = 16
    NE = 8
    CAP = 2048.0

    xin_r = xin.rearrange("(t p) s -> p t s", p=128)
    xout_r = xout.rearrange("(t p) s -> p t s", p=128)
    acc_r = acc_hbm.rearrange("(t p) s -> p t s", p=128)
    gw_r = W["d_Wg"].rearrange("(t p) e -> p t e", p=128)
    w1_r = W["d_W1"].rearrange("E (t p) f -> E p t f", p=128)
    w2_r = W["d_W2"].rearrange("E (t p) f -> E p t f", p=128)
    b1_r = W["d_b1fm"]
    b2_r = W["d_b2r"]

    xf_bf = ctx.sbuf("d_xf", [128, DT, T], BF16)
    gwt = ctx.sbuf("d_gwt_sb", [128, DT, 8], F32)
    tri = ctx.sbuf("d_tri_sb", [128, 128], F32)
    ident = ctx.sbuf("d_ident_sb", [128, 128], F32)
    sel = ctx.sbuf("d_sel_sb", [8, NE * 128], BF16)
    rowmask = ctx.sbuf("d_rowmask_sb", [1, 128], F32)
    thmask = ctx.sbuf("d_thmask_sb", [128, 24], F32)
    e_sb = ctx.sbuf("d_e", [128, 8], F32)
    nm1 = ctx.sbuf("d_nm1", [128, 1], F32)
    ssum = ctx.sbuf("d_ssum", [128, 1], F32)
    rsum = ctx.sbuf("d_rsum", [128, 1], F32)
    probs = ctx.sbuf("d_probs", [128, 8], F32)
    m1p = ctx.sbuf("d_m1p", [128, 1], F32)
    tnot = ctx.sbuf("d_tnot", [128, 8], F32)
    probs2 = ctx.sbuf("d_probs2", [128, 8], F32)
    m2p = ctx.sbuf("d_m2p", [128, 1], F32)
    den = ctx.sbuf("d_den", [128, 1], F32)
    rden = ctx.sbuf("d_rden", [128, 1], F32)
    oh0_sb = ctx.sbuf("d_oh0", [128, 32, 8], F32)
    oh1_sb = ctx.sbuf("d_oh1", [128, 32, 8], F32)
    cum0_sb = ctx.sbuf("d_cum0", [128, 32, 8], F32)
    cum1_sb = ctx.sbuf("d_cum1", [128, 32, 8], F32)
    w0_sb = ctx.sbuf("d_w0", [128, 32], F32)
    w1c_sb = ctx.sbuf("d_w1c", [128, 32], F32)
    off0 = ctx.sbuf("d_off0", [1, 8], F32)
    off1 = ctx.sbuf("d_off1", [1, 8], F32)
    n01 = ctx.sbuf("d_n01", [1, 16], F32)
    contrib = ctx.sbuf("d_contrib", [1, 128], F32)
    allcnt = ctx.sbuf("d_allcnt", [1, 128], F32)
    accol = ctx.sbuf("d_accol", [128, 1], F32)
    one11 = ctx.sbuf("d_one11", [1, 1], F32)
    th_row = ctx.sbuf("d_throw", [1, 24], F32)
    th0 = ctx.sbuf("d_th0", [1, 8], F32)
    th1 = ctx.sbuf("d_th1", [1, 8], F32)
    t0k = ctx.sbuf("d_t0k", [1, 8], F32)
    th0b = ctx.sbuf("d_th0b", [128, 8], F32)
    th1b = ctx.sbuf("d_th1b", [128, 8], F32)
    keep = ctx.sbuf("d_keep", [128, 8], F32)
    keep1 = ctx.sbuf("d_keep1", [128, 8], F32)
    comb = ctx.sbuf("d_comb", [128, 8], F32)
    combT = ctx.sbuf("d_combT", [8, T], BF16)
    w1hs = ctx.sbuf("d_w1hs", [128, DT, 2048], BF16)
    w2hs = ctx.sbuf("d_w2hs", [128, FT, 1024], BF16)
    b1s = ctx.sbuf("d_b1s", [128, 2, 32], F32)
    b2s = ctx.sbuf("d_b2s", [1, 2, 1024], BF16)
    onesrow_bf = ctx.sbuf("d_onesrow", [1, CH], BF16)
    h_bf = ctx.sbuf("d_h", [128, FT, CH], BF16)
    x3_sb = h_bf[:].bitcast(F32).rearrange("p a b -> p (a b)").rearrange("p (t s) -> p t s", t=DT)
    cb_sb = ctx.sbuf("d_cb", [128, CH], F32)
    t_sb = ctx.sbuf("d_t", [128, DT, CH], F32)
    ones128 = ctx.sbuf("d_ones128", [128, 1], F32)
    ones1 = ctx.sbuf("d_ones1", [1, 128], F32)
    sq = w1hs[:, 0:2, 0:1024].bitcast(F32).rearrange("p a (x b) -> p (a x) b", b=CH)[:, 0:2, :]
    tmp1 = w1hs[:, 2:4, 0:1024].bitcast(F32).rearrange("p a (x b) -> p (a x) b", b=CH)[:, 0:2, :]
    mb_sb = w1hs[:, 4, 0:1024].bitcast(F32)[:, 0:CH]
    invb_sb = w1hs[:, 5, 0:1024].bitcast(F32)[:, 0:CH]
    mean_sb = w2hs[0:1, 0, :].bitcast(F32)[:, 0:CH]
    var_sb = w2hs[0:1, 1, :].bitcast(F32)[:, 0:CH]
    tmprow = w2hs[0:1, 2, :].bitcast(F32)[:, 0:CH]

    psA = [ctx.psum("d_psA0", [128, CH]), ctx.psum("d_psA1", [128, CH]),
           ctx.psum("d_psA2", [128, CH]), ctx.psum("d_psA3", [128, CH])]
    psO = [ctx.psum("d_psO0", [128, CH]), ctx.psum("d_psO1", [128, CH])]
    psB = ctx.psum("d_psB", [128, CH])
    psG = ctx.psum("d_psG", [128, 512])

    @block.sync
    def _(s):
        s.wait_ge(ctx.sem("ld"), ctx.n("ld"))
        ctx.inc(s.dma_start(out=gwt[:], in_=gw_r[:]), "ld", 16)
        s.wait_ge(ctx.sem("ld"), ctx.n("ld"))
        ctx.inc(s.dma_start(out=tri[:], in_=W["d_tri"]), "ld", 16)
        s.wait_ge(ctx.sem("ld"), ctx.n("ld"))
        ctx.inc(s.dma_start(out=ident[:], in_=W["d_ident"]), "ld", 16)
        s.wait_ge(ctx.sem("ld"), ctx.n("ld"))
        ctx.inc(s.dma_start(out=rowmask[:], in_=W["d_rowmask"]), "ld", 16)
        s.wait_ge(ctx.sem("ld"), ctx.n("ld"))
        ctx.inc(s.dma_start(out=thmask[:], in_=W["d_thmask"]), "ld", 16)
    n_param_ld = ctx.n("ld")

    @block.gpsimd
    def _(g):
        g.wait_ge(ctx.sem("wq"), ctx.n("wq"))
        ctx.inc(g.dma_start(out=sel[:], in_=W["d_sel"]), "wq", 16)
        ctx.inc(g.memset(ones128[:], 1.0), "pl")
        ctx.inc(g.memset(ones1[:], 1.0), "pl")
        ctx.inc(g.memset(one11[:], 1.0), "pl")
        ctx.inc(g.memset(onesrow_bf[:], 1.0), "pl")
        ctx.inc(g.memset(off0[:], 0.0), "pl")
        ctx.inc(g.memset(off1[:], 0.0), "pl")
    n_init_pl = ctx.n("pl")
    n_init_wq = ctx.n("wq")

    for c in range(NCH):
        s0 = c * CH

        @block.sync
        def _(s):
            s.wait_ge(ctx.sem("ld"), ctx.n("ld"))
            if c >= 1:
                ctx.wait(s, "pe")
            ctx.inc(s.dma_start(out=x3_sb[:], in_=xin_r[:, :, s0:s0 + CH]), "ld", 16)
        n_xld = ctx.n("ld")

        @block.gpsimd
        def _(g):
            g.wait_ge(ctx.sem("wq"), ctx.n("wq"))
            ctx.inc(g.dma_start(out=xf_bf[:, :, s0:s0 + CH], in_=xin_r[:, :, s0:s0 + CH]), "wq", 16)
        n_xfld = ctx.n("wq")

        for st in range(4):
            sti = c * 4 + st

            @block.tensor
            def _(t):
                t.wait_ge(ctx.sem("ld"), n_xld)
                if sti == 0:
                    t.wait_ge(ctx.sem("pl"), n_init_pl)
                    t.wait_ge(ctx.sem("ld"), n_param_ld)
                ctx.wait(t, "dv")
                for dt in range(DT):
                    ins = t.matmul(psG[:, 0:8], x3_sb[:, dt, st * 128:(st + 1) * 128], gwt[:, dt, :], start=(dt == 0), stop=(dt == DT - 1))
                ctx.inc(ins, "pe")
            n_gmm = ctx.n("pe")

            @block.vector
            def _(v):
                v.wait_ge(ctx.sem("pe"), n_gmm)
                ctx.inc(v.tensor_reduce(nm1[:].rearrange("p (a b) -> p a b", b=1), psG[:, 0:8].rearrange("p (a b) -> p a b", b=8), AX.X, ALU.max, negate=True), "dv")
            n_nm1 = ctx.n("dv")

            @block.scalar
            def _(sc):
                sc.wait_ge(ctx.sem("dv"), n_nm1)
                ctx.inc(sc.activation(e_sb[:], psG[:, 0:8], AF.Exp, bias=nm1[:, 0:1]), "ac")
            n_exp = ctx.n("ac")

            @block.vector
            def _(v):
                v.wait_ge(ctx.sem("ac"), n_exp)
                ctx.inc(v.tensor_reduce(ssum[:].rearrange("p (a b) -> p a b", b=1), e_sb[:].rearrange("p (a b) -> p a b", b=8), AX.X, ALU.add), "dv")
                v.wait_ge(ctx.sem("dv"), ctx.n("dv"))
                ctx.inc(v.reciprocal(rsum[:], ssum[:]), "dv")
                v.wait_ge(ctx.sem("dv"), ctx.n("dv"))
                ctx.inc(v.tensor_scalar(probs[:], e_sb[:], rsum[:, 0:1], None, ALU.mult), "dv")
                v.wait_ge(ctx.sem("dv"), ctx.n("dv"))
                ctx.inc(v.tensor_reduce(m1p[:].rearrange("p (a b) -> p a b", b=1), probs[:].rearrange("p (a b) -> p a b", b=8), AX.X, ALU.max), "dv")
                v.wait_ge(ctx.sem("dv"), ctx.n("dv"))
                ctx.inc(v.tensor_scalar(oh0_sb[:, sti, :], probs[:], m1p[:, 0:1], None, ALU.is_ge), "dv")
                v.wait_ge(ctx.sem("dv"), ctx.n("dv"))
                ctx.inc(v.tensor_scalar(tnot[:], oh0_sb[:, sti, :], -1.0, 1.0, ALU.mult, ALU.add), "dv")
                v.wait_ge(ctx.sem("dv"), ctx.n("dv"))
                ctx.inc(v.tensor_mul(probs2[:], probs[:], tnot[:]), "dv")
                v.wait_ge(ctx.sem("dv"), ctx.n("dv"))
                ctx.inc(v.tensor_reduce(m2p[:].rearrange("p (a b) -> p a b", b=1), probs2[:].rearrange("p (a b) -> p a b", b=8), AX.X, ALU.max), "dv")
                v.wait_ge(ctx.sem("dv"), ctx.n("dv"))
                ctx.inc(v.tensor_scalar(oh1_sb[:, sti, :], probs2[:], m2p[:, 0:1], None, ALU.is_ge), "dv")
                v.wait_ge(ctx.sem("dv"), ctx.n("dv"))
                ctx.inc(v.tensor_scalar(den[:], m1p[:], 1.0, 1e-9, ALU.mult, ALU.add), "dv")
                v.wait_ge(ctx.sem("dv"), ctx.n("dv"))
                ctx.inc(v.tensor_add(den[:], den[:], m2p[:]), "dv")
                v.wait_ge(ctx.sem("dv"), ctx.n("dv"))
                ctx.inc(v.reciprocal(rden[:], den[:]), "dv")
                v.wait_ge(ctx.sem("dv"), ctx.n("dv"))
                ctx.inc(v.tensor_mul(w0_sb[:, sti:sti + 1], m1p[:], rden[:]), "dv")
                ctx.inc(v.tensor_mul(w1c_sb[:, sti:sti + 1], m2p[:], rden[:]), "dv")
            n_top2 = ctx.n("dv")

            @block.tensor
            def _(t):
                t.wait_ge(ctx.sem("dv"), n_top2)
                t.matmul(psG[:, 8:16], tri[:], oh0_sb[:, sti, :], start=True, stop=False)
                ctx.inc(t.matmul(psG[:, 8:16], ones1[:], off0[:], start=False, stop=True), "pe")
                t.matmul(psG[:, 16:24], tri[:], oh1_sb[:, sti, :], start=True, stop=False)
                ctx.inc(t.matmul(psG[:, 16:24], ones1[:], off1[:], start=False, stop=True), "pe")
                ctx.inc(t.matmul(psG[0:1, 24:32], ones128[:], oh0_sb[:, sti, :], start=True, stop=True), "pe")
                ctx.inc(t.matmul(psG[0:1, 32:40], ones128[:], oh1_sb[:, sti, :], start=True, stop=True), "pe")
            n_cum = ctx.n("pe")

            @block.vector
            def _(v):
                v.wait_ge(ctx.sem("pe"), n_cum)
                ctx.inc(v.tensor_copy(cum0_sb[:, sti, :], psG[:, 8:16]), "dv")
                ctx.inc(v.tensor_copy(cum1_sb[:, sti, :], psG[:, 16:24]), "dv")
                v.wait_ge(ctx.sem("dv"), ctx.n("dv"))
                ctx.inc(v.tensor_add(off0[:], off0[:], psG[0:1, 24:32]), "dv")
                ctx.inc(v.tensor_add(off1[:], off1[:], psG[0:1, 32:40]), "dv")
    n_gate_dv = ctx.n("dv")
    n_gate_wq = ctx.n("wq")

    @block.vector
    def _(v):
        v.wait_ge(ctx.sem("dv"), n_gate_dv)
        ctx.inc(v.tensor_copy(n01[:, 0:8], off0[:]), "dv")
        ctx.inc(v.tensor_copy(n01[:, 8:16], off1[:]), "dv")
        v.wait_ge(ctx.sem("dv"), ctx.n("dv"))
        ctx.inc(v.tensor_tensor(contrib[:].rearrange("o (a b) -> o a b", b=16), rowmask[:].rearrange("o (a b) -> o a b", b=16), n01[:].unsqueeze(1).broadcast_to([1, 8, 16]), ALU.mult), "dv")
    n_contrib = ctx.n("dv")

    @block.sync
    def _(s):
        s.wait_ge(ctx.sem("dv"), n_contrib)
        s.wait_ge(ctx.sem("ld"), ctx.n("ld"))
        ctx.inc(s.dma_start(out=ar_in[:], in_=contrib[:]), "ld", 16)
    n_arin = ctx.n("ld")

    @block.gpsimd
    def _(g):
        g.wait_ge(ctx.sem("ld"), n_arin)
        g.wait_ge(ctx.sem("wq"), ctx.n("wq"))
        if use_cc:
            ctx.inc(g.collective_compute("AllReduce", ALU.add, [[0, 1, 2, 3, 4, 5, 6, 7]], [ar_in[:]], [ar_out[:]]), "cc", 1)
            g.wait_ge(ctx.sem("cc"), ctx.n("cc"))
        else:
            ctx.inc(g.dma_start(out=ar_out[:], in_=ar_in[:]), "wq", 16)
            g.wait_ge(ctx.sem("wq"), ctx.n("wq"))
        ctx.inc(g.dma_start(out=allcnt[:], in_=ar_out[:]), "wq", 16)
    n_allcnt = ctx.n("wq")

    @block.tensor
    def _(t):
        t.wait_ge(ctx.sem("wq"), n_allcnt)
        ctx.wait(t, "dv")
        ctx.inc(t.matmul(psG[:, 40:41], allcnt[:], one11[:], start=True, stop=True), "pe")
    n_acol = ctx.n("pe")

    @block.vector
    def _(v):
        v.wait_ge(ctx.sem("pe"), n_acol)
        ctx.inc(v.tensor_copy(accol[:], psG[:, 40:41]), "dv")
    n_accol = ctx.n("dv")

    @block.tensor
    def _(t):
        t.wait_ge(ctx.sem("dv"), n_accol)
        ctx.inc(t.matmul(psG[0:1, 48:72], accol[:], thmask[:], start=True, stop=True), "pe")
    n_th = ctx.n("pe")

    @block.vector
    def _(v):
        v.wait_ge(ctx.sem("pe"), n_th)
        ctx.inc(v.tensor_copy(th_row[:], psG[0:1, 48:72]), "dv")
        v.wait_ge(ctx.sem("dv"), ctx.n("dv"))
        ctx.inc(v.tensor_scalar(th0[:], th_row[:, 0:8], -1.0, CAP, ALU.mult, ALU.add), "dv")
        ctx.inc(v.tensor_scalar_min(t0k[:], th_row[:, 8:16], CAP), "dv")
        v.wait_ge(ctx.sem("dv"), ctx.n("dv"))
        ctx.inc(v.tensor_scalar(th1[:], th_row[:, 16:24], -1.0, CAP, ALU.mult, ALU.add), "dv")
        v.wait_ge(ctx.sem("dv"), ctx.n("dv"))
        ctx.inc(v.tensor_sub(th1[:], th1[:], t0k[:]), "dv")
    n_thf = ctx.n("dv")

    @block.tensor
    def _(t):
        t.wait_ge(ctx.sem("dv"), n_thf)
        t.matmul(psG[:, 72:80], ones1[:], th0[:], start=True, stop=True)
        ctx.inc(t.matmul(psG[:, 80:88], ones1[:], th1[:], start=True, stop=True), "pe")
    n_thb = ctx.n("pe")

    @block.vector
    def _(v):
        v.wait_ge(ctx.sem("pe"), n_thb)
        ctx.inc(v.tensor_copy(th0b[:], psG[:, 72:80]), "dv")
        ctx.inc(v.tensor_copy(th1b[:], psG[:, 80:88]), "dv")
    n_thbc = ctx.n("dv")

    for sti in range(4 * NCH):
        @block.vector
        def _(v):
            if sti == 0:
                v.wait_ge(ctx.sem("dv"), n_thbc)
            ctx.inc(v.tensor_tensor(keep[:], cum0_sb[:, sti, :], th0b[:], ALU.is_le), "dv")
            ctx.inc(v.tensor_tensor(keep1[:], cum1_sb[:, sti, :], th1b[:], ALU.is_le), "dv")
            v.wait_ge(ctx.sem("dv"), ctx.n("dv"))
            ctx.inc(v.tensor_mul(keep[:], keep[:], oh0_sb[:, sti, :]), "dv")
            ctx.inc(v.tensor_mul(keep1[:], keep1[:], oh1_sb[:, sti, :]), "dv")
            v.wait_ge(ctx.sem("dv"), ctx.n("dv"))
            ctx.inc(v.tensor_scalar(comb[:], keep[:], w0_sb[:, sti:sti + 1], None, ALU.mult), "dv")
            v.wait_ge(ctx.sem("dv"), ctx.n("dv"))
            ctx.inc(v.scalar_tensor_tensor(comb[:], keep1[:], w1c_sb[:, sti:sti + 1], comb[:], ALU.mult, ALU.add), "dv")
        n_comb = ctx.n("dv")

        @block.tensor
        def _(t):
            t.wait_ge(ctx.sem("dv"), n_comb)
            ctx.inc(t.transpose(psG[0:8, 128:256], comb[:], ident[:]), "pe")
        n_tp = ctx.n("pe")

        @block.vector
        def _(v):
            v.wait_ge(ctx.sem("pe"), n_tp)
            ctx.inc(v.tensor_copy(combT[:, sti * 128:(sti + 1) * 128], psG[0:8, 128:256]), "dv")
    n_combT = ctx.n("dv")

    w1h_free = 0
    w2h_free = 0
    evacA_d = [("pe", 0), ("pe", 0), ("pe", 0), ("pe", 0)]
    w2rd_d = 0
    tsb_rd_d = 0

    wqD1_0 = ctx.n("wqD1")
    wqD2_0 = ctx.n("wqD2")
    wqD3_0 = ctx.n("wqD3")

    @block.gpsimd
    def _(g):
        ctx.inc(g.dma_start(out=w1hs[:], in_=w1_r[0, :, :, 0:2048]), "wqD1", 16)
        ctx.inc(g.dma_start(out=w2hs[:], in_=w2_r[0, :, 0:16, :]), "wqD2", 16)
        ctx.inc(g.dma_start(out=b1s[:, 0, :], in_=b1_r[0]), "wqD3", 16)
        ctx.inc(g.dma_start(out=b2s[0:1, 0, :], in_=b2_r[0]), "wqD3", 16)

    for e in range(NE):
        for fh in range(2):
            q = e * 2 + fh
            n_w1l = wqD1_0 + 16 * (q + 1)
            n_w2l = wqD2_0 + 16 * (q + 1)
            n_b12 = wqD3_0 + 32 * (e + 1)

            for c in range(NCH):
                s0 = c * CH
                ac_base = ctx.n("ac")
                for f in range(FT):
                    par = f % 4

                    @block.tensor
                    def _(t):
                        if f == 0 and c == 0:
                            t.wait_ge(ctx.sem("wqD1"), n_w1l)
                            if fh == 0:
                                t.wait_ge(ctx.sem("wqD3"), n_b12)
                        if f >= 4:
                            t.wait_ge(ctx.sem("ac"), ac_base + (f - 4) + 1)
                        elif c > 0 or q > 0:
                            sem_name, val = evacA_d[par]
                            t.wait_ge(ctx.sem(sem_name), val)
                        for dt in range(DT):
                            ins = t.matmul(psA[par][:], w1hs[:, dt, f * 128:(f + 1) * 128], xf_bf[:, dt, s0:s0 + CH], start=(dt == 0), stop=(dt == DT - 1))
                        ctx.inc(ins, "pe")
                    ctx.marks[("d_hmm", f)] = ctx.n("pe")
                n_h_pe = ctx.n("pe")

                @block.scalar
                def _(sc):
                    for f in range(FT):
                        sc.wait_ge(ctx.sem("pe"), ctx.marks[("d_hmm", f)])
                        if c > 0 or q > 0:
                            sc.wait_ge(ctx.sem("pe"), w2rd_d)
                        ctx.inc(sc.activation(h_bf[:, f, :], psA[f % 4][:], AF.Gelu, bias=b1s[:, e % 2, fh * 16 + f:fh * 16 + f + 1]), "ac")
                        if f >= FT - 4:
                            evacA_d[f % 4] = ("ac", ctx.n("ac"))
                n_h = ctx.n("ac")

                if c == NCH - 1 and q < 2 * NE - 1:
                    e2, fh2 = divmod(q + 1, 2)

                    @block.gpsimd
                    def _(g):
                        g.wait_ge(ctx.sem("pe"), n_h_pe)
                        ctx.inc(g.dma_start(out=w1hs[:], in_=w1_r[e2, :, :, fh2 * 2048:(fh2 + 1) * 2048]), "wqD1", 16)
                        if fh2 == 0:
                            ctx.inc(g.dma_start(out=b1s[:, e2 % 2, :], in_=b1_r[e2]), "wqD3", 16)
                            ctx.inc(g.dma_start(out=b2s[0:1, e2 % 2, :], in_=b2_r[e2]), "wqD3", 16)

                @block.tensor
                def _(t):
                    t.wait_ge(ctx.sem("dv"), n_combT)
                    t.wait_ge(ctx.sem("dv"), ctx.marks.get("d_cb_rd", 0))
                    ctx.inc(t.matmul(psB[:], sel[:, e * 128:(e + 1) * 128], combT[:, s0:s0 + CH], start=True, stop=True), "pe")
                n_cb = ctx.n("pe")

                @block.vector
                def _(v):
                    v.wait_ge(ctx.sem("pe"), n_cb)
                    ctx.inc(v.tensor_copy(cb_sb[:], psB[:]), "dv")
                n_cbc = ctx.n("dv")

                for dt in range(DT):
                    par = dt % 2

                    @block.tensor
                    def _(t):
                        if dt == 0 and c == 0:
                            t.wait_ge(ctx.sem("wqD2"), n_w2l)
                        if dt == 0:
                            t.wait_ge(ctx.sem("ac"), n_h)
                        if dt >= 2:
                            t.wait_ge(ctx.sem("dv"), ctx.marks[("d_scale", dt - 2)])
                        elif c > 0 or q > 0:
                            t.wait_ge(ctx.sem("dv"), ctx.marks[("d_scale_prev", dt)])
                        for f in range(FT):
                            last = (f == FT - 1) and (fh == 0)
                            ins = t.matmul(psO[par][:], w2hs[:, f, dt * 128:(dt + 1) * 128], h_bf[:, f, :], start=(f == 0), stop=last)
                        if fh == 1:
                            ins = t.matmul(psO[par][:], b2s[0:1, e % 2, dt * 128:(dt + 1) * 128], onesrow_bf[:], start=False, stop=True)
                        ctx.inc(ins, "pe")
                    n_mm2 = ctx.n("pe")

                    @block.vector
                    def _(v):
                        v.wait_ge(ctx.sem("pe"), n_mm2)
                        v.wait_ge(ctx.sem("dv"), n_cbc)
                        if q > 0 or c > 0:
                            v.wait_ge(ctx.sem("st"), tsb_rd_d)
                        ctx.inc(v.tensor_mul(t_sb[:, dt, :], psO[par][:], cb_sb[:]), "dv")
                    ctx.marks[("d_scale", dt)] = ctx.n("dv")
                for dt in range(DT):
                    ctx.marks[("d_scale_prev", dt)] = ctx.marks[("d_scale", dt)]
                w2rd_d = ctx.n("pe")
                ctx.marks["d_cb_rd"] = ctx.n("dv")
                n_scale = ctx.n("dv")

                @block.gpsimd
                def _(g):
                    g.wait_ge(ctx.sem("dv"), n_scale)
                    g.wait_ge(ctx.sem("st"), ctx.n("st"))
                    op = ALU.bypass if (e == 0 and fh == 0) else ALU.add
                    ctx.inc(g.dma_start(out=acc_r[:, :, s0:s0 + CH], in_=t_sb[:], accum_op=op), "st", 16)
                tsb_rd_d = ctx.n("st")

            if q < 2 * NE - 1:
                e2, fh2 = divmod(q + 1, 2)

                @block.gpsimd
                def _(g):
                    g.wait_ge(ctx.sem("pe"), w2rd_d)
                    ctx.inc(g.dma_start(out=w2hs[:], in_=w2_r[e2, :, fh2 * 16:(fh2 + 1) * 16, :]), "wqD2", 16)
    n_acc_done = ctx.n("st")


    xb = [x3_sb, w1hs[:, 0:4, :].bitcast(F32).rearrange("p a (x b) -> p (a x) b", b=CH)]
    tb = [t_sb, w1hs[:, 4:8, :].bitcast(F32).rearrange("p a (x b) -> p (a x) b", b=CH)]
    sq2 = [w2hs[:, 0:2, :].bitcast(F32).rearrange("p a (x b) -> p (a x) b", b=CH)[:, 0:2, :],
           w2hs[:, 2:4, :].bitcast(F32).rearrange("p a (x b) -> p (a x) b", b=CH)[:, 0:2, :]]
    tmp2 = [w2hs[:, 4:6, :].bitcast(F32).rearrange("p a (x b) -> p (a x) b", b=CH)[:, 0:2, :],
            w2hs[:, 6:8, :].bitcast(F32).rearrange("p a (x b) -> p (a x) b", b=CH)[:, 0:2, :]]
    mb2 = [w2hs[:, 8, :].bitcast(F32), w2hs[:, 9, :].bitcast(F32)]
    ivb2 = [w2hs[:, 10, :].bitcast(F32), w2hs[:, 11, :].bitcast(F32)]
    mrow2 = [w2hs[0:1, 12, :].bitcast(F32), w2hs[0:1, 13, :].bitcast(F32)]
    vrow2 = [w2hs[0:1, 14, :].bitcast(F32), w2hs[0:1, 15, :].bitcast(F32)]
    trow2 = [xf_bf[0:1, 0, 0:1024].bitcast(F32), xf_bf[0:1, 1, 0:1024].bitcast(F32)]
    emit_postln_pipelined(ctx, block, T, xin_r, xout_r, acc_r, n_acc_done, "Dp",
                          xb, tb, sq2, tmp2, mb2, ivb2, mrow2, vrow2, trow2,
                          [psA[0], psA[1], psO[0], psO[1]], ones128, ones1)
    return ctx.n("st2")


def blockD_host_inputs(p, core_id):
    sel = np.zeros((8, 8 * 128), np.float32)
    for e in range(8):
        sel[e, e * 128:(e + 1) * 128] = 1.0
    rowmask = np.zeros((1, 128), np.float32)
    rowmask[0, core_id * 16:(core_id + 1) * 16] = 1.0
    thmask = np.zeros((128, 24), np.float32)
    for cp in range(8):
        for kp in range(2):
            for ep in range(8):
                pidx = cp * 16 + kp * 8 + ep
                if kp == 0 and cp < core_id:
                    thmask[pidx, ep] = 1.0
                if kp == 0:
                    thmask[pidx, 8 + ep] = 1.0
                if kp == 1 and cp < core_id:
                    thmask[pidx, 16 + ep] = 1.0
    b1fm = np.stack([p["b1"][e].reshape(32, 128).T for e in range(8)])
    return {
        "d_Wg": np.ascontiguousarray(p["Wg"]),
        "d_W1": np.ascontiguousarray(p["W1"]),
        "d_W2": np.ascontiguousarray(p["W2"]),
        "d_b1fm": np.ascontiguousarray(b1fm),
        "d_b2r": np.ascontiguousarray(p["b2"][:, None, :]),
        "d_tri": np.tril(np.ones((128, 128), np.float32)).T.copy(),
        "d_ident": np.eye(128, dtype=np.float32),
        "d_sel": sel,
        "d_rowmask": rowmask,
        "d_thmask": thmask,
    }


def declare_blockD_drams(nc, T):
    F = mybir.dt.float32
    shapes = {
        "d_Wg": [1024, 8], "d_W1": [8, 1024, 4096], "d_W2": [8, 4096, 1024],
        "d_b1fm": [8, 128, 32], "d_b2r": [8, 1, 1024],
        "d_tri": [128, 128], "d_ident": [128, 128], "d_sel": [8, 8 * 128],
        "d_rowmask": [1, 128], "d_thmask": [128, 24],
    }
    return {k: nc.dram_tensor(k, v, F, kind="ExternalInput").ap() for k, v in shapes.items()}



def build_fused(T=4096, use_cc=True):
    nc = bass.Bass("TRN2", target_bir_lowering=False, debug=False)
    F = F32
    xin = nc.dram_tensor("xin", [1024, T], F, kind="ExternalInput").ap()
    xout = nc.dram_tensor("xout", [1024, T], F, kind="ExternalOutput").ap()
    x1buf = nc.dram_tensor("x1buf", [1024, T], F, kind="Internal").ap()
    x2buf = nc.dram_tensor("x2buf", [1024, T], F, kind="Internal").ap()
    x3buf = nc.dram_tensor("x3buf", [1024, T], F, kind="Internal").ap()
    accbuf = nc.dram_tensor("accbuf", [1024, T], F, kind="Internal").ap()
    kvscr = nc.dram_tensor("kvscr", [2, 64, 8, 64], BF16, kind="Internal").ap()
    ksscr = nc.dram_tensor("ksscr", [8, 2, 64], F, kind="Internal").ap()
    ar_in = nc.dram_tensor("ar_in", [1, 128], F, kind="Internal").ap()
    ar_out = nc.dram_tensor("ar_out", [1, 128], F, kind="Internal").ap()
    WA = declare_blockA_drams(nc, T)
    WB = declare_blockB_drams(nc, T)
    WC = declare_blockC_drams(nc, T)
    WD = declare_blockD_drams(nc, T)

    outer = contextlib.ExitStack()
    with outer:
        ctx = Ctx(nc, outer)
        with nc.Block() as block:
            with contextlib.ExitStack() as es_a:
                ctx.es = es_a
                nA = emit_blockA(ctx, block, T, xin, x1buf, WA)
            emit_barrier(ctx, block, "st2", nA)
            with contextlib.ExitStack() as es_b:
                ctx.es = es_b
                nB = emit_blockB(ctx, block, T, x1buf, x2buf, WB, accbuf)
            emit_barrier(ctx, block, "st2", nB)
            with contextlib.ExitStack() as es_c:
                ctx.es = es_c
                nC = emit_blockC(ctx, block, T, x2buf, x3buf, WC, kvscr, ksscr)
            emit_barrier(ctx, block, "st2", nC)
            with contextlib.ExitStack() as es_d:
                ctx.es = es_d
                nD = emit_blockD(ctx, block, T, x3buf, xout, WD, accbuf, ar_in, ar_out, use_cc=use_cc)

            @block.sync
            def _(s):
                s.wait_ge(ctx.sem("st2"), nD)
    return nc



import concourse.bass_utils as _bu


def kernel(**inputs):
    T = 4096
    NCORES = 8

    x_full = np.asarray(inputs["L0_attn_x"], dtype=np.float32)

    pA = {k: np.asarray(inputs["L0_attn_" + k], dtype=np.float32) for k in
          ["ln_g", "ln_b", "W_in", "b_in", "conv_w", "conv_b", "A", "Bm",
           "W_c", "b_c", "W_d", "b_d", "log_delta", "W_out", "b_out"]}
    pB = {"gate_w": np.asarray(inputs["L0_ffn_gate_w"], dtype=np.float32),
          "w1": np.asarray(inputs["L0_ffn_w1"], dtype=np.float32),
          "w2": np.asarray(inputs["L0_ffn_w2"], dtype=np.float32),
          "w3": np.asarray(inputs["L0_ffn_w3"], dtype=np.float32)}
    pC = {"W_qkv": np.asarray(inputs["L1_attn_W_qkv"], dtype=np.float32)}
    pD = {"Wg": np.asarray(inputs["L1_ffn_Wg"], dtype=np.float32),
          "W1": np.asarray(inputs["L1_ffn_W1"], dtype=np.float32),
          "b1": np.asarray(inputs["L1_ffn_b1"], dtype=np.float32),
          "W2": np.asarray(inputs["L1_ffn_W2"], dtype=np.float32),
          "b2": np.asarray(inputs["L1_ffn_b2"], dtype=np.float32)}

    WA = blockA_host_inputs(pA)
    WB = blockB_host_inputs(pB)
    WC = blockC_host_inputs(pC)

    nc = build_fused(T=T, use_cc=True)

    maps = []
    for c in range(NCORES):
        m = {"xin": np.ascontiguousarray(x_full[c].T)}
        m.update(WA)
        m.update(WB)
        m.update(WC)
        m.update(blockD_host_inputs(pD, c))
        maps.append(m)

    res = _bu.run_bass_kernel_spmd(nc, maps, core_ids=list(range(NCORES)))
    out = np.stack([np.asarray(res.results[c]["xout"]).T for c in range(NCORES)], axis=0).astype(np.float32)
    return out


# version 13
# speedup vs baseline: 1.0578x; 1.0578x over previous
import contextlib
import numpy as np
import concourse.bass as bass
import concourse.mybir as mybir

F32 = mybir.dt.float32
BF16 = mybir.dt.bfloat16
AF = mybir.ActivationFunctionType
ALU = mybir.AluOpType
AX = mybir.AxisListType

CH = 512


class Ctx:
    def __init__(self, nc, es):
        self.nc = nc
        self.es = es
        self.sem_es = es
        self.sems = {}
        self.cnt = {}
        self.marks = {}
        self.sb = {}
        self.ps = {}

    def sem(self, name):
        if name not in self.sems:
            self.sems[name] = self.sem_es.enter_context(self.nc.semaphore(name))
            self.cnt[name] = 0
        return self.sems[name]

    def inc(self, instr, name, k=1):
        s = self.sem(name)
        self.cnt[name] += k
        instr.then_inc(s, k)
        return self.cnt[name]

    def n(self, name):
        self.sem(name)
        return self.cnt[name]

    def wait(self, eng, name, val=None):
        s = self.sem(name)
        eng.wait_ge(s, self.cnt[name] if val is None else val)

    def sbuf(self, name, shape, dtype):
        t = self.es.enter_context(self.nc.sbuf_tensor(name, shape, dtype))
        self.sb[name] = t
        return t

    def psum(self, name, shape, dtype=F32):
        t = self.es.enter_context(self.nc.psum_tensor(name, shape, dtype))
        self.ps[name] = t
        return t



def emit_blockA(ctx, block, T, xin, xout, W):
    nc = ctx.nc
    NCH = T // CH
    DT = 8
    ET = 16

    xin_r = xin.rearrange("(t p) s -> p t s", p=128)
    xout_r = xout.rearrange("(t p) s -> p t s", p=128)
    w_in_r = W["a_w_inT"].rearrange("(t p) f -> p t f", p=128)
    w_c_r = W["a_w_cT"].rearrange("(t p) f -> p t f", p=128)
    w_d_r = W["a_w_dT"].rearrange("(t p) f -> p t f", p=128)
    w_out_r = W["a_w_outT"].rearrange("(t p) f -> p t f", p=128)

    x_sb2 = ctx.sbuf("a_x2", [128, 2, DT, CH], F32)
    mean_sb2 = ctx.sbuf("a_mean2", [1, 2, CH], F32)
    var_sb2 = ctx.sbuf("a_var2", [1, 2, CH], F32)
    tmprow2 = ctx.sbuf("a_tmprow2", [1, 2, CH], F32)
    mb_sb2 = ctx.sbuf("a_mb2", [128, 2, CH], F32)
    invb_sb2 = ctx.sbuf("a_invb2", [128, 2, CH], F32)
    tmp_ln = ctx.sbuf("a_tmpln", [128, 2, CH], F32)
    tmp_po = ctx.sbuf("a_tmppo", [128, 2, CH], F32)
    sq_ln = ctx.sbuf("a_sqln", [128, DT, CH], BF16)
    ones128b_a = ctx.sbuf("a_ones128b", [128, 1], BF16)
    ln1_bf2 = ctx.sbuf("a_ln12", [128, 2, DT, CH], BF16)
    NSB = 3
    ws_in = ctx.sbuf("a_ws_in", [128, NSB, DT, 128], BF16)
    ws_c = ctx.sbuf("a_ws_c", [128, 2, ET, 128], BF16)
    ws_d = ctx.sbuf("a_ws_d", [128, 2, ET, 128], BF16)
    w_out_sb = ctx.sbuf("a_w_out_sb", [128, ET, 1024], BF16)
    xp_ext = ctx.sbuf("a_xpext", [128, 2, CH + 3], F32)
    acc_sb = ctx.sbuf("a_acc", [128, 2, CH], F32)
    sq = acc_sb
    carry_sb = ctx.sbuf("a_carry", [128, ET, 3], F32)
    xc_bf = ctx.sbuf("a_xc", [128, ET, CH], BF16)
    sg_bf = ctx.sbuf("a_sg", [128, ET, CH], BF16)
    sd_sb = ctx.sbuf("a_sd", [128, 2, CH], F32)
    u_sb = ctx.sbuf("a_u", [128, 2, CH], F32)
    v_sb = ctx.sbuf("a_v", [128, 2, CH], F32)
    vcarry = ctx.sbuf("a_vcarry", [128, ET], F32)
    abm = sd_sb[:, 0, 0:256].rearrange("p (a b) -> p a b", b=16)
    bm2 = sd_sb[:, 1, 0:256].rearrange("p (a b) -> p a b", b=16)
    gated_bf = ctx.sbuf("a_gated", [128, ET, CH], BF16)
    ones128 = ctx.sbuf("a_ones128", [128, 1], F32)
    ones1 = ctx.sbuf("a_ones1", [1, 128], F32)
    lng = ctx.sbuf("a_lng", [128, DT], F32)
    lnb = ctx.sbuf("a_lnb", [128, DT], F32)
    binxp = ctx.sbuf("a_binxp", [128, ET], F32)
    bing = ctx.sbuf("a_bing", [128, ET], F32)
    convw = ctx.sbuf("a_convw", [128, ET, 4], F32)
    convb = ctx.sbuf("a_convb", [128, ET], F32)
    cfm = ctx.sbuf("a_cfm", [128, ET], F32)
    ld_fm = ctx.sbuf("a_ldfm", [128, ET], F32)
    a_fm = ctx.sbuf("a_afm", [128, ET], F32)
    sp_t1 = v_sb[:, 0, 0:ET]
    sp_t2 = sd_sb[:, 0, 0:ET]
    beta = ctx.sbuf("a_beta", [128, ET], F32)
    wsum = ctx.sbuf("a_wsum", [128, ET], F32)
    corr = u_sb[:, 0, 0:ET * 3].rearrange("p (a b) -> p a b", b=3)
    bc_fm = ctx.sbuf("a_bcfm", [128, ET], F32)
    bd_fm = ctx.sbuf("a_bdfm", [128, ET], F32)
    bout_fm = ctx.sbuf("a_bout", [128, DT], F32)

    psA = [ctx.psum("a_psA0", [128, CH]), ctx.psum("a_psA1", [128, CH])]
    psC = [ctx.psum("a_psC0", [128, CH]), ctx.psum("a_psC1", [128, CH])]
    psS4 = [ctx.psum("a_psS0", [128, CH]), ctx.psum("a_psS1", [128, CH]),
            ctx.psum("a_psS2", [128, CH]), ctx.psum("a_psS3", [128, CH])]
    psB1 = psS4[2]
    psB2 = psS4[3]

    @block.sync
    def _(s):
        for name, dst in [
            ("a_lng_fm", lng), ("a_lnb_fm", lnb), ("a_binxp_fm", binxp),
            ("a_bing_fm", bing), ("a_convw_fm", convw), ("a_convb_fm", convb),
            ("a_A_fm", abm), ("a_Bm_fm", bm2), ("a_ld_fm", ld_fm),
            ("a_bc_fm", bc_fm), ("a_bd_fm", bd_fm), ("a_bout_fm", bout_fm),
        ]:
            ctx.inc(s.dma_start(out=dst[:], in_=W[name]), "ld", 16)
    n_param_ld = ctx.n("ld")

    @block.gpsimd
    def _(g):
        ctx.inc(g.memset(ones128[:], 1.0), "pl")
        ctx.inc(g.memset(ones1[:], 1.0), "pl")
        ctx.inc(g.memset(carry_sb[:], 0.0), "pl")
        ctx.inc(g.memset(vcarry[:], 0.0), "pl")
        ctx.inc(g.memset(ones128b_a[:], 1.0), "pl")
        g.wait_ge(ctx.sem("wqo0"), ctx.n("wqo0"))
        ctx.inc(g.dma_start(out=w_out_sb[:], in_=w_out_r[:]), "wqo0", 16)
    n_init_pl = ctx.n("pl")
    n_wout_ld = ctx.n("wqo0")

    @block.vector
    def _(v):
        v.wait_ge(ctx.sem("ld"), n_param_ld)
        ctx.inc(v.tensor_mul(abm[:], abm[:], bm2[:]), "dv")
        v.wait_ge(ctx.sem("dv"), ctx.n("dv"))
        ctx.inc(v.tensor_reduce(cfm[:].rearrange("p (a b) -> p a b", b=1), abm[:], AX.X, ALU.add), "dv")
        ctx.inc(v.tensor_reduce(wsum[:].rearrange("p (a b) -> p a b", b=1), convw[:], AX.X, ALU.add), "dv")
        v.wait_ge(ctx.sem("dv"), ctx.n("dv"))
        ctx.inc(v.tensor_mul(beta[:], binxp[:], wsum[:]), "dv")
        v.wait_ge(ctx.sem("dv"), ctx.n("dv"))
        ctx.inc(v.tensor_add(beta[:], beta[:], convb[:]), "dv")
        v.wait_ge(ctx.sem("dv"), ctx.n("dv"))
        ctx.inc(v.tensor_copy(corr[:, :, 2], convw[:, :, 0]), "dv")
        v.wait_ge(ctx.sem("dv"), ctx.n("dv"))
        ctx.inc(v.tensor_add(corr[:, :, 1], convw[:, :, 0], convw[:, :, 1]), "dv")
        v.wait_ge(ctx.sem("dv"), ctx.n("dv"))
        ctx.inc(v.tensor_add(corr[:, :, 0], corr[:, :, 1], convw[:, :, 2]), "dv")
        v.wait_ge(ctx.sem("dv"), ctx.n("dv"))
        for k in range(3):
            ctx.inc(v.tensor_mul(corr[:, :, k], corr[:, :, k], binxp[:]), "dv")
            v.wait_ge(ctx.sem("dv"), ctx.n("dv"))
        ctx.inc(v.tensor_scalar_mul(corr[:], corr[:], -1.0), "dv")

    @block.scalar
    def _(sc):
        sc.wait_ge(ctx.sem("ld"), n_param_ld)
        ctx.inc(sc.activation(sp_t1[:], ld_fm[:], AF.Abs), "ac")
        sc.wait_ge(ctx.sem("ac"), ctx.n("ac"))
        ctx.inc(sc.activation(sp_t1[:], sp_t1[:], AF.Exp, scale=-1.0), "ac")
        sc.wait_ge(ctx.sem("ac"), ctx.n("ac"))
        ctx.inc(sc.activation(sp_t1[:], sp_t1[:], AF.Ln, bias=1.0), "ac")
    n_sp_ac = ctx.n("ac")

    @block.vector
    def _(v):
        v.wait_ge(ctx.sem("ac"), n_sp_ac)
        ctx.inc(v.tensor_scalar_max(sp_t2[:], ld_fm[:], 0.0), "dv")
        v.wait_ge(ctx.sem("dv"), ctx.n("dv"))
        ctx.inc(v.tensor_add(sp_t2[:], sp_t2[:], sp_t1[:]), "dv")
    n_sp_dv = ctx.n("dv")

    @block.scalar
    def _(sc):
        sc.wait_ge(ctx.sem("dv"), n_sp_dv)
        ctx.inc(sc.activation(a_fm[:], sp_t2[:], AF.Exp, scale=-1.0), "ac")
    n_pre_ac = ctx.n("ac")
    n_pre_dv = ctx.n("dv")

    @block.tensor
    def _(t):
        t.wait_ge(ctx.sem("pl"), n_init_pl)
        t.wait_ge(ctx.sem("dv"), n_pre_dv)
        t.wait_ge(ctx.sem("ac"), n_pre_ac)

    slotA = [0, 0, 0]
    slotCD = [0, 0, 0]
    slotO = [0, 0]

    def emit_lnphase(c):
        pp = c % 2
        s0 = c * CH
        xs = x_sb2[:, pp]
        pS, pQ = psS4[2 * pp], psS4[2 * pp + 1]

        @block.sync
        def _(s):
            if c >= 2:
                s.wait_ge(ctx.sem("st2"), ctx.marks[("a_st2", c - 2)])
            s.wait_ge(ctx.sem("ld"), ctx.n("ld"))
            ctx.inc(s.dma_start(out=xs[:], in_=xin_r[:, :, s0:s0 + CH]), "ld", 16)
        n_xld = ctx.n("ld")

        @block.gpsimd
        def _(g):
            g.wait_ge(ctx.sem("ld"), n_xld)
            g.wait_ge(ctx.sem("pe"), ctx.marks.get("a_sqln_free", 0))
            for dt in range(DT):
                ctx.inc(g.tensor_mul(sq_ln[:, dt, :], xs[:, dt, :], xs[:, dt, :]), "pl")
        n_sqall = ctx.n("pl")

        @block.tensor
        def _(t):
            t.wait_ge(ctx.sem("pl"), n_sqall)
            t.wait_ge(ctx.sem("dv"), ctx.marks.get(("a_psS_rd", pp), 0))
            for dt in range(DT):
                t.matmul(pS[0:1, :], ones128[:], xs[:, dt, :], start=(dt == 0), stop=(dt == DT - 1))
            for dt in range(DT):
                ins = t.matmul(pQ[0:1, :], ones128b_a[:], sq_ln[:, dt, :], start=(dt == 0), stop=(dt == DT - 1))
            ctx.inc(ins, "pe")
        n_stmm = ctx.n("pe")
        ctx.marks["a_sqln_free"] = n_stmm

        @block.vector
        def _(v):
            v.wait_ge(ctx.sem("pe"), n_stmm)
            ctx.inc(v.tensor_scalar(mean_sb2[:, pp, :], pS[0:1, :], 1.0 / 1024.0, None, ALU.mult), "dv")
            v.wait_ge(ctx.sem("dv"), ctx.n("dv"))
            ctx.inc(v.tensor_scalar(var_sb2[:, pp, :], pQ[0:1, :], 1.0 / 1024.0, 1e-5, ALU.mult, ALU.add), "dv")
            v.wait_ge(ctx.sem("dv"), ctx.n("dv"))
            ctx.inc(v.tensor_mul(tmprow2[:, pp, :], mean_sb2[:, pp, :], mean_sb2[:, pp, :]), "dv")
            v.wait_ge(ctx.sem("dv"), ctx.n("dv"))
            ctx.inc(v.tensor_sub(var_sb2[:, pp, :], var_sb2[:, pp, :], tmprow2[:, pp, :]), "dv")
        n_var = ctx.n("dv")

        @block.scalar
        def _(sc):
            sc.wait_ge(ctx.sem("dv"), n_var)
            ctx.inc(sc.activation(var_sb2[:, pp, :], var_sb2[:, pp, :], AF.Ln), "ac")
            sc.wait_ge(ctx.sem("ac"), ctx.n("ac"))
            ctx.inc(sc.activation(var_sb2[:, pp, :], var_sb2[:, pp, :], AF.Exp, scale=-0.5), "ac")
        n_inv = ctx.n("ac")

        @block.tensor
        def _(t):
            t.wait_ge(ctx.sem("ac"), n_inv)
            t.matmul(pS[:], ones1[:], mean_sb2[:, pp, :], start=True, stop=True)
            ins = t.matmul(pQ[:], ones1[:], var_sb2[:, pp, :], start=True, stop=True)
            ctx.inc(ins, "pe")
        n_bc = ctx.n("pe")

        @block.vector
        def _(v):
            v.wait_ge(ctx.sem("pe"), n_bc)
            ctx.inc(v.tensor_copy(mb_sb2[:, pp, :], pS[:]), "dv")
            ctx.inc(v.tensor_copy(invb_sb2[:, pp, :], pQ[:]), "dv")
        n_bcc = ctx.n("dv")
        ctx.marks[("a_psS_rd", pp)] = n_bcc

        dv_base = ctx.n("dv")

        @block.gpsimd
        def _(g):
            g.wait_ge(ctx.sem("dv"), n_bcc)
            for dt in range(DT):
                if dt >= 2:
                    g.wait_ge(ctx.sem("dv"), dv_base + 2 * (dt - 2) + 2)
                ctx.inc(g.tensor_sub(tmp_ln[:, dt % 2, :], xs[:, dt, :], mb_sb2[:, pp, :]), "pl")
                ctx.marks[("a_lnsub", dt)] = ctx.n("pl")

        @block.vector
        def _(v):
            for dt in range(DT):
                v.wait_ge(ctx.sem("pl"), ctx.marks[("a_lnsub", dt)])
                ctx.inc(v.scalar_tensor_tensor(tmp_ln[:, dt % 2, :], tmp_ln[:, dt % 2, :], lng[:, dt:dt + 1], invb_sb2[:, pp, :], ALU.mult, ALU.mult), "dv")
                v.wait_ge(ctx.sem("dv"), ctx.n("dv"))
                ctx.inc(v.tensor_scalar(ln1_bf2[:, pp, dt, :], tmp_ln[:, dt % 2, :], lnb[:, dt:dt + 1], None, ALU.add), "dv")
        ctx.marks[("a_ln1_done", c)] = ctx.n("dv")

    def emit_xproj(c):
        pp = c % 2
        ln1_bf = ln1_bf2[:, pp]
        ac_base_x = ctx.n("ac")
        dv_base_x = ctx.n("dv")
        PSX = [psA[0], psA[1], psC[0], psC[1]]
        per_x = 7 + (3 if c == 0 else 0)

        def dv_after_conv(ff):
            return dv_base_x + min(ff + 1, 16) * per_x

        for f in range(32):
            par = f % 4
            slot = f % NSB
            wsem = f"wqa{slot}"

            @block.gpsimd
            def _(g):
                g.wait_ge(ctx.sem("pe"), slotA[slot])
                ctx.inc(g.dma_start(out=ws_in[:, slot, :, :], in_=w_in_r[:, :, f * 128:(f + 1) * 128]), wsem, 16)
            n_wq = ctx.n(wsem)

            @block.tensor
            def _(t):
                t.wait_ge(ctx.sem(wsem), n_wq)
                if f == 0:
                    t.wait_ge(ctx.sem("dv"), ctx.marks[("a_ln1_done", c)])
                if f >= 4:
                    t.wait_ge(ctx.sem("ac"), ac_base_x + (f - 4) + 1)
                else:
                    sem_name, val = ctx.marks.get(("psfree", id(PSX[par])), ("pe", 0))
                    t.wait_ge(ctx.sem(sem_name), val)
                for dt in range(DT):
                    ins = t.matmul(PSX[par][:], ws_in[:, slot, dt, :], ln1_bf[:, dt, :], start=(dt == 0), stop=(dt == DT - 1))
                ctx.inc(ins, "pe")
            ctx.marks[("a_xmm", f)] = ctx.n("pe")
            slotA[slot] = ctx.n("pe")

            if f < 16:
                @block.vector
                def _(v):
                    v.wait_ge(ctx.sem("pe"), ctx.marks[("a_xmm", f)])
                    if f >= 2:
                        v.wait_ge(ctx.sem("ac"), ac_base_x + (f - 2) + 1)
                    if f < 2:
                        v.wait_ge(ctx.sem("pe"), ctx.marks.get("a_postmm", 0))
                    ctx.inc(v.tensor_copy(xp_ext[:, f % 2, 3:CH + 3], PSX[par][:]), "dv")
                    v.wait_ge(ctx.sem("dv"), ctx.n("dv"))
                    ctx.inc(v.tensor_copy(xp_ext[:, f % 2, 0:3], carry_sb[:, f, :]), "dv")
                    v.wait_ge(ctx.sem("dv"), ctx.n("dv"))
                    ctx.inc(v.tensor_copy(carry_sb[:, f, :], xp_ext[:, f % 2, CH:CH + 3]), "dv")
                    v.wait_ge(ctx.sem("dv"), ctx.n("dv"))
                    ctx.inc(v.tensor_scalar(acc_sb[:, f % 2, :], xp_ext[:, f % 2, 3:CH + 3], convw[:, f, 3:4], None, ALU.mult), "dv")
                    for k, sl in [(2, (2, CH + 2)), (1, (1, CH + 1)), (0, (0, CH))]:
                        v.wait_ge(ctx.sem("dv"), ctx.n("dv"))
                        ctx.inc(v.scalar_tensor_tensor(acc_sb[:, f % 2, :], xp_ext[:, f % 2, sl[0]:sl[1]], convw[:, f, k:k + 1], acc_sb[:, f % 2, :], ALU.mult, ALU.add), "dv")
                    if c == 0:
                        for k in range(3):
                            v.wait_ge(ctx.sem("dv"), ctx.n("dv"))
                            ctx.inc(v.tensor_add(acc_sb[:, f % 2, k:k + 1], acc_sb[:, f % 2, k:k + 1], corr[:, f, k:k + 1]), "dv")
                ctx.marks[("psfree", id(PSX[par]))] = ("dv", dv_base_x + f * per_x + 1)

        @block.scalar
        def _(sc):
            for f in range(16):
                sc.wait_ge(ctx.sem("dv"), dv_after_conv(f))
                ctx.inc(sc.activation(xc_bf[:, f, :], acc_sb[:, f % 2, :], AF.Silu, bias=beta[:, f:f + 1]), "ac")
            for f in range(16, 32):
                sc.wait_ge(ctx.sem("pe"), ctx.marks[("a_xmm", f)])
                ctx.inc(sc.activation(sg_bf[:, f - 16, :], PSX[f % 4][:], AF.Sigmoid, bias=bing[:, f - 16:f - 16 + 1]), "ac")
                ctx.marks[("psfree", id(PSX[f % 4]))] = ("ac", ctx.n("ac"))
        return ctx.n("ac")

    def emit_uphase(c, n_xproj_ac):
        pp = c % 2
        PSU_C = [psA[0], psA[1], psS4[2 * pp]]
        PSU_D = [psC[0], psC[1], psS4[2 * pp + 1]]
        dv_base_u = ctx.n("dv")
        for g_i in range(ET):
            par = g_i % 3
            slot = g_i % 2
            wsc = f"wqc{slot}"
            wsd = f"wqd{slot}"

            @block.gpsimd
            def _(g):
                g.wait_ge(ctx.sem("pe"), slotCD[slot])
                ctx.inc(g.dma_start(out=ws_c[:, slot, :, :], in_=w_c_r[:, :, g_i * 128:(g_i + 1) * 128]), wsc, 16)
                ctx.inc(g.dma_start(out=ws_d[:, slot, :, :], in_=w_d_r[:, :, g_i * 128:(g_i + 1) * 128]), wsd, 16)
            n_wq = ctx.n(wsc)
            n_wqd = ctx.n(wsd)

            @block.tensor
            def _(t):
                t.wait_ge(ctx.sem(wsc), n_wq)
                t.wait_ge(ctx.sem(wsd), n_wqd)
                if g_i == 0:
                    t.wait_ge(ctx.sem("ac"), n_xproj_ac)
                if g_i < 2:
                    sem_name, val = ctx.marks.get(("psfree", id(PSU_C[par])), ("pe", 0))
                    t.wait_ge(ctx.sem(sem_name), val)
                    sem_name, val = ctx.marks.get(("psfree", id(PSU_D[par])), ("pe", 0))
                    t.wait_ge(ctx.sem(sem_name), val)
                elif g_i == 2:
                    t.wait_ge(ctx.sem("dv"), ctx.marks.get(("a_psS_rd", pp), 0))
                else:
                    t.wait_ge(ctx.sem("ac"), ctx.marks.get(("a_usd", g_i - 3), 0))
                    t.wait_ge(ctx.sem("dv"), dv_base_u + 4 * (g_i - 3) + 1)
                for e in range(ET):
                    ins = t.matmul(PSU_C[par][:], ws_c[:, slot, e, :], xc_bf[:, e, :], start=(e == 0), stop=(e == ET - 1))
                ctx.inc(ins, "pe")
                for e in range(ET):
                    ins = t.matmul(PSU_D[par][:], ws_d[:, slot, e, :], xc_bf[:, e, :], start=(e == 0), stop=(e == ET - 1))
                ctx.inc(ins, "pe")
            n_mm = ctx.n("pe")
            slotCD[slot] = n_mm

            @block.scalar
            def _(sc):
                sc.wait_ge(ctx.sem("pe"), n_mm)
                if g_i >= 2:
                    sc.wait_ge(ctx.sem("dv"), dv_base_u + 4 * (g_i - 2) + 1)
                ctx.inc(sc.activation(sd_sb[:, g_i % 2, :], PSU_D[par][:], AF.Sigmoid, bias=bd_fm[:, g_i:g_i + 1]), "ac")
            n_sd = ctx.n("ac")
            ctx.marks[("a_usd", g_i)] = n_sd

            @block.vector
            def _(v):
                v.wait_ge(ctx.sem("ac"), n_sd)
                ctx.inc(v.scalar_tensor_tensor(u_sb[:, g_i % 2, :], PSU_C[par][:], bc_fm[:, g_i:g_i + 1], sd_sb[:, g_i % 2, :], ALU.add, ALU.mult), "dv")
                v.wait_ge(ctx.sem("dv"), ctx.n("dv"))
                ctx.inc(v.tensor_tensor_scan(v_sb[:, g_i % 2, :], a_fm[:, g_i:g_i + 1].broadcast_to([128, CH]), u_sb[:, g_i % 2, :], vcarry[:, g_i:g_i + 1], ALU.mult, ALU.add), "dv")
                v.wait_ge(ctx.sem("dv"), ctx.n("dv"))
                ctx.inc(v.tensor_copy(vcarry[:, g_i:g_i + 1], v_sb[:, g_i % 2, CH - 1:CH]), "dv")
                ctx.inc(v.scalar_tensor_tensor(gated_bf[:, g_i, :], v_sb[:, g_i % 2, :], cfm[:, g_i:g_i + 1], sg_bf[:, g_i, :], ALU.mult, ALU.mult), "dv")
        return ctx.n("dv")

    def emit_outpost(c, n_gated):
        pp = c % 2
        s0 = c * CH
        xs = x_sb2[:, pp]
        pS, pQ = psS4[2 * pp], psS4[2 * pp + 1]

        @block.gpsimd
        def _(g):
            g.wait_ge(ctx.sem("dv"), n_gated)
            for dt in range(DT):
                ctx.inc(g.tensor_scalar_mul(xs[:, dt, :], xs[:, dt, :], 2.0), "pl")
        n_dbl = ctx.n("pl")

        for dt in range(DT):
            par = dt % 2

            @block.tensor
            def _(t):
                if dt == 0:
                    t.wait_ge(ctx.sem("dv"), n_gated)
                    if c == 0:
                        t.wait_ge(ctx.sem("wqo0"), n_wout_ld)
                if dt >= 2:
                    t.wait_ge(ctx.sem("dv"), ctx.marks[("a_resid", dt - 2)])
                else:
                    sem_name, val = ctx.marks.get(("psfree", id(psA[par])), ("pe", 0))
                    t.wait_ge(ctx.sem(sem_name), val)
                for g_i in range(ET):
                    ins = t.matmul(psA[par][:], w_out_sb[:, g_i, dt * 128:(dt + 1) * 128], gated_bf[:, g_i, :], start=(g_i == 0), stop=(g_i == ET - 1))
                ctx.inc(ins, "pe")
            n_mm = ctx.n("pe")

            @block.vector
            def _(v):
                v.wait_ge(ctx.sem("pe"), n_mm)
                v.wait_ge(ctx.sem("pl"), n_dbl)
                ctx.inc(v.scalar_tensor_tensor(xs[:, dt, :], psA[par][:], bout_fm[:, dt:dt + 1], xs[:, dt, :], ALU.add, ALU.add), "dv")
            ctx.marks[("a_resid", dt)] = ctx.n("dv")
            ctx.marks[("psfree", id(psA[par]))] = ("dv", ctx.n("dv"))
        n_resid = ctx.n("dv")

        pl_b = ctx.n("pl")

        @block.gpsimd
        def _(g):
            g.wait_ge(ctx.sem("dv"), n_resid)
            ctx.wait(g, "ac")
            for dt in range(2):
                ctx.inc(g.tensor_mul(sq[:, dt, :], xs[:, dt, :], xs[:, dt, :]), "pl")
        for dt in range(DT):
            n_sqrdy = pl_b + 2 if dt < 2 else ctx.n("pl")

            @block.tensor
            def _(t):
                t.wait_ge(ctx.sem("pl"), n_sqrdy)
                if dt == 0:
                    t.wait_ge(ctx.sem("dv"), ctx.marks.get(("a_psS_rd", pp), 0))
                t.matmul(pS[0:1, :], ones128[:], xs[:, dt, :], start=(dt == 0), stop=(dt == DT - 1))
                ins = t.matmul(pQ[0:1, :], ones128[:], sq[:, dt % 2, :], start=(dt == 0), stop=(dt == DT - 1))
                ctx.inc(ins, "pe")
            n_pe_dt = ctx.n("pe")
            if dt + 2 < DT:
                @block.gpsimd
                def _(g):
                    g.wait_ge(ctx.sem("pe"), n_pe_dt)
                    ctx.inc(g.tensor_mul(sq[:, dt % 2, :], xs[:, dt + 2, :], xs[:, dt + 2, :]), "pl")
        n_stmm = ctx.n("pe")
        ctx.marks["a_postmm"] = n_stmm

        @block.vector
        def _(v):
            v.wait_ge(ctx.sem("pe"), n_stmm)
            ctx.inc(v.tensor_scalar(mean_sb2[:, pp, :], pS[0:1, :], 1.0 / 1024.0, None, ALU.mult), "dv")
            v.wait_ge(ctx.sem("dv"), ctx.n("dv"))
            ctx.inc(v.tensor_scalar(var_sb2[:, pp, :], pQ[0:1, :], 1.0 / 1024.0, 1e-5, ALU.mult, ALU.add), "dv")
            v.wait_ge(ctx.sem("dv"), ctx.n("dv"))
            ctx.inc(v.tensor_mul(tmprow2[:, pp, :], mean_sb2[:, pp, :], mean_sb2[:, pp, :]), "dv")
            v.wait_ge(ctx.sem("dv"), ctx.n("dv"))
            ctx.inc(v.tensor_sub(var_sb2[:, pp, :], var_sb2[:, pp, :], tmprow2[:, pp, :]), "dv")
        n_var = ctx.n("dv")

        @block.scalar
        def _(sc):
            sc.wait_ge(ctx.sem("dv"), n_var)
            ctx.inc(sc.activation(var_sb2[:, pp, :], var_sb2[:, pp, :], AF.Ln), "ac")
            sc.wait_ge(ctx.sem("ac"), ctx.n("ac"))
            ctx.inc(sc.activation(var_sb2[:, pp, :], var_sb2[:, pp, :], AF.Exp, scale=-0.5), "ac")
        n_inv = ctx.n("ac")

        @block.tensor
        def _(t):
            t.wait_ge(ctx.sem("ac"), n_inv)
            t.matmul(pS[:], ones1[:], mean_sb2[:, pp, :], start=True, stop=True)
            ins = t.matmul(pQ[:], ones1[:], var_sb2[:, pp, :], start=True, stop=True)
            ctx.inc(ins, "pe")
        n_bc = ctx.n("pe")

        @block.vector
        def _(v):
            v.wait_ge(ctx.sem("pe"), n_bc)
            ctx.inc(v.tensor_copy(mb_sb2[:, pp, :], pS[:]), "dv")
            ctx.inc(v.tensor_copy(invb_sb2[:, pp, :], pQ[:]), "dv")
        n_bcc = ctx.n("dv")
        ctx.marks[("a_psS_rd", pp)] = n_bcc

        dv_base = ctx.n("dv")

        @block.gpsimd
        def _(g):
            g.wait_ge(ctx.sem("dv"), n_bcc)
            for dt in range(DT):
                if dt >= 2:
                    g.wait_ge(ctx.sem("dv"), dv_base + (dt - 2) + 1)
                ctx.inc(g.tensor_sub(tmp_po[:, dt % 2, :], xs[:, dt, :], mb_sb2[:, pp, :]), "pl")
                ctx.marks[("a_posub", dt)] = ctx.n("pl")

        @block.vector
        def _(v):
            for dt in range(DT):
                v.wait_ge(ctx.sem("pl"), ctx.marks[("a_posub", dt)])
                ctx.inc(v.tensor_mul(xs[:, dt, :], tmp_po[:, dt % 2, :], invb_sb2[:, pp, :]), "dv")
        n_post = ctx.n("dv")

        @block.sync
        def _(s):
            s.wait_ge(ctx.sem("dv"), n_post)
            s.wait_ge(ctx.sem("st2"), ctx.n("st2"))
            ctx.inc(s.dma_start(out=xout_r[:, :, s0:s0 + CH], in_=xs[:]), "st2", 16)
        ctx.marks[("a_st2", c)] = ctx.n("st2")

    emit_lnphase(0)
    for c in range(NCH):
        n_xp = emit_xproj(c)
        if c + 1 < NCH:
            emit_lnphase(c + 1)
        n_g = emit_uphase(c, n_xp)
        emit_outpost(c, n_g)

    return ctx.n("st2")


def emit_stats(ctx, block, src_sb, sq, psSum, psSq, ones128, n_ld_wait, DT, extra_pe_wait):
    nc = ctx.nc
    for dt in range(DT):
        @block.scalar
        def _(sc):
            if dt == 0:
                if n_ld_wait is not None:
                    sc.wait_ge(ctx.sem("ld"), n_ld_wait)
                if extra_pe_wait is not None:
                    sc.wait_ge(ctx.sem(extra_pe_wait[0]), extra_pe_wait[1])
            if dt >= 2:
                ctx.wait(sc, "pe")
            ctx.inc(sc.square(sq[:, dt % 2, :], src_sb[:, dt, :]), "ac")
        n_sq = ctx.n("ac")

        @block.tensor
        def _(t):
            t.wait_ge(ctx.sem("ac"), n_sq)
            if dt == 0:
                ctx.wait(t, "dv")
            t.matmul(psSum[0:1, :], ones128[:], src_sb[:, dt, :], start=(dt == 0), stop=(dt == DT - 1))
            ins = t.matmul(psSq[0:1, :], ones128[:], sq[:, dt % 2, :], start=(dt == 0), stop=(dt == DT - 1))
            ctx.inc(ins, "pe")


def emit_stats_final(ctx, block, psSum, psSq, mean_sb, var_sb, tmprow, denom, eps):
    n_pe = ctx.n("pe")

    @block.vector
    def _(v):
        v.wait_ge(ctx.sem("pe"), n_pe)
        ctx.inc(v.tensor_scalar(mean_sb[:], psSum[0:1, :], 1.0 / denom, None, ALU.mult), "dv")
        v.wait_ge(ctx.sem("dv"), ctx.n("dv"))
        ctx.inc(v.tensor_scalar(var_sb[:], psSq[0:1, :], 1.0 / denom, eps, ALU.mult, ALU.add), "dv")
        v.wait_ge(ctx.sem("dv"), ctx.n("dv"))
        ctx.inc(v.tensor_mul(tmprow[:], mean_sb[:], mean_sb[:]), "dv")
        v.wait_ge(ctx.sem("dv"), ctx.n("dv"))
        ctx.inc(v.tensor_sub(var_sb[:], var_sb[:], tmprow[:]), "dv")
    n_var = ctx.n("dv")

    @block.scalar
    def _(sc):
        sc.wait_ge(ctx.sem("dv"), n_var)
        ctx.inc(sc.activation(var_sb[:], var_sb[:], AF.Ln), "ac")
        sc.wait_ge(ctx.sem("ac"), ctx.n("ac"))
        ctx.inc(sc.activation(tmprow[:], var_sb[:], AF.Exp, scale=-0.5), "ac")
    n_std = ctx.n("ac")


def emit_bcast2(ctx, block, mean_sb, inv_sb, mb_sb, invb_sb, psB1, psB2, ones1):
    n_inv = ctx.n("dv")
    n_inv_ac = ctx.n("ac")

    @block.tensor
    def _(t):
        t.wait_ge(ctx.sem("dv"), n_inv)
        t.wait_ge(ctx.sem("ac"), n_inv_ac)
        t.matmul(psB1[:], ones1[:], mean_sb[:], start=True, stop=True)
        ins = t.matmul(psB2[:], ones1[:], inv_sb[:], start=True, stop=True)
        ctx.inc(ins, "pe")
    n_bc = ctx.n("pe")

    @block.vector
    def _(v):
        v.wait_ge(ctx.sem("pe"), n_bc)
        ctx.inc(v.tensor_copy(mb_sb[:], psB1[:]), "dv")
        ctx.inc(v.tensor_copy(invb_sb[:], psB2[:]), "dv")



def blockA_host_inputs(p):
    def fm(vec, ntiles):
        return np.ascontiguousarray(vec.reshape(ntiles, 128).T)

    def fm2(mat, ntiles, k):
        return np.ascontiguousarray(mat.reshape(ntiles, 128, k).transpose(1, 0, 2))

    out = {
        "a_w_inT": np.ascontiguousarray(p["W_in"].T),
        "a_w_cT": np.ascontiguousarray(p["W_c"].T),
        "a_w_dT": np.ascontiguousarray(p["W_d"].T),
        "a_w_outT": np.ascontiguousarray(p["W_out"].T),
        "a_lng_fm": fm(p["ln_g"], 8),
        "a_lnb_fm": fm(p["ln_b"], 8),
        "a_binxp_fm": fm(p["b_in"][:2048], 16),
        "a_bing_fm": fm(p["b_in"][2048:], 16),
        "a_convw_fm": fm2(p["conv_w"][:, 0, :], 16, 4),
        "a_convb_fm": fm(p["conv_b"], 16),
        "a_A_fm": fm2(p["A"], 16, 16),
        "a_Bm_fm": fm2(p["Bm"], 16, 16),
        "a_ld_fm": fm(p["log_delta"], 16),
        "a_bc_fm": fm(p["b_c"], 16),
        "a_bd_fm": fm(p["b_d"], 16),
        "a_bout_fm": fm(p["b_out"], 8),
    }
    return out


def declare_blockA_drams(nc, T):
    F = mybir.dt.float32
    shapes = {
        "a_w_inT": [1024, 4096], "a_w_cT": [2048, 2048], "a_w_dT": [2048, 2048],
        "a_w_outT": [2048, 1024],
        "a_lng_fm": [128, 8], "a_lnb_fm": [128, 8],
        "a_binxp_fm": [128, 16], "a_bing_fm": [128, 16],
        "a_convw_fm": [128, 16, 4], "a_convb_fm": [128, 16],
        "a_A_fm": [128, 16, 16], "a_Bm_fm": [128, 16, 16],
        "a_ld_fm": [128, 16], "a_bc_fm": [128, 16], "a_bd_fm": [128, 16],
        "a_bout_fm": [128, 8],
    }
    return {k: nc.dram_tensor(k, v, F, kind="ExternalInput").ap() for k, v in shapes.items()}




def emit_barrier(ctx, block, dep_sem, dep_val):
    s = ctx.sem(dep_sem)

    @block.sync
    def _(e):
        e.wait_ge(s, dep_val)

    @block.gpsimd
    def _(e):
        e.wait_ge(s, dep_val)

    @block.tensor
    def _(e):
        e.wait_ge(s, dep_val)

    @block.vector
    def _(e):
        e.wait_ge(s, dep_val)

    @block.scalar
    def _(e):
        e.wait_ge(s, dep_val)




def emit_postln_pipelined(ctx, block, T, xin_r, xout_r, acc_r, n_acc_done, pref,
                          xb, tb, sq2, tmp2, mb2, ivb2, mrow2, vrow2, trow2,
                          psS, ones128, ones1):
    NCH = T // CH
    DT = 8

    for c in range(NCH):
        s0 = c * CH
        pp = c % 2
        psSum, psSq = psS[2 * pp], psS[2 * pp + 1]

        @block.sync
        def _(s):
            s.wait_ge(ctx.sem("ld"), ctx.n("ld"))
            s.wait_ge(ctx.sem("st"), n_acc_done)
            if c >= 2:
                s.wait_ge(ctx.sem("st2"), ctx.marks[(pref, "st2", c - 2)])
            ctx.inc(s.dma_start(out=xb[pp][:], in_=xin_r[:, :, s0:s0 + CH]), "ld", 16)
        n_x1 = ctx.n("ld")

        @block.gpsimd
        def _(g):
            g.wait_ge(ctx.sem("wq"), ctx.n("wq"))
            g.wait_ge(ctx.sem("st"), n_acc_done)
            if c >= 2:
                g.wait_ge(ctx.sem("dv"), ctx.marks[(pref, "resid", c - 2)])
            ctx.inc(g.dma_start(out=tb[pp][:], in_=acc_r[:, :, s0:s0 + CH]), "wq", 16)
        n_accld = ctx.n("wq")

        @block.vector
        def _(v):
            v.wait_ge(ctx.sem("ld"), n_x1)
            v.wait_ge(ctx.sem("wq"), n_accld)
            if c >= 2:
                v.wait_ge(ctx.sem("pe"), ctx.marks[(pref, "stmm", c - 2)])
            for dt in range(DT):
                ctx.inc(v.tensor_add(xb[pp][:, dt, :], xb[pp][:, dt, :], tb[pp][:, dt, :]), "dv")
        n_resid = ctx.n("dv")
        ctx.marks[(pref, "resid", c)] = n_resid

        pl_b = ctx.n("pl")

        @block.gpsimd
        def _(g):
            g.wait_ge(ctx.sem("dv"), n_resid)
            for dt in range(2):
                ctx.inc(g.tensor_mul(sq2[pp][:, dt, :], xb[pp][:, dt, :], xb[pp][:, dt, :]), "pl")
        for dt in range(DT):
            n_sqrdy = pl_b + 2 if dt < 2 else ctx.n("pl")

            @block.tensor
            def _(t):
                t.wait_ge(ctx.sem("pl"), n_sqrdy)
                if dt == 0:
                    t.wait_ge(ctx.sem("dv"), ctx.marks.get((pref, "statrd", c - 2), 0) if c >= 2 else 0)
                t.matmul(psSum[0:1, :], ones128[:], xb[pp][:, dt, :], start=(dt == 0), stop=(dt == DT - 1))
                ins = t.matmul(psSq[0:1, :], ones128[:], sq2[pp][:, dt % 2, :], start=(dt == 0), stop=(dt == DT - 1))
                ctx.inc(ins, "pe")
            n_pe_dt = ctx.n("pe")
            if dt + 2 < DT:
                @block.gpsimd
                def _(g):
                    g.wait_ge(ctx.sem("pe"), n_pe_dt)
                    ctx.inc(g.tensor_mul(sq2[pp][:, dt % 2, :], xb[pp][:, dt + 2, :], xb[pp][:, dt + 2, :]), "pl")
        n_stmm = ctx.n("pe")
        ctx.marks[(pref, "stmm", c)] = n_stmm

        @block.vector
        def _(v):
            v.wait_ge(ctx.sem("pe"), n_stmm)
            ctx.inc(v.tensor_scalar(mrow2[pp][:], psSum[0:1, :], 1.0 / 1024.0, None, ALU.mult), "dv")
            v.wait_ge(ctx.sem("dv"), ctx.n("dv"))
            ctx.inc(v.tensor_scalar(vrow2[pp][:], psSq[0:1, :], 1.0 / 1024.0, 1e-5, ALU.mult, ALU.add), "dv")
            v.wait_ge(ctx.sem("dv"), ctx.n("dv"))
            ctx.inc(v.tensor_mul(trow2[pp][:], mrow2[pp][:], mrow2[pp][:]), "dv")
            v.wait_ge(ctx.sem("dv"), ctx.n("dv"))
            ctx.inc(v.tensor_sub(vrow2[pp][:], vrow2[pp][:], trow2[pp][:]), "dv")
        n_var = ctx.n("dv")
        ctx.marks[(pref, "statrd", c)] = n_var

        @block.scalar
        def _(sc):
            sc.wait_ge(ctx.sem("dv"), n_var)
            ctx.inc(sc.activation(vrow2[pp][:], vrow2[pp][:], AF.Ln), "ac")
            sc.wait_ge(ctx.sem("ac"), ctx.n("ac"))
            ctx.inc(sc.activation(vrow2[pp][:], vrow2[pp][:], AF.Exp, scale=-0.5), "ac")
        n_inv = ctx.n("ac")

        @block.tensor
        def _(t):
            t.wait_ge(ctx.sem("ac"), n_inv)
            if c >= 2:
                t.wait_ge(ctx.sem("dv"), ctx.marks[(pref, "bcrd", c - 2)])
            t.matmul(psSum[:], ones1[:], mrow2[pp][:], start=True, stop=True)
            ins = t.matmul(psSq[:], ones1[:], vrow2[pp][:], start=True, stop=True)
            ctx.inc(ins, "pe")
        n_bc = ctx.n("pe")

        @block.vector
        def _(v):
            v.wait_ge(ctx.sem("pe"), n_bc)
            ctx.inc(v.tensor_copy(mb2[pp][:], psSum[:]), "dv")
            ctx.inc(v.tensor_copy(ivb2[pp][:], psSq[:]), "dv")
        n_bcc = ctx.n("dv")
        ctx.marks[(pref, "bcrd", c)] = n_bcc

        dv_b = ctx.n("dv")

        @block.gpsimd
        def _(g):
            g.wait_ge(ctx.sem("dv"), n_bcc)
            for dt in range(DT):
                if dt >= 2:
                    g.wait_ge(ctx.sem("dv"), dv_b + (dt - 2) + 1)
                ctx.inc(g.tensor_sub(tmp2[pp][:, dt % 2, :], xb[pp][:, dt, :], mb2[pp][:]), "pl")
                ctx.marks[(pref, "plsub", dt)] = ctx.n("pl")

        @block.vector
        def _(v):
            for dt in range(DT):
                v.wait_ge(ctx.sem("pl"), ctx.marks[(pref, "plsub", dt)])
                ctx.inc(v.tensor_mul(xb[pp][:, dt, :], tmp2[pp][:, dt % 2, :], ivb2[pp][:]), "dv")
        n_post = ctx.n("dv")

        @block.sync
        def _(s):
            s.wait_ge(ctx.sem("dv"), n_post)
            s.wait_ge(ctx.sem("st2"), ctx.n("st2"))
            ctx.inc(s.dma_start(out=xout_r[:, :, s0:s0 + CH], in_=xb[pp][:]), "st2", 16)
        ctx.marks[(pref, "st2", c)] = ctx.n("st2")

    return ctx.n("st2")



def emit_blockB(ctx, block, T, xin, xout, W, acc_hbm):
    nc = ctx.nc
    NCH = T // CH
    DT = 8
    FT = 16
    NE = 8

    xin_r = xin.rearrange("(t p) s -> p t s", p=128)
    xout_r = xout.rearrange("(t p) s -> p t s", p=128)
    acc_r = acc_hbm.rearrange("(t p) s -> p t s", p=128)
    gw_r = W["b_gwT"].rearrange("(t p) e -> p t e", p=128)
    w1_r = W["b_w1T"].rearrange("E (t p) f -> E p t f", p=128)
    w3_r = W["b_w3T"].rearrange("E (t p) f -> E p t f", p=128)
    w2_r = W["b_w2T"].rearrange("E (t p) f -> E p t f", p=128)

    xf_bf = ctx.sbuf("b_xf", [128, DT, T], BF16)
    gwt = ctx.sbuf("b_gwtsb", [128, DT, 8], F32)
    ident = ctx.sbuf("b_identsb", [128, 128], F32)
    sel = ctx.sbuf("b_selsb", [8, NE * 128], BF16)
    e_sb = ctx.sbuf("b_e", [128, 8], F32)
    nm1 = ctx.sbuf("b_nm1", [128, 1], F32)
    ssum = ctx.sbuf("b_ssum", [128, 1], F32)
    rsum = ctx.sbuf("b_rsum", [128, 1], F32)
    probs = ctx.sbuf("b_probs", [128, 8], F32)
    m1p = ctx.sbuf("b_m1p", [128, 1], F32)
    oh0 = ctx.sbuf("b_oh0", [128, 8], F32)
    tnot = ctx.sbuf("b_tnot", [128, 8], F32)
    probs2 = ctx.sbuf("b_probs2", [128, 8], F32)
    m2p = ctx.sbuf("b_m2p", [128, 1], F32)
    oh1 = ctx.sbuf("b_oh1", [128, 8], F32)
    den = ctx.sbuf("b_den", [128, 1], F32)
    rden = ctx.sbuf("b_rden", [128, 1], F32)
    w0c = ctx.sbuf("b_w0c", [128, 1], F32)
    w1c = ctx.sbuf("b_w1c", [128, 1], F32)
    comb = ctx.sbuf("b_comb", [128, 8], F32)
    combT = ctx.sbuf("b_combT", [8, T], BF16)
    w1s = ctx.sbuf("b_w1s", [128, DT, 2048], BF16)
    w3s = ctx.sbuf("b_w3s", [128, DT, 2048], BF16)
    w2s = ctx.sbuf("b_w2s", [128, FT, 1024], BF16)
    s1_sb = ctx.sbuf("b_s1", [128, 2, CH], BF16)
    h_bf = ctx.sbuf("b_h", [128, FT, CH], BF16)
    x1_sb = h_bf[:].bitcast(F32).rearrange("p a b -> p (a b)").rearrange("p (t s) -> p t s", t=DT)
    cb_sb = ctx.sbuf("b_cb", [128, CH], F32)
    t_sb = ctx.sbuf("b_t", [128, DT, CH], F32)
    sq = w3s[:, 0, :].bitcast(F32).rearrange("p (a b) -> p a b", b=CH)
    tmp1 = w3s[:, 1, :].bitcast(F32).rearrange("p (a b) -> p a b", b=CH)
    mb_sb = w1s[:, 0, 0:1024].bitcast(F32)
    invb_sb = w1s[:, 1, 0:1024].bitcast(F32)
    mean_sb = w2s[0:1, 0, :].bitcast(F32)[:, 0:CH]
    var_sb = w2s[0:1, 1, :].bitcast(F32)[:, 0:CH]
    tmprow = w2s[0:1, 2, :].bitcast(F32)[:, 0:CH]
    ones128 = ctx.sbuf("b_ones128", [128, 1], F32)
    ones1 = ctx.sbuf("b_ones1", [1, 128], F32)

    psA = [ctx.psum("b_psA0", [128, CH]), ctx.psum("b_psA1", [128, CH])]
    psC = [ctx.psum("b_psC0", [128, CH]), ctx.psum("b_psC1", [128, CH])]
    psO = [ctx.psum("b_psO0", [128, CH]), ctx.psum("b_psO1", [128, CH])]
    psB = ctx.psum("b_psB", [128, CH])
    psG = ctx.psum("b_psG", [128, 512])
    psT = psG

    @block.sync
    def _(s):
        s.wait_ge(ctx.sem("ld"), ctx.n("ld"))
        ctx.inc(s.dma_start(out=gwt[:], in_=gw_r[:]), "ld", 16)
        ctx.inc(s.dma_start(out=ident[:], in_=W["ident"]), "ld", 16)
    n_param_ld = ctx.n("ld")

    @block.gpsimd
    def _(g):
        g.wait_ge(ctx.sem("wq"), ctx.n("wq"))
        ctx.inc(g.dma_start(out=sel[:], in_=W["b_sel"]), "wq", 16)
        ctx.inc(g.memset(ones128[:], 1.0), "pl")
        ctx.inc(g.memset(ones1[:], 1.0), "pl")
    n_init_pl = ctx.n("pl")
    n_init_wq = ctx.n("wq")

    for c in range(NCH):
        s0 = c * CH

        @block.sync
        def _(s):
            s.wait_ge(ctx.sem("ld"), ctx.n("ld"))
            if c >= 1:
                ctx.wait(s, "pe")
            ctx.inc(s.dma_start(out=x1_sb[:], in_=xin_r[:, :, s0:s0 + CH]), "ld", 16)
        n_xld = ctx.n("ld")

        @block.gpsimd
        def _(g):
            g.wait_ge(ctx.sem("wq"), ctx.n("wq"))
            ctx.inc(g.dma_start(out=xf_bf[:, :, s0:s0 + CH], in_=xin_r[:, :, s0:s0 + CH]), "wq", 16)
        n_xfld = ctx.n("wq")

        for st in range(4):
            @block.tensor
            def _(t):
                t.wait_ge(ctx.sem("ld"), n_xld)
                if c == 0 and st == 0:
                    t.wait_ge(ctx.sem("pl"), n_init_pl)
                    t.wait_ge(ctx.sem("ld"), n_param_ld)
                if c >= 1 or st >= 1:
                    ctx.wait(t, "dv")
                for dt in range(DT):
                    ins = t.matmul(psG[:, st * 8:(st + 1) * 8], x1_sb[:, dt, st * 128:(st + 1) * 128], gwt[:, dt, :], start=(dt == 0), stop=(dt == DT - 1))
                ctx.inc(ins, "pe")
            n_gmm = ctx.n("pe")

            @block.vector
            def _(v):
                v.wait_ge(ctx.sem("pe"), n_gmm)
                ctx.inc(v.tensor_reduce(nm1[:].rearrange("p (a b) -> p a b", b=1), psG[:, st * 8:(st + 1) * 8].rearrange("p (a b) -> p a b", b=8), AX.X, ALU.max, negate=True), "dv")
            n_nm1 = ctx.n("dv")

            @block.scalar
            def _(sc):
                sc.wait_ge(ctx.sem("dv"), n_nm1)
                ctx.inc(sc.activation(e_sb[:], psG[:, st * 8:(st + 1) * 8], AF.Exp, bias=nm1[:, 0:1]), "ac")
            n_exp = ctx.n("ac")

            @block.vector
            def _(v):
                v.wait_ge(ctx.sem("ac"), n_exp)
                ctx.inc(v.tensor_reduce(ssum[:].rearrange("p (a b) -> p a b", b=1), e_sb[:].rearrange("p (a b) -> p a b", b=8), AX.X, ALU.add), "dv")
                v.wait_ge(ctx.sem("dv"), ctx.n("dv"))
                ctx.inc(v.reciprocal(rsum[:], ssum[:]), "dv")
                v.wait_ge(ctx.sem("dv"), ctx.n("dv"))
                ctx.inc(v.tensor_scalar(probs[:], e_sb[:], rsum[:, 0:1], None, ALU.mult), "dv")
                v.wait_ge(ctx.sem("dv"), ctx.n("dv"))
                ctx.inc(v.tensor_reduce(m1p[:].rearrange("p (a b) -> p a b", b=1), probs[:].rearrange("p (a b) -> p a b", b=8), AX.X, ALU.max), "dv")
                v.wait_ge(ctx.sem("dv"), ctx.n("dv"))
                ctx.inc(v.tensor_scalar(oh0[:], probs[:], m1p[:, 0:1], None, ALU.is_ge), "dv")
                v.wait_ge(ctx.sem("dv"), ctx.n("dv"))
                ctx.inc(v.tensor_scalar(tnot[:], oh0[:], -1.0, 1.0, ALU.mult, ALU.add), "dv")
                v.wait_ge(ctx.sem("dv"), ctx.n("dv"))
                ctx.inc(v.tensor_mul(probs2[:], probs[:], tnot[:]), "dv")
                v.wait_ge(ctx.sem("dv"), ctx.n("dv"))
                ctx.inc(v.tensor_reduce(m2p[:].rearrange("p (a b) -> p a b", b=1), probs2[:].rearrange("p (a b) -> p a b", b=8), AX.X, ALU.max), "dv")
                v.wait_ge(ctx.sem("dv"), ctx.n("dv"))
                ctx.inc(v.tensor_scalar(oh1[:], probs2[:], m2p[:, 0:1], None, ALU.is_ge), "dv")
                v.wait_ge(ctx.sem("dv"), ctx.n("dv"))
                ctx.inc(v.tensor_add(den[:], m1p[:], m2p[:]), "dv")
                v.wait_ge(ctx.sem("dv"), ctx.n("dv"))
                ctx.inc(v.reciprocal(rden[:], den[:]), "dv")
                v.wait_ge(ctx.sem("dv"), ctx.n("dv"))
                ctx.inc(v.tensor_mul(w0c[:], m1p[:], rden[:]), "dv")
                ctx.inc(v.tensor_mul(w1c[:], m2p[:], rden[:]), "dv")
                v.wait_ge(ctx.sem("dv"), ctx.n("dv"))
                ctx.inc(v.tensor_scalar(comb[:], oh0[:], w0c[:, 0:1], None, ALU.mult), "dv")
                v.wait_ge(ctx.sem("dv"), ctx.n("dv"))
                ctx.inc(v.scalar_tensor_tensor(comb[:], oh1[:], w1c[:, 0:1], comb[:], ALU.mult, ALU.add), "dv")
            n_comb = ctx.n("dv")

            @block.tensor
            def _(t):
                t.wait_ge(ctx.sem("dv"), n_comb)
                ctx.inc(t.transpose(psT[0:8, 128:256], comb[:], ident[:]), "pe")
            n_tp = ctx.n("pe")

            @block.vector
            def _(v):
                v.wait_ge(ctx.sem("pe"), n_tp)
                ctx.inc(v.tensor_copy(combT[:, s0 + st * 128:s0 + (st + 1) * 128], psT[0:8, 128:256]), "dv")
    n_gate_dv = ctx.n("dv")
    n_gate_wq = ctx.n("wq")

    w13_free = 0
    w2_free = 0
    evacA_b = [("pe", 0), ("pe", 0)]
    evacC_b = [("pe", 0), ("pe", 0)]
    w2rd = 0
    tsb_rd = 0

    wqb1_0 = ctx.n("wqb1")
    wqb3_0 = ctx.n("wqb3")
    wqb2_0 = ctx.n("wqb2")

    @block.gpsimd
    def _(g):
        ctx.inc(g.dma_start(out=w1s[:], in_=w1_r[0]), "wqb1", 16)
        ctx.inc(g.dma_start(out=w3s[:], in_=w3_r[0]), "wqb3", 16)
        ctx.inc(g.dma_start(out=w2s[:], in_=w2_r[0]), "wqb2", 16)

    for e in range(NE):
        n_w13 = wqb1_0 + 16 * (e + 1)
        n_w2l = wqb2_0 + 16 * (e + 1)

        for c in range(NCH):
            s0 = c * CH
            ac_base = ctx.n("ac")
            dv_base = ctx.n("dv")
            for f in range(FT):
                par = f % 2

                @block.tensor
                def _(t):
                    if f == 0 and c == 0:
                        t.wait_ge(ctx.sem("wqb1"), n_w13)
                        t.wait_ge(ctx.sem("wqb3"), n_w13)
                        if e == 0:
                            t.wait_ge(ctx.sem("wq"), n_gate_wq)
                    if f >= 2:
                        t.wait_ge(ctx.sem("ac"), ac_base + (f - 2) + 1)
                        t.wait_ge(ctx.sem("dv"), dv_base + (f - 2) + 1)
                    elif c > 0 or e > 0:
                        sem_name, val = evacA_b[par]
                        t.wait_ge(ctx.sem(sem_name), val)
                        sem_name, val = evacC_b[par]
                        t.wait_ge(ctx.sem(sem_name), val)
                    for dt in range(DT):
                        t.matmul(psA[par][:], w1s[:, dt, f * 128:(f + 1) * 128], xf_bf[:, dt, s0:s0 + CH], start=(dt == 0), stop=(dt == DT - 1))
                    for dt in range(DT):
                        ins = t.matmul(psC[par][:], w3s[:, dt, f * 128:(f + 1) * 128], xf_bf[:, dt, s0:s0 + CH], start=(dt == 0), stop=(dt == DT - 1))
                    ctx.inc(ins, "pe")
                ctx.marks[("b_hmm", f)] = ctx.n("pe")
            n_h_pe = ctx.n("pe")

            @block.scalar
            def _(sc):
                for f in range(FT):
                    sc.wait_ge(ctx.sem("pe"), ctx.marks[("b_hmm", f)])
                    if c > 0 or e > 0:
                        sc.wait_ge(ctx.sem("pe"), w2rd)
                    ctx.inc(sc.activation(s1_sb[:, f % 2, :], psA[f % 2][:], AF.Silu), "ac")
                    ctx.marks[("b_silu", f)] = ctx.n("ac")

            @block.vector
            def _(v):
                for f in range(FT):
                    v.wait_ge(ctx.sem("ac"), ctx.marks[("b_silu", f)])
                    ctx.inc(v.tensor_mul(h_bf[:, f, :], s1_sb[:, f % 2, :], psC[f % 2][:]), "dv")
                    if f >= FT - 2:
                        which = f % 2
                        evacC_b[which] = ("dv", ctx.n("dv"))
                        evacA_b[which] = ("ac", ctx.marks[("b_silu", f)])
            n_h = ctx.n("dv")

            if c == NCH - 1 and e < NE - 1:
                @block.gpsimd
                def _(g):
                    g.wait_ge(ctx.sem("pe"), n_h_pe)
                    ctx.inc(g.dma_start(out=w1s[:], in_=w1_r[e + 1]), "wqb1", 16)
                    ctx.inc(g.dma_start(out=w3s[:], in_=w3_r[e + 1]), "wqb3", 16)

            @block.tensor
            def _(t):
                t.wait_ge(ctx.sem("dv"), n_gate_dv)
                t.wait_ge(ctx.sem("dv"), ctx.marks.get("b_cb_rd", 0))
                ctx.inc(t.matmul(psB[:], sel[:, e * 128:(e + 1) * 128], combT[:, s0:s0 + CH], start=True, stop=True), "pe")
            n_cb = ctx.n("pe")

            @block.vector
            def _(v):
                v.wait_ge(ctx.sem("pe"), n_cb)
                ctx.inc(v.tensor_copy(cb_sb[:], psB[:]), "dv")
            n_cbc = ctx.n("dv")
            ctx.marks["b_cb_rd"] = None

            dv_scale_base = None
            for dt in range(DT):
                par = dt % 2

                @block.tensor
                def _(t):
                    if dt == 0 and c == 0:
                        t.wait_ge(ctx.sem("wqb2"), n_w2l)
                    if dt == 0:
                        t.wait_ge(ctx.sem("dv"), n_h)
                    if dt >= 2:
                        t.wait_ge(ctx.sem("dv"), ctx.marks[("b_scale", dt - 2)])
                    elif c > 0 or e > 0:
                        t.wait_ge(ctx.sem("dv"), ctx.marks[("b_scale_prev", dt)])
                    for f in range(FT):
                        ins = t.matmul(psO[par][:], w2s[:, f, dt * 128:(dt + 1) * 128], h_bf[:, f, :], start=(f == 0), stop=(f == FT - 1))
                    ctx.inc(ins, "pe")
                n_mm2 = ctx.n("pe")

                @block.vector
                def _(v):
                    v.wait_ge(ctx.sem("pe"), n_mm2)
                    v.wait_ge(ctx.sem("dv"), n_cbc)
                    if e > 0 or c > 0:
                        v.wait_ge(ctx.sem("st"), tsb_rd)
                    ctx.inc(v.tensor_mul(t_sb[:, dt, :], psO[par][:], cb_sb[:]), "dv")
                ctx.marks[("b_scale", dt)] = ctx.n("dv")
            for dt in range(DT):
                ctx.marks[("b_scale_prev", dt)] = ctx.marks[("b_scale", dt)]
            w2rd = ctx.n("pe")
            ctx.marks["b_cb_rd"] = ctx.n("dv")
            n_scale = ctx.n("dv")

            @block.gpsimd
            def _(g):
                g.wait_ge(ctx.sem("dv"), n_scale)
                g.wait_ge(ctx.sem("st"), ctx.n("st"))
                op = ALU.bypass if e == 0 else ALU.add
                ctx.inc(g.dma_start(out=acc_r[:, :, s0:s0 + CH], in_=t_sb[:], accum_op=op), "st", 16)
            tsb_rd = ctx.n("st")

        if e < NE - 1:
            @block.gpsimd
            def _(g):
                g.wait_ge(ctx.sem("pe"), w2rd)
                ctx.inc(g.dma_start(out=w2s[:], in_=w2_r[e + 1]), "wqb2", 16)
    n_acc_done = ctx.n("st")
    xb = [x1_sb, w1s[:, 0:4, :].bitcast(F32).rearrange("p a (x b) -> p (a x) b", b=CH)]
    tb = [t_sb, w1s[:, 4:8, :].bitcast(F32).rearrange("p a (x b) -> p (a x) b", b=CH)]
    sq2 = [w3s[:, 0, :].bitcast(F32).rearrange("p (a b) -> p a b", b=CH),
           w3s[:, 1, :].bitcast(F32).rearrange("p (a b) -> p a b", b=CH)]
    tmp2 = [w3s[:, 2, :].bitcast(F32).rearrange("p (a b) -> p a b", b=CH),
            w3s[:, 3, :].bitcast(F32).rearrange("p (a b) -> p a b", b=CH)]
    mb2 = [w3s[:, 4, 0:1024].bitcast(F32), w3s[:, 4, 1024:2048].bitcast(F32)]
    ivb2 = [w3s[:, 5, 0:1024].bitcast(F32), w3s[:, 5, 1024:2048].bitcast(F32)]
    mrow2 = [w3s[0:1, 6, 0:1024].bitcast(F32), w3s[0:1, 6, 1024:2048].bitcast(F32)]
    vrow2 = [w3s[0:1, 7, 0:1024].bitcast(F32), w3s[0:1, 7, 1024:2048].bitcast(F32)]
    trow2 = [w2s[0:1, 0, :].bitcast(F32), w2s[0:1, 1, :].bitcast(F32)]
    emit_postln_pipelined(ctx, block, T, xin_r, xout_r, acc_r, n_acc_done, "Bp",
                          xb, tb, sq2, tmp2, mb2, ivb2, mrow2, vrow2, trow2,
                          [psA[0], psA[1], psO[0], psO[1]], ones128, ones1)
    return ctx.n("st2")


def blockB_host_inputs(p):
    out = {
        "b_gwT": np.ascontiguousarray(p["gate_w"].T),
        "b_w1T": np.ascontiguousarray(p["w1"].transpose(0, 2, 1)),
        "b_w3T": np.ascontiguousarray(p["w3"].transpose(0, 2, 1)),
        "b_w2T": np.ascontiguousarray(p["w2"].transpose(0, 2, 1)),
        "ident": np.eye(128, dtype=np.float32),
    }
    sel = np.zeros((8, 8 * 128), np.float32)
    for e in range(8):
        sel[e, e * 128:(e + 1) * 128] = 1.0
    out["b_sel"] = sel
    return out


def declare_blockB_drams(nc, T):
    F = mybir.dt.float32
    shapes = {
        "b_gwT": [1024, 8], "b_w1T": [8, 1024, 2048], "b_w3T": [8, 1024, 2048],
        "b_w2T": [8, 2048, 1024], "ident": [128, 128], "b_sel": [8, 8 * 128],
    }
    return {k: nc.dram_tensor(k, v, F, kind="ExternalInput").ap() for k, v in shapes.items()}



def emit_blockC(ctx, block, T, xin, xout, W, kvscr, ksscr):
    nc = ctx.nc
    NCH = T // CH
    DT = 8
    NH = 16

    xin_r = xin.rearrange("(t p) s -> p t s", p=128)
    xout_r = xout.rearrange("(t p) s -> p t s", p=128)
    wq_r = W["c_wqT"].rearrange("(t p) f -> p t f", p=128)
    wk_r = W["c_wkT"].rearrange("(t p) f -> p t f", p=128)
    wv_r = W["c_wvT"].rearrange("(t p) f -> p t f", p=128)

    x_sb = ctx.sbuf("c_x", [128, DT, CH], F32)
    xbf = ctx.sbuf("c_xbf", [128, DT, CH], BF16)
    wqs = ctx.sbuf("c_wqs", [128, DT, 1024], BF16)
    wks = ctx.sbuf("c_wks", [128, DT, 1024], BF16)
    wvs = ctx.sbuf("c_wvs", [128, DT, 1024], BF16)
    q_bf = ctx.sbuf("c_q", [128, DT, T], BF16)
    sq_sb = ctx.sbuf("c_sq", [128, 2, CH], F32)
    rb_sb = ctx.sbuf("c_rb", [128, 2, CH], F32)
    t_sb = ctx.sbuf("c_t", [128, 2, CH], F32)
    nrm2 = ctx.sbuf("c_nrm2", [2, CH], F32)
    nrm_all = ctx.sbuf("c_nrmall", [2, DT, CH], F32)
    k_bf = ctx.sbuf("c_k", [128, 2, 1024], BF16)
    v_bf = ctx.sbuf("c_v", [128, 2, 1024], BF16)
    sk_sb = ctx.sbuf("c_sk", [128, 1024], F32)
    ssk = ctx.sbuf("c_ssk", [128, 16], F32)
    rk = ctx.sbuf("c_rk", [128, 16], F32)
    kv_sb = ctx.sbuf("c_kv", [64, 1024], BF16)
    kvacc = ctx.sbuf("c_kvacc", [64, 1024], F32)
    kv2_sb = ctx.sbuf("c_kv2", [128, 8, 64], BF16)
    ks_row = ctx.sbuf("c_ksrow", [1, 1024], F32)
    ksfm = ctx.sbuf("c_ksfm", [128, 8], BF16)
    ks2 = ctx.sbuf("c_ks2", [128, 8, 2], BF16)
    rn_bf_all = ctx.sbuf("c_rnbfall", [2, DT, CH], BF16)
    qn_all = ctx.sbuf("c_qnall", [128, DT, CH], BF16)
    indA = ctx.sbuf("c_indA_sb", [128, 2], F32)
    ident_c = ctx.sbuf("c_ident_sb", [128, 128], F32)
    indT2 = ctx.sbuf("c_indT2_sb", [2, 128], F32)
    indT2b = ctx.sbuf("c_indT2b_sb", [2, 128], BF16)
    maskA = ctx.sbuf("c_maskA_sb", [128, 1], F32)
    maskB = ctx.sbuf("c_maskB_sb", [128, 1], F32)
    ones128 = ctx.sbuf("c_ones128", [128, 1], F32)
    ones128b = ctx.sbuf("c_ones128b", [128, 1], BF16)
    ones1 = ctx.sbuf("c_ones1", [1, 128], F32)
    sq = wks[:, 0:2, :].bitcast(F32).rearrange("p a (x b) -> p (a x) b", b=CH)[:, 0:2, :]
    tmp1 = wks[:, 2:4, :].bitcast(F32).rearrange("p a (x b) -> p (a x) b", b=CH)[:, 0:2, :]
    mb_sb = wks[:, 4, :].bitcast(F32)[:, 0:CH]
    invb_sb = wks[:, 5, :].bitcast(F32)[:, 0:CH]
    mean_sb = wvs[0:1, 0, :].bitcast(F32)[:, 0:CH]
    var_sb = wvs[0:1, 1, :].bitcast(F32)[:, 0:CH]
    tmprow = wvs[0:1, 2, :].bitcast(F32)[:, 0:CH]

    P = [ctx.psum("c_P0", [128, CH]), ctx.psum("c_P1", [128, CH]),
         ctx.psum("c_P2", [128, CH]), ctx.psum("c_P3", [128, CH])]
    psKV = ctx.psum("c_psKV", [64, 512])
    psKS = ctx.psum("c_psKS", [1, 1024])
    psN = ctx.psum("c_psN", [2, CH])

    @block.sync
    def _(s):
        s.wait_ge(ctx.sem("ld"), ctx.n("ld"))
        for name, dst in [("c_indA", indA), ("c_indT2", indT2), ("c_ident", ident_c)]:
            ctx.inc(s.dma_start(out=dst[:], in_=W[name]), "ld", 16)
    n_param_ld = ctx.n("ld")

    @block.gpsimd
    def _(g):
        g.wait_ge(ctx.sem("wq"), ctx.n("wq"))
        ctx.inc(g.dma_start(out=wqs[:], in_=wq_r[:]), "wq", 16)
        g.wait_ge(ctx.sem("wq"), ctx.n("wq"))
        ctx.inc(g.dma_start(out=wks[:], in_=wk_r[:]), "wq", 16)
        g.wait_ge(ctx.sem("wq"), ctx.n("wq"))
        ctx.inc(g.dma_start(out=wvs[:], in_=wv_r[:]), "wq", 16)
        g.wait_ge(ctx.sem("wq"), ctx.n("wq"))
        ctx.inc(g.dma_start(out=indT2b[:], in_=W["c_indT2"]), "wq", 16)
        g.wait_ge(ctx.sem("wq"), ctx.n("wq"))
        ctx.inc(g.dma_start(out=maskA[:], in_=W["c_maskA"]), "wq", 16)
        g.wait_ge(ctx.sem("wq"), ctx.n("wq"))
        ctx.inc(g.dma_start(out=maskB[:], in_=W["c_maskB"]), "wq", 16)
        ctx.inc(g.memset(ones128[:], 1.0), "pl")
        ctx.inc(g.memset(ones128b[:], 1.0), "pl")
        ctx.inc(g.memset(ones1[:], 1.0), "pl")
    n_init_pl = ctx.n("pl")
    n_wload = ctx.n("wq")

    for c in range(NCH):
        s0 = c * CH

        @block.gpsimd
        def _(g):
            g.wait_ge(ctx.sem("wq"), ctx.n("wq"))
            if c >= 1:
                ctx.wait(g, "pe")
            ctx.inc(g.dma_start(out=xbf[:], in_=xin_r[:, :, s0:s0 + CH]), "wq", 16)
        n_xbfld = ctx.n("wq")

        dv_base_q = ctx.n("dv")
        ac_base_q = ctx.n("ac")
        def emit_qgroup(qf):
            par3 = qf % 3

            @block.tensor
            def _(t):
                t.wait_ge(ctx.sem("wq"), n_xbfld)
                if c == 0 and qf == 0:
                    t.wait_ge(ctx.sem("wq"), n_wload)
                    t.wait_ge(ctx.sem("pl"), n_init_pl)
                    t.wait_ge(ctx.sem("ld"), n_param_ld)
                if qf >= 3:
                    t.wait_ge(ctx.sem("dv"), dv_base_q + 4 * (qf - 3) + 4)
                else:
                    sem_name, val = ctx.marks.get(("c_P", par3), ("pe", 0))
                    t.wait_ge(ctx.sem(sem_name), val)
                for dt in range(DT):
                    ins = t.matmul(P[par3][:], wqs[:, dt, qf * 128:(qf + 1) * 128], xbf[:, dt, :], start=(dt == 0), stop=(dt == DT - 1))
                ctx.inc(ins, "pe")
            ctx.marks[("c_qmm", qf)] = ctx.n("pe")


        def emit_qchain(qf):
            par3 = qf % 3

            @block.scalar
            def _(sc):
                sc.wait_ge(ctx.sem("pe"), ctx.marks[("c_qmm", qf)])
                if qf >= 2:
                    sc.wait_ge(ctx.sem("pe"), ctx.marks[("c_ind", qf - 2)])
                ctx.inc(sc.square(sq_sb[:, qf % 2, :], P[par3][:]), "ac")
            n_sq = ctx.n("ac")

            @block.tensor
            def _(t):
                t.wait_ge(ctx.sem("ac"), n_sq)
                if qf >= 1:
                    t.wait_ge(ctx.sem("dv"), dv_base_q + 4 * (qf - 1) + 1)
                ctx.inc(t.matmul(psN[0:2, :], indA[:], sq_sb[:, qf % 2, :], start=True, stop=True), "pe")
            n_ind = ctx.n("pe")
            ctx.marks[("c_ind", qf)] = n_ind

            @block.vector
            def _(v):
                v.wait_ge(ctx.sem("pe"), n_ind)
                ctx.inc(v.tensor_copy(nrm_all[:, qf, :], psN[0:2, :]), "dv")
                v.wait_ge(ctx.sem("dv"), ctx.n("dv"))
                ctx.inc(v.tensor_scalar_max(nrm_all[:, qf, :], nrm_all[:, qf, :], 1e-24), "dv")
            n_mx = ctx.n("dv")

            @block.scalar
            def _(sc):
                sc.wait_ge(ctx.sem("dv"), n_mx)
                ctx.inc(sc.activation(nrm_all[:, qf, :], nrm_all[:, qf, :], AF.Ln), "ac")
                sc.wait_ge(ctx.sem("ac"), ctx.n("ac"))
                ctx.inc(sc.activation(nrm_all[:, qf, :], nrm_all[:, qf, :], AF.Exp, scale=-0.5), "ac")
            n_rsq = ctx.n("ac")

            @block.tensor
            def _(t):
                t.wait_ge(ctx.sem("ac"), n_rsq)
                if qf >= 1:
                    t.wait_ge(ctx.sem("dv"), dv_base_q + 4 * (qf - 1) + 3)
                ctx.inc(t.matmul(P[3][:], indT2[:], nrm_all[:, qf, :], start=True, stop=True), "pe")
            n_bc = ctx.n("pe")

            @block.vector
            def _(v):
                v.wait_ge(ctx.sem("pe"), n_bc)
                ctx.inc(v.tensor_copy(rb_sb[:, qf % 2, :], P[3][:]), "dv")
                v.wait_ge(ctx.sem("dv"), ctx.n("dv"))
                if qf >= 2:
                    v.wait_ge(ctx.sem("ac"), ac_base_q + 5 * (qf - 2) + 5)
                ctx.inc(v.tensor_mul(t_sb[:, qf % 2, :], P[par3][:], rb_sb[:, qf % 2, :]), "dv")
            n_ql2 = ctx.n("dv")
            ctx.marks[("c_P", par3)] = ("dv", n_ql2)

            @block.scalar
            def _(sc):
                sc.wait_ge(ctx.sem("dv"), n_ql2)
                ctx.inc(sc.activation(t_sb[:, qf % 2, :], t_sb[:, qf % 2, :], AF.Exp), "ac")
                sc.wait_ge(ctx.sem("ac"), ctx.n("ac"))
                ctx.inc(sc.activation(q_bf[:, qf, s0:s0 + CH], t_sb[:, qf % 2, :], AF.Ln, bias=1.0), "ac")

        for qf in range(DT):
            emit_qgroup(qf)
            if qf >= 2:
                emit_qchain(qf - 2)
        emit_qchain(6)
        emit_qchain(7)
        n_qdone = ctx.n("ac")

        for st in range(4):
            par = st % 2
            for half in range(2):
                @block.tensor
                def _(t):
                    sem_name, val = ctx.marks.get(("c_P", par), ("pe", 0))
                    t.wait_ge(ctx.sem(sem_name), val)
                    sem_name, val = ctx.marks.get(("c_P", 2 + par), ("pe", 0))
                    t.wait_ge(ctx.sem(sem_name), val)
                    if st == 0 and half == 0:
                        t.wait_ge(ctx.sem("ac"), n_qdone)
                    for dt in range(DT):
                        t.matmul(P[par][:], xbf[:, dt, st * 128:(st + 1) * 128], wks[:, dt, half * 512:(half + 1) * 512], start=(dt == 0), stop=(dt == DT - 1))
                    for dt in range(DT):
                        ins = t.matmul(P[2 + par][:], xbf[:, dt, st * 128:(st + 1) * 128], wvs[:, dt, half * 512:(half + 1) * 512], start=(dt == 0), stop=(dt == DT - 1))
                    ctx.inc(ins, "pe")
                n_kvmm = ctx.n("pe")

                @block.scalar
                def _(sc):
                    sc.wait_ge(ctx.sem("pe"), n_kvmm)
                    ctx.inc(sc.square(sk_sb[:, half * 512:(half + 1) * 512], P[par][:]), "ac")
                n_sqk = ctx.n("ac")

                @block.vector
                def _(v):
                    v.wait_ge(ctx.sem("ac"), n_sqk)
                    ctx.inc(v.tensor_reduce(ssk[:, half * 8:(half + 1) * 8].rearrange("p (a b) -> p a b", b=1), sk_sb[:, half * 512:(half + 1) * 512].rearrange("p (a b) -> p a b", b=64), AX.X, ALU.add), "dv")
                n_ssk = ctx.n("dv")

                @block.vector
                def _(v):
                    v.wait_ge(ctx.sem("dv"), n_ssk)
                    ctx.inc(v.tensor_scalar_max(rk[:, half * 8:(half + 1) * 8], ssk[:, half * 8:(half + 1) * 8], 1e-24), "dv")
                n_rkmx = ctx.n("dv")

                @block.scalar
                def _(sc):
                    sc.wait_ge(ctx.sem("dv"), n_rkmx)
                    ctx.inc(sc.activation(rk[:, half * 8:(half + 1) * 8], rk[:, half * 8:(half + 1) * 8], AF.Ln), "ac")
                    sc.wait_ge(ctx.sem("ac"), ctx.n("ac"))
                    ctx.inc(sc.activation(rk[:, half * 8:(half + 1) * 8], rk[:, half * 8:(half + 1) * 8], AF.Exp, scale=-0.5), "ac")
                n_rk2 = ctx.n("ac")

                @block.vector
                def _(v):
                    v.wait_ge(ctx.sem("ac"), n_rk2)
                    ctx.inc(v.tensor_tensor(sk_sb[:, half * 512:(half + 1) * 512].rearrange("p (a b) -> p a b", b=64), P[par][:].rearrange("p (a b) -> p a b", b=64), rk[:, half * 8:(half + 1) * 8].rearrange("p (a b) -> p a b", b=1).broadcast_to([128, 8, 64]), ALU.mult), "dv")
                n_kn = ctx.n("dv")

                @block.scalar
                def _(sc):
                    sc.wait_ge(ctx.sem("dv"), n_kn)
                    ctx.inc(sc.activation(sk_sb[:, half * 512:(half + 1) * 512], sk_sb[:, half * 512:(half + 1) * 512], AF.Exp), "ac")
                    sc.wait_ge(ctx.sem("ac"), ctx.n("ac"))
                    ctx.inc(sc.activation(k_bf[:, par, half * 512:(half + 1) * 512], sk_sb[:, half * 512:(half + 1) * 512], AF.Ln, bias=1.0), "ac")

                @block.vector
                def _(v):
                    v.wait_ge(ctx.sem("pe"), n_kvmm)
                    ctx.inc(v.tensor_copy(v_bf[:, par, half * 512:(half + 1) * 512], P[2 + par][:]), "dv")
                ctx.marks[("c_P", 2 + par)] = ("dv", ctx.n("dv"))

            gst = c * 4 + st
            n_kv_ac = ctx.n("ac")
            n_kv_dv = ctx.n("dv")

            for hh in range(2):
                @block.tensor
                def _(t):
                    t.wait_ge(ctx.sem("ac"), n_kv_ac)
                    t.wait_ge(ctx.sem("dv"), n_kv_dv)
                    if gst > 0 or hh > 0:
                        t.wait_ge(ctx.sem("dv"), ctx.marks.get("c_kvacc", 0))
                    for h in range(hh * 8, hh * 8 + 8):
                        ins = t.matmul(psKV[0:64, (h - hh * 8) * 64:(h - hh * 8 + 1) * 64], k_bf[:, par, h * 64:(h + 1) * 64], v_bf[:, par, h * 64:(h + 1) * 64], start=True, stop=True)
                    if hh == 1:
                        for half in range(2):
                            ins = t.matmul(psKS[0:1, half * 512:(half + 1) * 512], ones128b[:], k_bf[:, par, half * 512:(half + 1) * 512], start=(gst == 0), stop=(gst == 4 * NCH - 1))
                    ctx.inc(ins, "pe")
                n_kvpe = ctx.n("pe")

                @block.vector
                def _(v):
                    v.wait_ge(ctx.sem("pe"), n_kvpe)
                    if gst == 0:
                        ctx.inc(v.tensor_copy(kvacc[:, hh * 512:(hh + 1) * 512], psKV[:]), "dv")
                    else:
                        ctx.inc(v.tensor_add(kvacc[:, hh * 512:(hh + 1) * 512], kvacc[:, hh * 512:(hh + 1) * 512], psKV[:]), "dv")
                ctx.marks["c_kvacc"] = ctx.n("dv")
    n_pass1_pe = ctx.n("pe")

    n_pass1_dv = ctx.n("dv")

    @block.vector
    def _(v):
        v.wait_ge(ctx.sem("pe"), n_pass1_pe)
        v.wait_ge(ctx.sem("dv"), n_pass1_dv)
        ctx.inc(v.tensor_scalar(kv_sb[:], kvacc[:], 0.125, None, ALU.mult), "dv")
        ctx.inc(v.tensor_scalar(ks_row[:], psKS[0:1, :], 1.0, 1e-6, ALU.mult, ALU.add), "dv")
    n_fin = ctx.n("dv")

    @block.sync
    def _(s):
        s.wait_ge(ctx.sem("dv"), n_fin)
        s.wait_ge(ctx.sem("ld"), ctx.n("ld"))
        ctx.inc(s.dma_start(out=kvscr.transpose([1, 2, 0, 3]), in_=kv_sb[:].rearrange("dh (j two c) -> dh j two c", two=2, c=64)), "ld", 16)
        s.wait_ge(ctx.sem("ld"), ctx.n("ld"))
        ctx.inc(s.dma_start(out=kv2_sb[:], in_=kvscr.rearrange("two dh j c -> (two dh) j c")), "ld", 16)
    n_kv2 = ctx.n("ld")

    @block.tensor
    def _(t):
        t.wait_ge(ctx.sem("dv"), n_fin)
        for t_i in range(8):
            ins = t.transpose(P[0][:, t_i:t_i + 1], ks_row[0:1, t_i * 128:(t_i + 1) * 128], ident_c[0:1, 0:1])
        ctx.inc(ins, "pe")
    n_ksfm_pe = ctx.n("pe")

    @block.vector
    def _(v):
        v.wait_ge(ctx.sem("pe"), n_ksfm_pe)
        ctx.inc(v.tensor_copy(ksfm[:], P[0][:, 0:8]), "dv")
    n_ksfm = ctx.n("dv")

    @block.vector
    def _(v):
        v.wait_ge(ctx.sem("dv"), n_ksfm)
        for t_i in range(8):
            ctx.inc(v.tensor_scalar(ks2[:, t_i, 0:1], ksfm[:, t_i:t_i + 1], maskA[:, 0:1], None, ALU.mult), "dv")
            ctx.inc(v.tensor_scalar(ks2[:, t_i, 1:2], ksfm[:, t_i:t_i + 1], maskB[:, 0:1], None, ALU.mult), "dv")
    n_ks2 = ctx.n("dv")

    st2_0 = ctx.n("st2")
    for c in range(NCH):
        s0 = c * CH

        @block.sync
        def _(s):
            s.wait_ge(ctx.sem("ld"), ctx.n("ld"))
            if c >= 1:
                s.wait_ge(ctx.sem("st2"), st2_0 + 16 * c)
            ctx.inc(s.dma_start(out=x_sb[:], in_=xin_r[:, :, s0:s0 + CH]), "ld", 16)
        n_xld = ctx.n("ld")

        dv_base_p2 = ctx.n("dv")
        for t_i in range(DT):
            @block.tensor
            def _(t):
                if c == 0 and t_i == 0:
                    t.wait_ge(ctx.sem("dv"), n_ks2)
                    t.wait_ge(ctx.sem("ld"), n_kv2)
                if t_i >= 1:
                    t.wait_ge(ctx.sem("dv"), dv_base_p2 + 3 * (t_i - 1) + 1)
                else:
                    ctx.wait(t, "dv")
                ctx.inc(t.matmul(psN[0:2, :], ks2[:, t_i, :], q_bf[:, t_i, s0:s0 + CH], start=True, stop=True), "pe")
            n_nmm = ctx.n("pe")

            @block.vector
            def _(v):
                v.wait_ge(ctx.sem("pe"), n_nmm)
                if t_i >= 1:
                    v.wait_ge(ctx.sem("dv"), dv_base_p2 + 3 * (t_i - 1) + 3)
                ctx.inc(v.tensor_copy(nrm2[:], psN[0:2, :]), "dv")
                v.wait_ge(ctx.sem("dv"), ctx.n("dv"))
                ctx.inc(v.reciprocal(nrm2[:], nrm2[:]), "dv")
                v.wait_ge(ctx.sem("dv"), ctx.n("dv"))
                ctx.inc(v.tensor_copy(rn_bf_all[:, t_i, :], nrm2[:]), "dv")

        for t_i in range(DT):
            par = t_i % 2

            @block.tensor
            def _(t):
                t.wait_ge(ctx.sem("dv"), dv_base_p2 + 3 * t_i + 3)
                if t_i >= 2:
                    t.wait_ge(ctx.sem("dv"), ctx.marks[("c2_qn", t_i - 2)])
                ctx.inc(t.matmul(P[2 + par][:], indT2b[:], rn_bf_all[:, t_i, :], start=True, stop=True), "pe")
            n_rb = ctx.n("pe")

            @block.vector
            def _(v):
                v.wait_ge(ctx.sem("pe"), n_rb)
                ctx.inc(v.tensor_mul(qn_all[:, t_i, :], q_bf[:, t_i, s0:s0 + CH], P[2 + par][:]), "dv")
            ctx.marks[("c2_qn", t_i)] = ctx.n("dv")

        for t_i in range(DT):
            par = t_i % 2

            @block.tensor
            def _(t):
                t.wait_ge(ctx.sem("dv"), ctx.marks[("c2_qn", t_i)])
                if t_i >= 2:
                    t.wait_ge(ctx.sem("dv"), ctx.marks[("c2_res", t_i - 2)])
                elif c > 0:
                    t.wait_ge(ctx.sem("dv"), ctx.marks.get(("c2_res_prev", t_i), 0))
                for h2 in range(2):
                    b0 = h2 * 64
                    ins = t.matmul(P[par][b0:b0 + 64, :], kv2_sb[b0:b0 + 64, t_i, :], qn_all[b0:b0 + 64, t_i, :], start=True, stop=True)
                ctx.inc(ins, "pe")
            n_omm = ctx.n("pe")

            @block.vector
            def _(v):
                v.wait_ge(ctx.sem("pe"), n_omm)
                v.wait_ge(ctx.sem("ld"), n_xld)
                ctx.inc(v.tensor_add(x_sb[:, t_i, :], x_sb[:, t_i, :], P[par][:]), "dv")
            ctx.marks[("c2_res", t_i)] = ctx.n("dv")
        for t_i in range(DT):
            ctx.marks[("c2_res_prev", t_i)] = ctx.marks[("c2_res", t_i)]
        n_resid = ctx.n("dv")

        emit_stats(ctx, block, x_sb, sq, P[0], P[1], ones128, None, DT,
                   extra_pe_wait=("dv", n_resid))
        emit_stats_final(ctx, block, P[0], P[1], mean_sb, var_sb, tmprow, 1024.0, 1e-5)
        emit_bcast2(ctx, block, mean_sb, tmprow, mb_sb, invb_sb, P[2], P[3], ones1)
        n_bc = ctx.n("dv")
        for dt in range(DT):
            @block.gpsimd
            def _(g):
                g.wait_ge(ctx.sem("dv"), n_bc)
                if dt >= 2:
                    ctx.wait(g, "dv")
                ctx.inc(g.tensor_sub(tmp1[:, dt % 2, :], x_sb[:, dt, :], mb_sb[:]), "pl")
            n_pl_sub = ctx.n("pl")

            @block.vector
            def _(v):
                v.wait_ge(ctx.sem("pl"), n_pl_sub)
                ctx.inc(v.tensor_mul(x_sb[:, dt, :], tmp1[:, dt % 2, :], invb_sb[:]), "dv")
        n_post = ctx.n("dv")

        @block.sync
        def _(s):
            s.wait_ge(ctx.sem("dv"), n_post)
            s.wait_ge(ctx.sem("st2"), ctx.n("st2"))
            ctx.inc(s.dma_start(out=xout_r[:, :, s0:s0 + CH], in_=x_sb[:]), "st2", 16)

    return ctx.n("st2")


def blockC_host_inputs(p):
    W_qkv = p["W_qkv"]
    indA = np.zeros((128, 2), np.float32)
    indA[0:64, 0] = 1.0
    indA[64:128, 1] = 1.0
    indT2 = np.ascontiguousarray(indA.T)
    return {
        "c_ident": np.eye(128, dtype=np.float32),
        "c_wqT": np.ascontiguousarray(W_qkv[0:1024].T),
        "c_wkT": np.ascontiguousarray(W_qkv[1024:2048].T),
        "c_wvT": np.ascontiguousarray(W_qkv[2048:3072].T),
        "c_indA": indA,
        "c_indT2": indT2,
        "c_maskA": indA[:, 0:1].copy(),
        "c_maskB": indA[:, 1:2].copy(),
    }


def declare_blockC_drams(nc, T):
    F = mybir.dt.float32
    shapes = {
        "c_ident": [128, 128],
        "c_wqT": [1024, 1024], "c_wkT": [1024, 1024], "c_wvT": [1024, 1024],
        "c_indA": [128, 2], "c_indT2": [2, 128], "c_maskA": [128, 1], "c_maskB": [128, 1],
    }
    return {k: nc.dram_tensor(k, v, F, kind="ExternalInput").ap() for k, v in shapes.items()}



def emit_blockD(ctx, block, T, xin, xout, W, acc_hbm, ar_in, ar_out, use_cc=True):
    nc = ctx.nc
    NCH = T // CH
    DT = 8
    FT = 16
    NE = 8
    CAP = 2048.0

    xin_r = xin.rearrange("(t p) s -> p t s", p=128)
    xout_r = xout.rearrange("(t p) s -> p t s", p=128)
    acc_r = acc_hbm.rearrange("(t p) s -> p t s", p=128)
    gw_r = W["d_Wg"].rearrange("(t p) e -> p t e", p=128)
    w1_r = W["d_W1"].rearrange("E (t p) f -> E p t f", p=128)
    w2_r = W["d_W2"].rearrange("E (t p) f -> E p t f", p=128)
    b1_r = W["d_b1fm"]
    b2_r = W["d_b2r"]

    xf_bf = ctx.sbuf("d_xf", [128, DT, T], BF16)
    gwt = ctx.sbuf("d_gwt_sb", [128, DT, 8], F32)
    tri = ctx.sbuf("d_tri_sb", [128, 128], F32)
    ident = ctx.sbuf("d_ident_sb", [128, 128], F32)
    sel = ctx.sbuf("d_sel_sb", [8, NE * 128], BF16)
    rowmask = ctx.sbuf("d_rowmask_sb", [1, 128], F32)
    thmask = ctx.sbuf("d_thmask_sb", [128, 24], F32)
    e_sb = ctx.sbuf("d_e", [128, 8], F32)
    nm1 = ctx.sbuf("d_nm1", [128, 1], F32)
    ssum = ctx.sbuf("d_ssum", [128, 1], F32)
    rsum = ctx.sbuf("d_rsum", [128, 1], F32)
    probs = ctx.sbuf("d_probs", [128, 8], F32)
    m1p = ctx.sbuf("d_m1p", [128, 1], F32)
    tnot = ctx.sbuf("d_tnot", [128, 8], F32)
    probs2 = ctx.sbuf("d_probs2", [128, 8], F32)
    m2p = ctx.sbuf("d_m2p", [128, 1], F32)
    den = ctx.sbuf("d_den", [128, 1], F32)
    rden = ctx.sbuf("d_rden", [128, 1], F32)
    oh0_sb = ctx.sbuf("d_oh0", [128, 32, 8], F32)
    oh1_sb = ctx.sbuf("d_oh1", [128, 32, 8], F32)
    cum0_sb = ctx.sbuf("d_cum0", [128, 32, 8], F32)
    cum1_sb = ctx.sbuf("d_cum1", [128, 32, 8], F32)
    w0_sb = ctx.sbuf("d_w0", [128, 32], F32)
    w1c_sb = ctx.sbuf("d_w1c", [128, 32], F32)
    off0 = ctx.sbuf("d_off0", [1, 8], F32)
    off1 = ctx.sbuf("d_off1", [1, 8], F32)
    n01 = ctx.sbuf("d_n01", [1, 16], F32)
    contrib = ctx.sbuf("d_contrib", [1, 128], F32)
    allcnt = ctx.sbuf("d_allcnt", [1, 128], F32)
    accol = ctx.sbuf("d_accol", [128, 1], F32)
    one11 = ctx.sbuf("d_one11", [1, 1], F32)
    th_row = ctx.sbuf("d_throw", [1, 24], F32)
    th0 = ctx.sbuf("d_th0", [1, 8], F32)
    th1 = ctx.sbuf("d_th1", [1, 8], F32)
    t0k = ctx.sbuf("d_t0k", [1, 8], F32)
    th0b = ctx.sbuf("d_th0b", [128, 8], F32)
    th1b = ctx.sbuf("d_th1b", [128, 8], F32)
    keep = ctx.sbuf("d_keep", [128, 8], F32)
    keep1 = ctx.sbuf("d_keep1", [128, 8], F32)
    comb = ctx.sbuf("d_comb", [128, 8], F32)
    combT = ctx.sbuf("d_combT", [8, T], BF16)
    w1hs = ctx.sbuf("d_w1hs", [128, DT, 2048], BF16)
    w2hs = ctx.sbuf("d_w2hs", [128, FT, 1024], BF16)
    b1s = ctx.sbuf("d_b1s", [128, 2, 32], F32)
    b2s = ctx.sbuf("d_b2s", [1, 2, 1024], BF16)
    onesrow_bf = ctx.sbuf("d_onesrow", [1, CH], BF16)
    h_bf = ctx.sbuf("d_h", [128, FT, CH], BF16)
    x3_sb = h_bf[:].bitcast(F32).rearrange("p a b -> p (a b)").rearrange("p (t s) -> p t s", t=DT)
    cb_sb = ctx.sbuf("d_cb", [128, CH], F32)
    t_sb = ctx.sbuf("d_t", [128, DT, CH], F32)
    ones128 = ctx.sbuf("d_ones128", [128, 1], F32)
    ones1 = ctx.sbuf("d_ones1", [1, 128], F32)
    sq = w1hs[:, 0:2, 0:1024].bitcast(F32).rearrange("p a (x b) -> p (a x) b", b=CH)[:, 0:2, :]
    tmp1 = w1hs[:, 2:4, 0:1024].bitcast(F32).rearrange("p a (x b) -> p (a x) b", b=CH)[:, 0:2, :]
    mb_sb = w1hs[:, 4, 0:1024].bitcast(F32)[:, 0:CH]
    invb_sb = w1hs[:, 5, 0:1024].bitcast(F32)[:, 0:CH]
    mean_sb = w2hs[0:1, 0, :].bitcast(F32)[:, 0:CH]
    var_sb = w2hs[0:1, 1, :].bitcast(F32)[:, 0:CH]
    tmprow = w2hs[0:1, 2, :].bitcast(F32)[:, 0:CH]

    psA = [ctx.psum("d_psA0", [128, CH]), ctx.psum("d_psA1", [128, CH]),
           ctx.psum("d_psA2", [128, CH]), ctx.psum("d_psA3", [128, CH])]
    psO = [ctx.psum("d_psO0", [128, CH]), ctx.psum("d_psO1", [128, CH])]
    psB = ctx.psum("d_psB", [128, CH])
    psG = ctx.psum("d_psG", [128, 512])

    @block.sync
    def _(s):
        s.wait_ge(ctx.sem("ld"), ctx.n("ld"))
        ctx.inc(s.dma_start(out=gwt[:], in_=gw_r[:]), "ld", 16)
        s.wait_ge(ctx.sem("ld"), ctx.n("ld"))
        ctx.inc(s.dma_start(out=tri[:], in_=W["d_tri"]), "ld", 16)
        s.wait_ge(ctx.sem("ld"), ctx.n("ld"))
        ctx.inc(s.dma_start(out=ident[:], in_=W["d_ident"]), "ld", 16)
        s.wait_ge(ctx.sem("ld"), ctx.n("ld"))
        ctx.inc(s.dma_start(out=rowmask[:], in_=W["d_rowmask"]), "ld", 16)
        s.wait_ge(ctx.sem("ld"), ctx.n("ld"))
        ctx.inc(s.dma_start(out=thmask[:], in_=W["d_thmask"]), "ld", 16)
    n_param_ld = ctx.n("ld")

    @block.gpsimd
    def _(g):
        g.wait_ge(ctx.sem("wq"), ctx.n("wq"))
        ctx.inc(g.dma_start(out=sel[:], in_=W["d_sel"]), "wq", 16)
        ctx.inc(g.memset(ones128[:], 1.0), "pl")
        ctx.inc(g.memset(ones1[:], 1.0), "pl")
        ctx.inc(g.memset(one11[:], 1.0), "pl")
        ctx.inc(g.memset(onesrow_bf[:], 1.0), "pl")
        ctx.inc(g.memset(off0[:], 0.0), "pl")
        ctx.inc(g.memset(off1[:], 0.0), "pl")
    n_init_pl = ctx.n("pl")
    n_init_wq = ctx.n("wq")

    for c in range(NCH):
        s0 = c * CH

        @block.sync
        def _(s):
            s.wait_ge(ctx.sem("ld"), ctx.n("ld"))
            if c >= 1:
                ctx.wait(s, "pe")
            ctx.inc(s.dma_start(out=x3_sb[:], in_=xin_r[:, :, s0:s0 + CH]), "ld", 16)
        n_xld = ctx.n("ld")

        @block.gpsimd
        def _(g):
            g.wait_ge(ctx.sem("wq"), ctx.n("wq"))
            ctx.inc(g.dma_start(out=xf_bf[:, :, s0:s0 + CH], in_=xin_r[:, :, s0:s0 + CH]), "wq", 16)
        n_xfld = ctx.n("wq")

        for st in range(4):
            sti = c * 4 + st

            @block.tensor
            def _(t):
                t.wait_ge(ctx.sem("ld"), n_xld)
                if sti == 0:
                    t.wait_ge(ctx.sem("pl"), n_init_pl)
                    t.wait_ge(ctx.sem("ld"), n_param_ld)
                ctx.wait(t, "dv")
                for dt in range(DT):
                    ins = t.matmul(psG[:, 0:8], x3_sb[:, dt, st * 128:(st + 1) * 128], gwt[:, dt, :], start=(dt == 0), stop=(dt == DT - 1))
                ctx.inc(ins, "pe")
            n_gmm = ctx.n("pe")

            @block.vector
            def _(v):
                v.wait_ge(ctx.sem("pe"), n_gmm)
                ctx.inc(v.tensor_reduce(nm1[:].rearrange("p (a b) -> p a b", b=1), psG[:, 0:8].rearrange("p (a b) -> p a b", b=8), AX.X, ALU.max, negate=True), "dv")
            n_nm1 = ctx.n("dv")

            @block.scalar
            def _(sc):
                sc.wait_ge(ctx.sem("dv"), n_nm1)
                ctx.inc(sc.activation(e_sb[:], psG[:, 0:8], AF.Exp, bias=nm1[:, 0:1]), "ac")
            n_exp = ctx.n("ac")

            @block.vector
            def _(v):
                v.wait_ge(ctx.sem("ac"), n_exp)
                ctx.inc(v.tensor_reduce(ssum[:].rearrange("p (a b) -> p a b", b=1), e_sb[:].rearrange("p (a b) -> p a b", b=8), AX.X, ALU.add), "dv")
                v.wait_ge(ctx.sem("dv"), ctx.n("dv"))
                ctx.inc(v.reciprocal(rsum[:], ssum[:]), "dv")
                v.wait_ge(ctx.sem("dv"), ctx.n("dv"))
                ctx.inc(v.tensor_scalar(probs[:], e_sb[:], rsum[:, 0:1], None, ALU.mult), "dv")
                v.wait_ge(ctx.sem("dv"), ctx.n("dv"))
                ctx.inc(v.tensor_reduce(m1p[:].rearrange("p (a b) -> p a b", b=1), probs[:].rearrange("p (a b) -> p a b", b=8), AX.X, ALU.max), "dv")
                v.wait_ge(ctx.sem("dv"), ctx.n("dv"))
                ctx.inc(v.tensor_scalar(oh0_sb[:, sti, :], probs[:], m1p[:, 0:1], None, ALU.is_ge), "dv")
                v.wait_ge(ctx.sem("dv"), ctx.n("dv"))
                ctx.inc(v.tensor_scalar(tnot[:], oh0_sb[:, sti, :], -1.0, 1.0, ALU.mult, ALU.add), "dv")
                v.wait_ge(ctx.sem("dv"), ctx.n("dv"))
                ctx.inc(v.tensor_mul(probs2[:], probs[:], tnot[:]), "dv")
                v.wait_ge(ctx.sem("dv"), ctx.n("dv"))
                ctx.inc(v.tensor_reduce(m2p[:].rearrange("p (a b) -> p a b", b=1), probs2[:].rearrange("p (a b) -> p a b", b=8), AX.X, ALU.max), "dv")
                v.wait_ge(ctx.sem("dv"), ctx.n("dv"))
                ctx.inc(v.tensor_scalar(oh1_sb[:, sti, :], probs2[:], m2p[:, 0:1], None, ALU.is_ge), "dv")
                v.wait_ge(ctx.sem("dv"), ctx.n("dv"))
                ctx.inc(v.tensor_scalar(den[:], m1p[:], 1.0, 1e-9, ALU.mult, ALU.add), "dv")
                v.wait_ge(ctx.sem("dv"), ctx.n("dv"))
                ctx.inc(v.tensor_add(den[:], den[:], m2p[:]), "dv")
                v.wait_ge(ctx.sem("dv"), ctx.n("dv"))
                ctx.inc(v.reciprocal(rden[:], den[:]), "dv")
                v.wait_ge(ctx.sem("dv"), ctx.n("dv"))
                ctx.inc(v.tensor_mul(w0_sb[:, sti:sti + 1], m1p[:], rden[:]), "dv")
                ctx.inc(v.tensor_mul(w1c_sb[:, sti:sti + 1], m2p[:], rden[:]), "dv")
            n_top2 = ctx.n("dv")

            @block.tensor
            def _(t):
                t.wait_ge(ctx.sem("dv"), n_top2)
                t.matmul(psG[:, 8:16], tri[:], oh0_sb[:, sti, :], start=True, stop=False)
                ctx.inc(t.matmul(psG[:, 8:16], ones1[:], off0[:], start=False, stop=True), "pe")
                t.matmul(psG[:, 16:24], tri[:], oh1_sb[:, sti, :], start=True, stop=False)
                ctx.inc(t.matmul(psG[:, 16:24], ones1[:], off1[:], start=False, stop=True), "pe")
                ctx.inc(t.matmul(psG[0:1, 24:32], ones128[:], oh0_sb[:, sti, :], start=True, stop=True), "pe")
                ctx.inc(t.matmul(psG[0:1, 32:40], ones128[:], oh1_sb[:, sti, :], start=True, stop=True), "pe")
            n_cum = ctx.n("pe")

            @block.vector
            def _(v):
                v.wait_ge(ctx.sem("pe"), n_cum)
                ctx.inc(v.tensor_copy(cum0_sb[:, sti, :], psG[:, 8:16]), "dv")
                ctx.inc(v.tensor_copy(cum1_sb[:, sti, :], psG[:, 16:24]), "dv")
                v.wait_ge(ctx.sem("dv"), ctx.n("dv"))
                ctx.inc(v.tensor_add(off0[:], off0[:], psG[0:1, 24:32]), "dv")
                ctx.inc(v.tensor_add(off1[:], off1[:], psG[0:1, 32:40]), "dv")
    n_gate_dv = ctx.n("dv")
    n_gate_wq = ctx.n("wq")

    @block.vector
    def _(v):
        v.wait_ge(ctx.sem("dv"), n_gate_dv)
        ctx.inc(v.tensor_copy(n01[:, 0:8], off0[:]), "dv")
        ctx.inc(v.tensor_copy(n01[:, 8:16], off1[:]), "dv")
        v.wait_ge(ctx.sem("dv"), ctx.n("dv"))
        ctx.inc(v.tensor_tensor(contrib[:].rearrange("o (a b) -> o a b", b=16), rowmask[:].rearrange("o (a b) -> o a b", b=16), n01[:].unsqueeze(1).broadcast_to([1, 8, 16]), ALU.mult), "dv")
    n_contrib = ctx.n("dv")

    @block.sync
    def _(s):
        s.wait_ge(ctx.sem("dv"), n_contrib)
        s.wait_ge(ctx.sem("ld"), ctx.n("ld"))
        ctx.inc(s.dma_start(out=ar_in[:], in_=contrib[:]), "ld", 16)
    n_arin = ctx.n("ld")

    @block.gpsimd
    def _(g):
        g.wait_ge(ctx.sem("ld"), n_arin)
        g.wait_ge(ctx.sem("wq"), ctx.n("wq"))
        if use_cc:
            ctx.inc(g.collective_compute("AllReduce", ALU.add, [[0, 1, 2, 3, 4, 5, 6, 7]], [ar_in[:]], [ar_out[:]]), "cc", 1)
            g.wait_ge(ctx.sem("cc"), ctx.n("cc"))
        else:
            ctx.inc(g.dma_start(out=ar_out[:], in_=ar_in[:]), "wq", 16)
            g.wait_ge(ctx.sem("wq"), ctx.n("wq"))
        ctx.inc(g.dma_start(out=allcnt[:], in_=ar_out[:]), "wq", 16)
    n_allcnt = ctx.n("wq")

    @block.tensor
    def _(t):
        t.wait_ge(ctx.sem("wq"), n_allcnt)
        ctx.wait(t, "dv")
        ctx.inc(t.matmul(psG[:, 40:41], allcnt[:], one11[:], start=True, stop=True), "pe")
    n_acol = ctx.n("pe")

    @block.vector
    def _(v):
        v.wait_ge(ctx.sem("pe"), n_acol)
        ctx.inc(v.tensor_copy(accol[:], psG[:, 40:41]), "dv")
    n_accol = ctx.n("dv")

    @block.tensor
    def _(t):
        t.wait_ge(ctx.sem("dv"), n_accol)
        ctx.inc(t.matmul(psG[0:1, 48:72], accol[:], thmask[:], start=True, stop=True), "pe")
    n_th = ctx.n("pe")

    @block.vector
    def _(v):
        v.wait_ge(ctx.sem("pe"), n_th)
        ctx.inc(v.tensor_copy(th_row[:], psG[0:1, 48:72]), "dv")
        v.wait_ge(ctx.sem("dv"), ctx.n("dv"))
        ctx.inc(v.tensor_scalar(th0[:], th_row[:, 0:8], -1.0, CAP, ALU.mult, ALU.add), "dv")
        ctx.inc(v.tensor_scalar_min(t0k[:], th_row[:, 8:16], CAP), "dv")
        v.wait_ge(ctx.sem("dv"), ctx.n("dv"))
        ctx.inc(v.tensor_scalar(th1[:], th_row[:, 16:24], -1.0, CAP, ALU.mult, ALU.add), "dv")
        v.wait_ge(ctx.sem("dv"), ctx.n("dv"))
        ctx.inc(v.tensor_sub(th1[:], th1[:], t0k[:]), "dv")
    n_thf = ctx.n("dv")

    @block.tensor
    def _(t):
        t.wait_ge(ctx.sem("dv"), n_thf)
        t.matmul(psG[:, 72:80], ones1[:], th0[:], start=True, stop=True)
        ctx.inc(t.matmul(psG[:, 80:88], ones1[:], th1[:], start=True, stop=True), "pe")
    n_thb = ctx.n("pe")

    @block.vector
    def _(v):
        v.wait_ge(ctx.sem("pe"), n_thb)
        ctx.inc(v.tensor_copy(th0b[:], psG[:, 72:80]), "dv")
        ctx.inc(v.tensor_copy(th1b[:], psG[:, 80:88]), "dv")
    n_thbc = ctx.n("dv")

    for sti in range(4 * NCH):
        @block.vector
        def _(v):
            if sti == 0:
                v.wait_ge(ctx.sem("dv"), n_thbc)
            ctx.inc(v.tensor_tensor(keep[:], cum0_sb[:, sti, :], th0b[:], ALU.is_le), "dv")
            ctx.inc(v.tensor_tensor(keep1[:], cum1_sb[:, sti, :], th1b[:], ALU.is_le), "dv")
            v.wait_ge(ctx.sem("dv"), ctx.n("dv"))
            ctx.inc(v.tensor_mul(keep[:], keep[:], oh0_sb[:, sti, :]), "dv")
            ctx.inc(v.tensor_mul(keep1[:], keep1[:], oh1_sb[:, sti, :]), "dv")
            v.wait_ge(ctx.sem("dv"), ctx.n("dv"))
            ctx.inc(v.tensor_scalar(comb[:], keep[:], w0_sb[:, sti:sti + 1], None, ALU.mult), "dv")
            v.wait_ge(ctx.sem("dv"), ctx.n("dv"))
            ctx.inc(v.scalar_tensor_tensor(comb[:], keep1[:], w1c_sb[:, sti:sti + 1], comb[:], ALU.mult, ALU.add), "dv")
        n_comb = ctx.n("dv")

        @block.tensor
        def _(t):
            t.wait_ge(ctx.sem("dv"), n_comb)
            ctx.inc(t.transpose(psG[0:8, 128:256], comb[:], ident[:]), "pe")
        n_tp = ctx.n("pe")

        @block.vector
        def _(v):
            v.wait_ge(ctx.sem("pe"), n_tp)
            ctx.inc(v.tensor_copy(combT[:, sti * 128:(sti + 1) * 128], psG[0:8, 128:256]), "dv")
    n_combT = ctx.n("dv")

    w1h_free = 0
    w2h_free = 0
    evacA_d = [("pe", 0), ("pe", 0), ("pe", 0), ("pe", 0)]
    w2rd_d = 0
    tsb_rd_d = 0

    wqD1_0 = ctx.n("wqD1")
    wqD2_0 = ctx.n("wqD2")
    wqD3_0 = ctx.n("wqD3")

    @block.gpsimd
    def _(g):
        ctx.inc(g.dma_start(out=w1hs[:], in_=w1_r[0, :, :, 0:2048]), "wqD1", 16)
        ctx.inc(g.dma_start(out=w2hs[:], in_=w2_r[0, :, 0:16, :]), "wqD2", 16)
        ctx.inc(g.dma_start(out=b1s[:, 0, :], in_=b1_r[0]), "wqD3", 16)
        ctx.inc(g.dma_start(out=b2s[0:1, 0, :], in_=b2_r[0]), "wqD3", 16)

    for e in range(NE):
        for fh in range(2):
            q = e * 2 + fh
            n_w1l = wqD1_0 + 16 * (q + 1)
            n_w2l = wqD2_0 + 16 * (q + 1)
            n_b12 = wqD3_0 + 32 * (e + 1)

            for c in range(NCH):
                s0 = c * CH
                ac_base = ctx.n("ac")
                for f in range(FT):
                    par = f % 4

                    @block.tensor
                    def _(t):
                        if f == 0 and c == 0:
                            t.wait_ge(ctx.sem("wqD1"), n_w1l)
                            if fh == 0:
                                t.wait_ge(ctx.sem("wqD3"), n_b12)
                        if f >= 4:
                            t.wait_ge(ctx.sem("ac"), ac_base + (f - 4) + 1)
                        elif c > 0 or q > 0:
                            sem_name, val = evacA_d[par]
                            t.wait_ge(ctx.sem(sem_name), val)
                        for dt in range(DT):
                            ins = t.matmul(psA[par][:], w1hs[:, dt, f * 128:(f + 1) * 128], xf_bf[:, dt, s0:s0 + CH], start=(dt == 0), stop=(dt == DT - 1))
                        ctx.inc(ins, "pe")
                    ctx.marks[("d_hmm", f)] = ctx.n("pe")
                n_h_pe = ctx.n("pe")

                @block.scalar
                def _(sc):
                    for f in range(FT):
                        sc.wait_ge(ctx.sem("pe"), ctx.marks[("d_hmm", f)])
                        if c > 0 or q > 0:
                            sc.wait_ge(ctx.sem("pe"), w2rd_d)
                        ctx.inc(sc.activation(h_bf[:, f, :], psA[f % 4][:], AF.Gelu, bias=b1s[:, e % 2, fh * 16 + f:fh * 16 + f + 1]), "ac")
                        if f >= FT - 4:
                            evacA_d[f % 4] = ("ac", ctx.n("ac"))
                n_h = ctx.n("ac")

                if c == NCH - 1 and q < 2 * NE - 1:
                    e2, fh2 = divmod(q + 1, 2)

                    @block.gpsimd
                    def _(g):
                        g.wait_ge(ctx.sem("pe"), n_h_pe)
                        ctx.inc(g.dma_start(out=w1hs[:], in_=w1_r[e2, :, :, fh2 * 2048:(fh2 + 1) * 2048]), "wqD1", 16)
                        if fh2 == 0:
                            ctx.inc(g.dma_start(out=b1s[:, e2 % 2, :], in_=b1_r[e2]), "wqD3", 16)
                            ctx.inc(g.dma_start(out=b2s[0:1, e2 % 2, :], in_=b2_r[e2]), "wqD3", 16)

                @block.tensor
                def _(t):
                    t.wait_ge(ctx.sem("dv"), n_combT)
                    t.wait_ge(ctx.sem("dv"), ctx.marks.get("d_cb_rd", 0))
                    ctx.inc(t.matmul(psB[:], sel[:, e * 128:(e + 1) * 128], combT[:, s0:s0 + CH], start=True, stop=True), "pe")
                n_cb = ctx.n("pe")

                @block.vector
                def _(v):
                    v.wait_ge(ctx.sem("pe"), n_cb)
                    ctx.inc(v.tensor_copy(cb_sb[:], psB[:]), "dv")
                n_cbc = ctx.n("dv")

                for dt in range(DT):
                    par = dt % 2

                    @block.tensor
                    def _(t):
                        if dt == 0 and c == 0:
                            t.wait_ge(ctx.sem("wqD2"), n_w2l)
                        if dt == 0:
                            t.wait_ge(ctx.sem("ac"), n_h)
                        if dt >= 2:
                            t.wait_ge(ctx.sem("dv"), ctx.marks[("d_scale", dt - 2)])
                        elif c > 0 or q > 0:
                            t.wait_ge(ctx.sem("dv"), ctx.marks[("d_scale_prev", dt)])
                        for f in range(FT):
                            last = (f == FT - 1) and (fh == 0)
                            ins = t.matmul(psO[par][:], w2hs[:, f, dt * 128:(dt + 1) * 128], h_bf[:, f, :], start=(f == 0), stop=last)
                        if fh == 1:
                            ins = t.matmul(psO[par][:], b2s[0:1, e % 2, dt * 128:(dt + 1) * 128], onesrow_bf[:], start=False, stop=True)
                        ctx.inc(ins, "pe")
                    n_mm2 = ctx.n("pe")

                    @block.vector
                    def _(v):
                        v.wait_ge(ctx.sem("pe"), n_mm2)
                        v.wait_ge(ctx.sem("dv"), n_cbc)
                        if q > 0 or c > 0:
                            v.wait_ge(ctx.sem("st"), tsb_rd_d)
                        ctx.inc(v.tensor_mul(t_sb[:, dt, :], psO[par][:], cb_sb[:]), "dv")
                    ctx.marks[("d_scale", dt)] = ctx.n("dv")
                for dt in range(DT):
                    ctx.marks[("d_scale_prev", dt)] = ctx.marks[("d_scale", dt)]
                w2rd_d = ctx.n("pe")
                ctx.marks["d_cb_rd"] = ctx.n("dv")
                n_scale = ctx.n("dv")

                @block.gpsimd
                def _(g):
                    g.wait_ge(ctx.sem("dv"), n_scale)
                    g.wait_ge(ctx.sem("st"), ctx.n("st"))
                    op = ALU.bypass if (e == 0 and fh == 0) else ALU.add
                    ctx.inc(g.dma_start(out=acc_r[:, :, s0:s0 + CH], in_=t_sb[:], accum_op=op), "st", 16)
                tsb_rd_d = ctx.n("st")

            if q < 2 * NE - 1:
                e2, fh2 = divmod(q + 1, 2)

                @block.gpsimd
                def _(g):
                    g.wait_ge(ctx.sem("pe"), w2rd_d)
                    ctx.inc(g.dma_start(out=w2hs[:], in_=w2_r[e2, :, fh2 * 16:(fh2 + 1) * 16, :]), "wqD2", 16)
    n_acc_done = ctx.n("st")


    xb = [x3_sb, w1hs[:, 0:4, :].bitcast(F32).rearrange("p a (x b) -> p (a x) b", b=CH)]
    tb = [t_sb, w1hs[:, 4:8, :].bitcast(F32).rearrange("p a (x b) -> p (a x) b", b=CH)]
    sq2 = [w2hs[:, 0:2, :].bitcast(F32).rearrange("p a (x b) -> p (a x) b", b=CH)[:, 0:2, :],
           w2hs[:, 2:4, :].bitcast(F32).rearrange("p a (x b) -> p (a x) b", b=CH)[:, 0:2, :]]
    tmp2 = [w2hs[:, 4:6, :].bitcast(F32).rearrange("p a (x b) -> p (a x) b", b=CH)[:, 0:2, :],
            w2hs[:, 6:8, :].bitcast(F32).rearrange("p a (x b) -> p (a x) b", b=CH)[:, 0:2, :]]
    mb2 = [w2hs[:, 8, :].bitcast(F32), w2hs[:, 9, :].bitcast(F32)]
    ivb2 = [w2hs[:, 10, :].bitcast(F32), w2hs[:, 11, :].bitcast(F32)]
    mrow2 = [w2hs[0:1, 12, :].bitcast(F32), w2hs[0:1, 13, :].bitcast(F32)]
    vrow2 = [w2hs[0:1, 14, :].bitcast(F32), w2hs[0:1, 15, :].bitcast(F32)]
    trow2 = [xf_bf[0:1, 0, 0:1024].bitcast(F32), xf_bf[0:1, 1, 0:1024].bitcast(F32)]
    emit_postln_pipelined(ctx, block, T, xin_r, xout_r, acc_r, n_acc_done, "Dp",
                          xb, tb, sq2, tmp2, mb2, ivb2, mrow2, vrow2, trow2,
                          [psA[0], psA[1], psO[0], psO[1]], ones128, ones1)
    return ctx.n("st2")


def blockD_host_inputs(p, core_id):
    sel = np.zeros((8, 8 * 128), np.float32)
    for e in range(8):
        sel[e, e * 128:(e + 1) * 128] = 1.0
    rowmask = np.zeros((1, 128), np.float32)
    rowmask[0, core_id * 16:(core_id + 1) * 16] = 1.0
    thmask = np.zeros((128, 24), np.float32)
    for cp in range(8):
        for kp in range(2):
            for ep in range(8):
                pidx = cp * 16 + kp * 8 + ep
                if kp == 0 and cp < core_id:
                    thmask[pidx, ep] = 1.0
                if kp == 0:
                    thmask[pidx, 8 + ep] = 1.0
                if kp == 1 and cp < core_id:
                    thmask[pidx, 16 + ep] = 1.0
    b1fm = np.stack([p["b1"][e].reshape(32, 128).T for e in range(8)])
    return {
        "d_Wg": np.ascontiguousarray(p["Wg"]),
        "d_W1": np.ascontiguousarray(p["W1"]),
        "d_W2": np.ascontiguousarray(p["W2"]),
        "d_b1fm": np.ascontiguousarray(b1fm),
        "d_b2r": np.ascontiguousarray(p["b2"][:, None, :]),
        "d_tri": np.tril(np.ones((128, 128), np.float32)).T.copy(),
        "d_ident": np.eye(128, dtype=np.float32),
        "d_sel": sel,
        "d_rowmask": rowmask,
        "d_thmask": thmask,
    }


def declare_blockD_drams(nc, T):
    F = mybir.dt.float32
    shapes = {
        "d_Wg": [1024, 8], "d_W1": [8, 1024, 4096], "d_W2": [8, 4096, 1024],
        "d_b1fm": [8, 128, 32], "d_b2r": [8, 1, 1024],
        "d_tri": [128, 128], "d_ident": [128, 128], "d_sel": [8, 8 * 128],
        "d_rowmask": [1, 128], "d_thmask": [128, 24],
    }
    return {k: nc.dram_tensor(k, v, F, kind="ExternalInput").ap() for k, v in shapes.items()}



def build_fused(T=4096, use_cc=True):
    nc = bass.Bass("TRN2", target_bir_lowering=False, debug=False)
    F = F32
    xin = nc.dram_tensor("xin", [1024, T], F, kind="ExternalInput").ap()
    xout = nc.dram_tensor("xout", [1024, T], F, kind="ExternalOutput").ap()
    x1buf = nc.dram_tensor("x1buf", [1024, T], F, kind="Internal").ap()
    x2buf = nc.dram_tensor("x2buf", [1024, T], F, kind="Internal").ap()
    x3buf = nc.dram_tensor("x3buf", [1024, T], F, kind="Internal").ap()
    accbuf = nc.dram_tensor("accbuf", [1024, T], F, kind="Internal").ap()
    kvscr = nc.dram_tensor("kvscr", [2, 64, 8, 64], BF16, kind="Internal").ap()
    ksscr = nc.dram_tensor("ksscr", [8, 2, 64], F, kind="Internal").ap()
    ar_in = nc.dram_tensor("ar_in", [1, 128], F, kind="Internal").ap()
    ar_out = nc.dram_tensor("ar_out", [1, 128], F, kind="Internal").ap()
    WA = declare_blockA_drams(nc, T)
    WB = declare_blockB_drams(nc, T)
    WC = declare_blockC_drams(nc, T)
    WD = declare_blockD_drams(nc, T)

    outer = contextlib.ExitStack()
    with outer:
        ctx = Ctx(nc, outer)
        with nc.Block() as block:
            with contextlib.ExitStack() as es_a:
                ctx.es = es_a
                nA = emit_blockA(ctx, block, T, xin, x1buf, WA)
            emit_barrier(ctx, block, "st2", nA)
            with contextlib.ExitStack() as es_b:
                ctx.es = es_b
                nB = emit_blockB(ctx, block, T, x1buf, x2buf, WB, accbuf)
            emit_barrier(ctx, block, "st2", nB)
            with contextlib.ExitStack() as es_c:
                ctx.es = es_c
                nC = emit_blockC(ctx, block, T, x2buf, x3buf, WC, kvscr, ksscr)
            emit_barrier(ctx, block, "st2", nC)
            with contextlib.ExitStack() as es_d:
                ctx.es = es_d
                nD = emit_blockD(ctx, block, T, x3buf, xout, WD, accbuf, ar_in, ar_out, use_cc=use_cc)

            @block.sync
            def _(s):
                s.wait_ge(ctx.sem("st2"), nD)
    return nc



import concourse.bass_utils as _bu


def kernel(**inputs):
    T = 4096
    NCORES = 8

    x_full = np.asarray(inputs["L0_attn_x"], dtype=np.float32)

    pA = {k: np.asarray(inputs["L0_attn_" + k], dtype=np.float32) for k in
          ["ln_g", "ln_b", "W_in", "b_in", "conv_w", "conv_b", "A", "Bm",
           "W_c", "b_c", "W_d", "b_d", "log_delta", "W_out", "b_out"]}
    pB = {"gate_w": np.asarray(inputs["L0_ffn_gate_w"], dtype=np.float32),
          "w1": np.asarray(inputs["L0_ffn_w1"], dtype=np.float32),
          "w2": np.asarray(inputs["L0_ffn_w2"], dtype=np.float32),
          "w3": np.asarray(inputs["L0_ffn_w3"], dtype=np.float32)}
    pC = {"W_qkv": np.asarray(inputs["L1_attn_W_qkv"], dtype=np.float32)}
    pD = {"Wg": np.asarray(inputs["L1_ffn_Wg"], dtype=np.float32),
          "W1": np.asarray(inputs["L1_ffn_W1"], dtype=np.float32),
          "b1": np.asarray(inputs["L1_ffn_b1"], dtype=np.float32),
          "W2": np.asarray(inputs["L1_ffn_W2"], dtype=np.float32),
          "b2": np.asarray(inputs["L1_ffn_b2"], dtype=np.float32)}

    WA = blockA_host_inputs(pA)
    WB = blockB_host_inputs(pB)
    WC = blockC_host_inputs(pC)

    nc = build_fused(T=T, use_cc=True)

    maps = []
    for c in range(NCORES):
        m = {"xin": np.ascontiguousarray(x_full[c].T)}
        m.update(WA)
        m.update(WB)
        m.update(WC)
        m.update(blockD_host_inputs(pD, c))
        maps.append(m)

    res = _bu.run_bass_kernel_spmd(nc, maps, core_ids=list(range(NCORES)))
    out = np.stack([np.asarray(res.results[c]["xout"]).T for c in range(NCORES)], axis=0).astype(np.float32)
    return out


# version 14
# speedup vs baseline: 1.0681x; 1.0098x over previous
import contextlib
import numpy as np
import concourse.bass as bass
import concourse.mybir as mybir

F32 = mybir.dt.float32
BF16 = mybir.dt.bfloat16
AF = mybir.ActivationFunctionType
ALU = mybir.AluOpType
AX = mybir.AxisListType

CH = 512


class Ctx:
    def __init__(self, nc, es):
        self.nc = nc
        self.es = es
        self.sem_es = es
        self.sems = {}
        self.cnt = {}
        self.marks = {}
        self.sb = {}
        self.ps = {}

    def sem(self, name):
        if name not in self.sems:
            self.sems[name] = self.sem_es.enter_context(self.nc.semaphore(name))
            self.cnt[name] = 0
        return self.sems[name]

    def inc(self, instr, name, k=1):
        s = self.sem(name)
        self.cnt[name] += k
        instr.then_inc(s, k)
        return self.cnt[name]

    def n(self, name):
        self.sem(name)
        return self.cnt[name]

    def wait(self, eng, name, val=None):
        s = self.sem(name)
        eng.wait_ge(s, self.cnt[name] if val is None else val)

    def sbuf(self, name, shape, dtype):
        t = self.es.enter_context(self.nc.sbuf_tensor(name, shape, dtype))
        self.sb[name] = t
        return t

    def psum(self, name, shape, dtype=F32):
        t = self.es.enter_context(self.nc.psum_tensor(name, shape, dtype))
        self.ps[name] = t
        return t



def emit_blockA(ctx, block, T, xin, xout, W):
    nc = ctx.nc
    NCH = T // CH
    DT = 8
    ET = 16

    xin_r = xin.rearrange("(t p) s -> p t s", p=128)
    xout_r = xout.rearrange("(t p) s -> p t s", p=128)
    w_in_r = W["a_w_inT"].rearrange("(t p) f -> p t f", p=128)
    w_c_r = W["a_w_cT"].rearrange("(t p) f -> p t f", p=128)
    w_d_r = W["a_w_dT"].rearrange("(t p) f -> p t f", p=128)
    w_out_r = W["a_w_outT"].rearrange("(t p) f -> p t f", p=128)

    x_sb2 = ctx.sbuf("a_x2", [128, 2, DT, CH], F32)
    mean_sb2 = ctx.sbuf("a_mean2", [1, 2, CH], F32)
    var_sb2 = ctx.sbuf("a_var2", [1, 2, CH], F32)
    tmprow2 = ctx.sbuf("a_tmprow2", [1, 2, CH], F32)
    mb_sb2 = ctx.sbuf("a_mb2", [128, 2, CH], F32)
    invb_sb2 = ctx.sbuf("a_invb2", [128, 2, CH], F32)
    tmp_ln = ctx.sbuf("a_tmpln", [128, 2, CH], F32)
    tmp_po = ctx.sbuf("a_tmppo", [128, 2, CH], F32)
    sq_ln = ctx.sbuf("a_sqln", [128, DT, CH], BF16)
    ones128b_a = ctx.sbuf("a_ones128b", [128, 1], BF16)
    ln1_bf2 = ctx.sbuf("a_ln12", [128, 2, DT, CH], BF16)
    NSB = 3
    ws_in = ctx.sbuf("a_ws_in", [128, NSB, DT, 128], BF16)
    ws_c = ctx.sbuf("a_ws_c", [128, 2, ET, 128], BF16)
    ws_d = ctx.sbuf("a_ws_d", [128, 2, ET, 128], BF16)
    w_out_sb = ctx.sbuf("a_w_out_sb", [128, ET, 1024], BF16)
    xp_ext = ctx.sbuf("a_xpext", [128, 2, CH + 3], F32)
    acc_sb = ctx.sbuf("a_acc", [128, 2, CH], F32)
    sq = acc_sb
    carry_sb = ctx.sbuf("a_carry", [128, ET, 3], F32)
    xc_bf = ctx.sbuf("a_xc", [128, ET, CH], BF16)
    sg_bf = ctx.sbuf("a_sg", [128, ET, CH], BF16)
    sd_sb = ctx.sbuf("a_sd", [128, 2, CH], F32)
    u_sb = ctx.sbuf("a_u", [128, 2, CH], F32)
    v_sb = ctx.sbuf("a_v", [128, 2, CH], F32)
    vcarry = ctx.sbuf("a_vcarry", [128, ET], F32)
    abm = sd_sb[:, 0, 0:256].rearrange("p (a b) -> p a b", b=16)
    bm2 = sd_sb[:, 1, 0:256].rearrange("p (a b) -> p a b", b=16)
    gated_bf = ctx.sbuf("a_gated", [128, ET, CH], BF16)
    ones128 = ctx.sbuf("a_ones128", [128, 1], F32)
    ones1 = ctx.sbuf("a_ones1", [1, 128], F32)
    lng = ctx.sbuf("a_lng", [128, DT], F32)
    lnb = ctx.sbuf("a_lnb", [128, DT], F32)
    binxp = ctx.sbuf("a_binxp", [128, ET], F32)
    bing = ctx.sbuf("a_bing", [128, ET], F32)
    convw = ctx.sbuf("a_convw", [128, ET, 4], F32)
    convb = ctx.sbuf("a_convb", [128, ET], F32)
    cfm = ctx.sbuf("a_cfm", [128, ET], F32)
    ld_fm = ctx.sbuf("a_ldfm", [128, ET], F32)
    a_fm = ctx.sbuf("a_afm", [128, ET], F32)
    sp_t1 = v_sb[:, 0, 0:ET]
    sp_t2 = sd_sb[:, 0, 0:ET]
    beta = ctx.sbuf("a_beta", [128, ET], F32)
    wsum = ctx.sbuf("a_wsum", [128, ET], F32)
    corr = u_sb[:, 0, 0:ET * 3].rearrange("p (a b) -> p a b", b=3)
    bc_fm = ctx.sbuf("a_bcfm", [128, ET], F32)
    bd_fm = ctx.sbuf("a_bdfm", [128, ET], F32)
    bout_fm = ctx.sbuf("a_bout", [128, DT], F32)

    psA = [ctx.psum("a_psA0", [128, CH]), ctx.psum("a_psA1", [128, CH])]
    psC = [ctx.psum("a_psC0", [128, CH]), ctx.psum("a_psC1", [128, CH])]
    psS4 = [ctx.psum("a_psS0", [128, CH]), ctx.psum("a_psS1", [128, CH]),
            ctx.psum("a_psS2", [128, CH]), ctx.psum("a_psS3", [128, CH])]
    psB1 = psS4[2]
    psB2 = psS4[3]

    @block.sync
    def _(s):
        for name, dst in [
            ("a_lng_fm", lng), ("a_lnb_fm", lnb), ("a_binxp_fm", binxp),
            ("a_bing_fm", bing), ("a_convw_fm", convw), ("a_convb_fm", convb),
            ("a_A_fm", abm), ("a_Bm_fm", bm2), ("a_ld_fm", ld_fm),
            ("a_bc_fm", bc_fm), ("a_bd_fm", bd_fm), ("a_bout_fm", bout_fm),
        ]:
            ctx.inc(s.dma_start(out=dst[:], in_=W[name]), "ld", 16)
    n_param_ld = ctx.n("ld")

    @block.gpsimd
    def _(g):
        ctx.inc(g.memset(ones128[:], 1.0), "pl")
        ctx.inc(g.memset(ones1[:], 1.0), "pl")
        ctx.inc(g.memset(carry_sb[:], 0.0), "pl")
        ctx.inc(g.memset(vcarry[:], 0.0), "pl")
        ctx.inc(g.memset(ones128b_a[:], 1.0), "pl")
        g.wait_ge(ctx.sem("wqo0"), ctx.n("wqo0"))
        ctx.inc(g.dma_start(out=w_out_sb[:], in_=w_out_r[:]), "wqo0", 16)
    n_init_pl = ctx.n("pl")
    n_wout_ld = ctx.n("wqo0")

    @block.vector
    def _(v):
        v.wait_ge(ctx.sem("ld"), n_param_ld)
        ctx.inc(v.tensor_mul(abm[:], abm[:], bm2[:]), "dv")
        v.wait_ge(ctx.sem("dv"), ctx.n("dv"))
        ctx.inc(v.tensor_reduce(cfm[:].rearrange("p (a b) -> p a b", b=1), abm[:], AX.X, ALU.add), "dv")
        ctx.inc(v.tensor_reduce(wsum[:].rearrange("p (a b) -> p a b", b=1), convw[:], AX.X, ALU.add), "dv")
        v.wait_ge(ctx.sem("dv"), ctx.n("dv"))
        ctx.inc(v.tensor_mul(beta[:], binxp[:], wsum[:]), "dv")
        v.wait_ge(ctx.sem("dv"), ctx.n("dv"))
        ctx.inc(v.tensor_add(beta[:], beta[:], convb[:]), "dv")
        v.wait_ge(ctx.sem("dv"), ctx.n("dv"))
        ctx.inc(v.tensor_copy(corr[:, :, 2], convw[:, :, 0]), "dv")
        v.wait_ge(ctx.sem("dv"), ctx.n("dv"))
        ctx.inc(v.tensor_add(corr[:, :, 1], convw[:, :, 0], convw[:, :, 1]), "dv")
        v.wait_ge(ctx.sem("dv"), ctx.n("dv"))
        ctx.inc(v.tensor_add(corr[:, :, 0], corr[:, :, 1], convw[:, :, 2]), "dv")
        v.wait_ge(ctx.sem("dv"), ctx.n("dv"))
        for k in range(3):
            ctx.inc(v.tensor_mul(corr[:, :, k], corr[:, :, k], binxp[:]), "dv")
            v.wait_ge(ctx.sem("dv"), ctx.n("dv"))
        ctx.inc(v.tensor_scalar_mul(corr[:], corr[:], -1.0), "dv")

    @block.scalar
    def _(sc):
        sc.wait_ge(ctx.sem("ld"), n_param_ld)
        ctx.inc(sc.activation(sp_t1[:], ld_fm[:], AF.Abs), "ac")
        sc.wait_ge(ctx.sem("ac"), ctx.n("ac"))
        ctx.inc(sc.activation(sp_t1[:], sp_t1[:], AF.Exp, scale=-1.0), "ac")
        sc.wait_ge(ctx.sem("ac"), ctx.n("ac"))
        ctx.inc(sc.activation(sp_t1[:], sp_t1[:], AF.Ln, bias=1.0), "ac")
    n_sp_ac = ctx.n("ac")

    @block.vector
    def _(v):
        v.wait_ge(ctx.sem("ac"), n_sp_ac)
        ctx.inc(v.tensor_scalar_max(sp_t2[:], ld_fm[:], 0.0), "dv")
        v.wait_ge(ctx.sem("dv"), ctx.n("dv"))
        ctx.inc(v.tensor_add(sp_t2[:], sp_t2[:], sp_t1[:]), "dv")
    n_sp_dv = ctx.n("dv")

    @block.scalar
    def _(sc):
        sc.wait_ge(ctx.sem("dv"), n_sp_dv)
        ctx.inc(sc.activation(a_fm[:], sp_t2[:], AF.Exp, scale=-1.0), "ac")
    n_pre_ac = ctx.n("ac")
    n_pre_dv = ctx.n("dv")

    @block.tensor
    def _(t):
        t.wait_ge(ctx.sem("pl"), n_init_pl)
        t.wait_ge(ctx.sem("dv"), n_pre_dv)
        t.wait_ge(ctx.sem("ac"), n_pre_ac)

    slotA = [0, 0, 0]
    slotCD = [0, 0, 0]
    slotO = [0, 0]

    def emit_lnphase(c):
        pp = c % 2
        s0 = c * CH
        xs = x_sb2[:, pp]
        pS, pQ = psS4[2 * pp], psS4[2 * pp + 1]

        @block.sync
        def _(s):
            if c >= 2:
                s.wait_ge(ctx.sem("st2"), ctx.marks[("a_st2", c - 2)])
            s.wait_ge(ctx.sem("ld"), ctx.n("ld"))
            ctx.inc(s.dma_start(out=xs[:], in_=xin_r[:, :, s0:s0 + CH]), "ld", 16)
        n_xld = ctx.n("ld")

        @block.gpsimd
        def _(g):
            g.wait_ge(ctx.sem("ld"), n_xld)
            g.wait_ge(ctx.sem("pe"), ctx.marks.get("a_sqln_free", 0))
            for dt in range(DT):
                ctx.inc(g.tensor_mul(sq_ln[:, dt, :], xs[:, dt, :], xs[:, dt, :]), "pl")
        n_sqall = ctx.n("pl")

        @block.tensor
        def _(t):
            t.wait_ge(ctx.sem("pl"), n_sqall)
            t.wait_ge(ctx.sem("dv"), ctx.marks.get(("a_psS_rd", pp), 0))
            for dt in range(DT):
                t.matmul(pS[0:1, :], ones128[:], xs[:, dt, :], start=(dt == 0), stop=(dt == DT - 1))
            for dt in range(DT):
                ins = t.matmul(pQ[0:1, :], ones128b_a[:], sq_ln[:, dt, :], start=(dt == 0), stop=(dt == DT - 1))
            ctx.inc(ins, "pe")
        n_stmm = ctx.n("pe")
        ctx.marks["a_sqln_free"] = n_stmm

        @block.vector
        def _(v):
            v.wait_ge(ctx.sem("pe"), n_stmm)
            ctx.inc(v.tensor_scalar(mean_sb2[:, pp, :], pS[0:1, :], 1.0 / 1024.0, None, ALU.mult), "dv")
            v.wait_ge(ctx.sem("dv"), ctx.n("dv"))
            ctx.inc(v.tensor_scalar(var_sb2[:, pp, :], pQ[0:1, :], 1.0 / 1024.0, 1e-5, ALU.mult, ALU.add), "dv")
            v.wait_ge(ctx.sem("dv"), ctx.n("dv"))
            ctx.inc(v.tensor_mul(tmprow2[:, pp, :], mean_sb2[:, pp, :], mean_sb2[:, pp, :]), "dv")
            v.wait_ge(ctx.sem("dv"), ctx.n("dv"))
            ctx.inc(v.tensor_sub(var_sb2[:, pp, :], var_sb2[:, pp, :], tmprow2[:, pp, :]), "dv")
        n_var = ctx.n("dv")

        @block.scalar
        def _(sc):
            sc.wait_ge(ctx.sem("dv"), n_var)
            ctx.inc(sc.activation(var_sb2[:, pp, :], var_sb2[:, pp, :], AF.Ln), "ac")
            sc.wait_ge(ctx.sem("ac"), ctx.n("ac"))
            ctx.inc(sc.activation(var_sb2[:, pp, :], var_sb2[:, pp, :], AF.Exp, scale=-0.5), "ac")
        n_inv = ctx.n("ac")

        @block.tensor
        def _(t):
            t.wait_ge(ctx.sem("ac"), n_inv)
            t.matmul(pS[:], ones1[:], mean_sb2[:, pp, :], start=True, stop=True)
            ins = t.matmul(pQ[:], ones1[:], var_sb2[:, pp, :], start=True, stop=True)
            ctx.inc(ins, "pe")
        n_bc = ctx.n("pe")

        @block.vector
        def _(v):
            v.wait_ge(ctx.sem("pe"), n_bc)
            ctx.inc(v.tensor_copy(mb_sb2[:, pp, :], pS[:]), "dv")
            ctx.inc(v.tensor_copy(invb_sb2[:, pp, :], pQ[:]), "dv")
        n_bcc = ctx.n("dv")
        ctx.marks[("a_psS_rd", pp)] = n_bcc

        dv_base = ctx.n("dv")

        @block.gpsimd
        def _(g):
            g.wait_ge(ctx.sem("dv"), n_bcc)
            for dt in range(DT):
                if dt >= 2:
                    g.wait_ge(ctx.sem("dv"), dv_base + 2 * (dt - 2) + 2)
                ctx.inc(g.tensor_sub(tmp_ln[:, dt % 2, :], xs[:, dt, :], mb_sb2[:, pp, :]), "pl")
                ctx.marks[("a_lnsub", dt)] = ctx.n("pl")

        @block.vector
        def _(v):
            for dt in range(DT):
                v.wait_ge(ctx.sem("pl"), ctx.marks[("a_lnsub", dt)])
                ctx.inc(v.scalar_tensor_tensor(tmp_ln[:, dt % 2, :], tmp_ln[:, dt % 2, :], lng[:, dt:dt + 1], invb_sb2[:, pp, :], ALU.mult, ALU.mult), "dv")
                v.wait_ge(ctx.sem("dv"), ctx.n("dv"))
                ctx.inc(v.tensor_scalar(ln1_bf2[:, pp, dt, :], tmp_ln[:, dt % 2, :], lnb[:, dt:dt + 1], None, ALU.add), "dv")
        ctx.marks[("a_ln1_done", c)] = ctx.n("dv")

    def emit_xproj(c):
        pp = c % 2
        ln1_bf = ln1_bf2[:, pp]
        ac_base_x = ctx.n("ac")
        dv_base_x = ctx.n("dv")
        PSX = [psA[0], psA[1], psC[0], psC[1]]
        per_x = 7 + (3 if c == 0 else 0)

        def dv_after_conv(ff):
            return dv_base_x + min(ff + 1, 16) * per_x

        for f in range(32):
            par = f % 4
            slot = f % NSB
            wsem = f"wqa{slot}"

            @block.gpsimd
            def _(g):
                g.wait_ge(ctx.sem("pe"), slotA[slot])
                ctx.inc(g.dma_start(out=ws_in[:, slot, :, :], in_=w_in_r[:, :, f * 128:(f + 1) * 128]), wsem, 16)
            n_wq = ctx.n(wsem)

            @block.tensor
            def _(t):
                t.wait_ge(ctx.sem(wsem), n_wq)
                if f == 0:
                    t.wait_ge(ctx.sem("dv"), ctx.marks[("a_ln1_done", c)])
                if f >= 4:
                    t.wait_ge(ctx.sem("ac"), ac_base_x + (f - 4) + 1)
                else:
                    sem_name, val = ctx.marks.get(("psfree", id(PSX[par])), ("pe", 0))
                    t.wait_ge(ctx.sem(sem_name), val)
                for dt in range(DT):
                    ins = t.matmul(PSX[par][:], ws_in[:, slot, dt, :], ln1_bf[:, dt, :], start=(dt == 0), stop=(dt == DT - 1))
                ctx.inc(ins, "pe")
            ctx.marks[("a_xmm", f)] = ctx.n("pe")
            slotA[slot] = ctx.n("pe")

            if f < 16:
                @block.vector
                def _(v):
                    v.wait_ge(ctx.sem("pe"), ctx.marks[("a_xmm", f)])
                    if f >= 2:
                        v.wait_ge(ctx.sem("ac"), ac_base_x + (f - 2) + 1)
                    if f < 2:
                        v.wait_ge(ctx.sem("pe"), ctx.marks.get("a_postmm", 0))
                    ctx.inc(v.tensor_copy(xp_ext[:, f % 2, 3:CH + 3], PSX[par][:]), "dv")
                    v.wait_ge(ctx.sem("dv"), ctx.n("dv"))
                    ctx.inc(v.tensor_copy(xp_ext[:, f % 2, 0:3], carry_sb[:, f, :]), "dv")
                    v.wait_ge(ctx.sem("dv"), ctx.n("dv"))
                    ctx.inc(v.tensor_copy(carry_sb[:, f, :], xp_ext[:, f % 2, CH:CH + 3]), "dv")
                    v.wait_ge(ctx.sem("dv"), ctx.n("dv"))
                    ctx.inc(v.tensor_scalar(acc_sb[:, f % 2, :], xp_ext[:, f % 2, 3:CH + 3], convw[:, f, 3:4], None, ALU.mult), "dv")
                    for k, sl in [(2, (2, CH + 2)), (1, (1, CH + 1)), (0, (0, CH))]:
                        v.wait_ge(ctx.sem("dv"), ctx.n("dv"))
                        ctx.inc(v.scalar_tensor_tensor(acc_sb[:, f % 2, :], xp_ext[:, f % 2, sl[0]:sl[1]], convw[:, f, k:k + 1], acc_sb[:, f % 2, :], ALU.mult, ALU.add), "dv")
                    if c == 0:
                        for k in range(3):
                            v.wait_ge(ctx.sem("dv"), ctx.n("dv"))
                            ctx.inc(v.tensor_add(acc_sb[:, f % 2, k:k + 1], acc_sb[:, f % 2, k:k + 1], corr[:, f, k:k + 1]), "dv")
                ctx.marks[("psfree", id(PSX[par]))] = ("dv", dv_base_x + f * per_x + 1)

        @block.scalar
        def _(sc):
            for f in range(16):
                sc.wait_ge(ctx.sem("dv"), dv_after_conv(f))
                ctx.inc(sc.activation(xc_bf[:, f, :], acc_sb[:, f % 2, :], AF.Silu, bias=beta[:, f:f + 1]), "ac")
            for f in range(16, 32):
                sc.wait_ge(ctx.sem("pe"), ctx.marks[("a_xmm", f)])
                ctx.inc(sc.activation(sg_bf[:, f - 16, :], PSX[f % 4][:], AF.Sigmoid, bias=bing[:, f - 16:f - 16 + 1]), "ac")
                ctx.marks[("psfree", id(PSX[f % 4]))] = ("ac", ctx.n("ac"))
        return ctx.n("ac")

    def emit_uphase(c, n_xproj_ac):
        pp = c % 2
        PSU_C = [psA[0], psA[1], psS4[2 * pp]]
        PSU_D = [psC[0], psC[1], psS4[2 * pp + 1]]
        dv_base_u = ctx.n("dv")
        for g_i in range(ET):
            par = g_i % 3
            slot = g_i % 2
            wsc = f"wqc{slot}"
            wsd = f"wqd{slot}"

            @block.gpsimd
            def _(g):
                g.wait_ge(ctx.sem("pe"), slotCD[slot])
                ctx.inc(g.dma_start(out=ws_c[:, slot, :, :], in_=w_c_r[:, :, g_i * 128:(g_i + 1) * 128]), wsc, 16)
                ctx.inc(g.dma_start(out=ws_d[:, slot, :, :], in_=w_d_r[:, :, g_i * 128:(g_i + 1) * 128]), wsd, 16)
            n_wq = ctx.n(wsc)
            n_wqd = ctx.n(wsd)

            @block.tensor
            def _(t):
                t.wait_ge(ctx.sem(wsc), n_wq)
                t.wait_ge(ctx.sem(wsd), n_wqd)
                if g_i == 0:
                    t.wait_ge(ctx.sem("ac"), n_xproj_ac)
                if g_i < 2:
                    sem_name, val = ctx.marks.get(("psfree", id(PSU_C[par])), ("pe", 0))
                    t.wait_ge(ctx.sem(sem_name), val)
                    sem_name, val = ctx.marks.get(("psfree", id(PSU_D[par])), ("pe", 0))
                    t.wait_ge(ctx.sem(sem_name), val)
                elif g_i == 2:
                    t.wait_ge(ctx.sem("dv"), ctx.marks.get(("a_psS_rd", pp), 0))
                else:
                    t.wait_ge(ctx.sem("ac"), ctx.marks.get(("a_usd", g_i - 3), 0))
                    t.wait_ge(ctx.sem("dv"), dv_base_u + 4 * (g_i - 3) + 1)
                for e in range(ET):
                    ins = t.matmul(PSU_C[par][:], ws_c[:, slot, e, :], xc_bf[:, e, :], start=(e == 0), stop=(e == ET - 1))
                ctx.inc(ins, "pe")
                for e in range(ET):
                    ins = t.matmul(PSU_D[par][:], ws_d[:, slot, e, :], xc_bf[:, e, :], start=(e == 0), stop=(e == ET - 1))
                ctx.inc(ins, "pe")
            n_mm = ctx.n("pe")
            slotCD[slot] = n_mm

            @block.scalar
            def _(sc):
                sc.wait_ge(ctx.sem("pe"), n_mm)
                if g_i >= 2:
                    sc.wait_ge(ctx.sem("dv"), dv_base_u + 4 * (g_i - 2) + 1)
                ctx.inc(sc.activation(sd_sb[:, g_i % 2, :], PSU_D[par][:], AF.Sigmoid, bias=bd_fm[:, g_i:g_i + 1]), "ac")
            n_sd = ctx.n("ac")
            ctx.marks[("a_usd", g_i)] = n_sd

            @block.vector
            def _(v):
                v.wait_ge(ctx.sem("ac"), n_sd)
                ctx.inc(v.scalar_tensor_tensor(u_sb[:, g_i % 2, :], PSU_C[par][:], bc_fm[:, g_i:g_i + 1], sd_sb[:, g_i % 2, :], ALU.add, ALU.mult), "dv")
                v.wait_ge(ctx.sem("dv"), ctx.n("dv"))
                ctx.inc(v.tensor_tensor_scan(v_sb[:, g_i % 2, :], a_fm[:, g_i:g_i + 1].broadcast_to([128, CH]), u_sb[:, g_i % 2, :], vcarry[:, g_i:g_i + 1], ALU.mult, ALU.add), "dv")
                v.wait_ge(ctx.sem("dv"), ctx.n("dv"))
                ctx.inc(v.tensor_copy(vcarry[:, g_i:g_i + 1], v_sb[:, g_i % 2, CH - 1:CH]), "dv")
                ctx.inc(v.scalar_tensor_tensor(gated_bf[:, g_i, :], v_sb[:, g_i % 2, :], cfm[:, g_i:g_i + 1], sg_bf[:, g_i, :], ALU.mult, ALU.mult), "dv")
        return ctx.n("dv")

    def emit_outpost(c, n_gated):
        pp = c % 2
        s0 = c * CH
        xs = x_sb2[:, pp]
        pS, pQ = psS4[2 * pp], psS4[2 * pp + 1]

        @block.gpsimd
        def _(g):
            g.wait_ge(ctx.sem("dv"), n_gated)
            for dt in range(DT):
                ctx.inc(g.tensor_scalar_mul(xs[:, dt, :], xs[:, dt, :], 2.0), "pl")
        n_dbl = ctx.n("pl")

        for dt in range(DT):
            par = dt % 2

            @block.tensor
            def _(t):
                if dt == 0:
                    t.wait_ge(ctx.sem("dv"), n_gated)
                    if c == 0:
                        t.wait_ge(ctx.sem("wqo0"), n_wout_ld)
                if dt >= 2:
                    t.wait_ge(ctx.sem("dv"), ctx.marks[("a_resid", dt - 2)])
                else:
                    sem_name, val = ctx.marks.get(("psfree", id(psA[par])), ("pe", 0))
                    t.wait_ge(ctx.sem(sem_name), val)
                for g_i in range(ET):
                    ins = t.matmul(psA[par][:], w_out_sb[:, g_i, dt * 128:(dt + 1) * 128], gated_bf[:, g_i, :], start=(g_i == 0), stop=(g_i == ET - 1))
                ctx.inc(ins, "pe")
            n_mm = ctx.n("pe")

            @block.vector
            def _(v):
                v.wait_ge(ctx.sem("pe"), n_mm)
                v.wait_ge(ctx.sem("pl"), n_dbl)
                ctx.inc(v.scalar_tensor_tensor(xs[:, dt, :], psA[par][:], bout_fm[:, dt:dt + 1], xs[:, dt, :], ALU.add, ALU.add), "dv")
            ctx.marks[("a_resid", dt)] = ctx.n("dv")
            ctx.marks[("psfree", id(psA[par]))] = ("dv", ctx.n("dv"))
        n_resid = ctx.n("dv")

        pl_b = ctx.n("pl")

        @block.gpsimd
        def _(g):
            g.wait_ge(ctx.sem("dv"), n_resid)
            ctx.wait(g, "ac")
            for dt in range(2):
                ctx.inc(g.tensor_mul(sq[:, dt, :], xs[:, dt, :], xs[:, dt, :]), "pl")
        for dt in range(DT):
            n_sqrdy = pl_b + 2 if dt < 2 else ctx.n("pl")

            @block.tensor
            def _(t):
                t.wait_ge(ctx.sem("pl"), n_sqrdy)
                if dt == 0:
                    t.wait_ge(ctx.sem("dv"), ctx.marks.get(("a_psS_rd", pp), 0))
                t.matmul(pS[0:1, :], ones128[:], xs[:, dt, :], start=(dt == 0), stop=(dt == DT - 1))
                ins = t.matmul(pQ[0:1, :], ones128[:], sq[:, dt % 2, :], start=(dt == 0), stop=(dt == DT - 1))
                ctx.inc(ins, "pe")
            n_pe_dt = ctx.n("pe")
            if dt + 2 < DT:
                @block.gpsimd
                def _(g):
                    g.wait_ge(ctx.sem("pe"), n_pe_dt)
                    ctx.inc(g.tensor_mul(sq[:, dt % 2, :], xs[:, dt + 2, :], xs[:, dt + 2, :]), "pl")
        n_stmm = ctx.n("pe")
        ctx.marks["a_postmm"] = n_stmm

        @block.vector
        def _(v):
            v.wait_ge(ctx.sem("pe"), n_stmm)
            ctx.inc(v.tensor_scalar(mean_sb2[:, pp, :], pS[0:1, :], 1.0 / 1024.0, None, ALU.mult), "dv")
            v.wait_ge(ctx.sem("dv"), ctx.n("dv"))
            ctx.inc(v.tensor_scalar(var_sb2[:, pp, :], pQ[0:1, :], 1.0 / 1024.0, 1e-5, ALU.mult, ALU.add), "dv")
            v.wait_ge(ctx.sem("dv"), ctx.n("dv"))
            ctx.inc(v.tensor_mul(tmprow2[:, pp, :], mean_sb2[:, pp, :], mean_sb2[:, pp, :]), "dv")
            v.wait_ge(ctx.sem("dv"), ctx.n("dv"))
            ctx.inc(v.tensor_sub(var_sb2[:, pp, :], var_sb2[:, pp, :], tmprow2[:, pp, :]), "dv")
        n_var = ctx.n("dv")

        @block.scalar
        def _(sc):
            sc.wait_ge(ctx.sem("dv"), n_var)
            ctx.inc(sc.activation(var_sb2[:, pp, :], var_sb2[:, pp, :], AF.Ln), "ac")
            sc.wait_ge(ctx.sem("ac"), ctx.n("ac"))
            ctx.inc(sc.activation(var_sb2[:, pp, :], var_sb2[:, pp, :], AF.Exp, scale=-0.5), "ac")
        n_inv = ctx.n("ac")

        @block.tensor
        def _(t):
            t.wait_ge(ctx.sem("ac"), n_inv)
            t.matmul(pS[:], ones1[:], mean_sb2[:, pp, :], start=True, stop=True)
            ins = t.matmul(pQ[:], ones1[:], var_sb2[:, pp, :], start=True, stop=True)
            ctx.inc(ins, "pe")
        n_bc = ctx.n("pe")

        @block.vector
        def _(v):
            v.wait_ge(ctx.sem("pe"), n_bc)
            ctx.inc(v.tensor_copy(mb_sb2[:, pp, :], pS[:]), "dv")
            ctx.inc(v.tensor_copy(invb_sb2[:, pp, :], pQ[:]), "dv")
        n_bcc = ctx.n("dv")
        ctx.marks[("a_psS_rd", pp)] = n_bcc

        dv_base = ctx.n("dv")

        @block.gpsimd
        def _(g):
            g.wait_ge(ctx.sem("dv"), n_bcc)
            for dt in range(DT):
                if dt >= 2:
                    g.wait_ge(ctx.sem("dv"), dv_base + (dt - 2) + 1)
                ctx.inc(g.tensor_sub(tmp_po[:, dt % 2, :], xs[:, dt, :], mb_sb2[:, pp, :]), "pl")
                ctx.marks[("a_posub", dt)] = ctx.n("pl")

        @block.vector
        def _(v):
            for dt in range(DT):
                v.wait_ge(ctx.sem("pl"), ctx.marks[("a_posub", dt)])
                ctx.inc(v.tensor_mul(xs[:, dt, :], tmp_po[:, dt % 2, :], invb_sb2[:, pp, :]), "dv")
        n_post = ctx.n("dv")

        @block.sync
        def _(s):
            s.wait_ge(ctx.sem("dv"), n_post)
            s.wait_ge(ctx.sem("st2"), ctx.n("st2"))
            ctx.inc(s.dma_start(out=xout_r[:, :, s0:s0 + CH], in_=xs[:]), "st2", 16)
        ctx.marks[("a_st2", c)] = ctx.n("st2")

    emit_lnphase(0)
    for c in range(NCH):
        n_xp = emit_xproj(c)
        if c + 1 < NCH:
            emit_lnphase(c + 1)
        n_g = emit_uphase(c, n_xp)
        emit_outpost(c, n_g)

    return ctx.n("st2")


def emit_stats(ctx, block, src_sb, sq, psSum, psSq, ones128, n_ld_wait, DT, extra_pe_wait):
    nc = ctx.nc
    for dt in range(DT):
        @block.scalar
        def _(sc):
            if dt == 0:
                if n_ld_wait is not None:
                    sc.wait_ge(ctx.sem("ld"), n_ld_wait)
                if extra_pe_wait is not None:
                    sc.wait_ge(ctx.sem(extra_pe_wait[0]), extra_pe_wait[1])
            if dt >= 2:
                ctx.wait(sc, "pe")
            ctx.inc(sc.square(sq[:, dt % 2, :], src_sb[:, dt, :]), "ac")
        n_sq = ctx.n("ac")

        @block.tensor
        def _(t):
            t.wait_ge(ctx.sem("ac"), n_sq)
            if dt == 0:
                ctx.wait(t, "dv")
            t.matmul(psSum[0:1, :], ones128[:], src_sb[:, dt, :], start=(dt == 0), stop=(dt == DT - 1))
            ins = t.matmul(psSq[0:1, :], ones128[:], sq[:, dt % 2, :], start=(dt == 0), stop=(dt == DT - 1))
            ctx.inc(ins, "pe")


def emit_stats_final(ctx, block, psSum, psSq, mean_sb, var_sb, tmprow, denom, eps):
    n_pe = ctx.n("pe")

    @block.vector
    def _(v):
        v.wait_ge(ctx.sem("pe"), n_pe)
        ctx.inc(v.tensor_scalar(mean_sb[:], psSum[0:1, :], 1.0 / denom, None, ALU.mult), "dv")
        v.wait_ge(ctx.sem("dv"), ctx.n("dv"))
        ctx.inc(v.tensor_scalar(var_sb[:], psSq[0:1, :], 1.0 / denom, eps, ALU.mult, ALU.add), "dv")
        v.wait_ge(ctx.sem("dv"), ctx.n("dv"))
        ctx.inc(v.tensor_mul(tmprow[:], mean_sb[:], mean_sb[:]), "dv")
        v.wait_ge(ctx.sem("dv"), ctx.n("dv"))
        ctx.inc(v.tensor_sub(var_sb[:], var_sb[:], tmprow[:]), "dv")
    n_var = ctx.n("dv")

    @block.scalar
    def _(sc):
        sc.wait_ge(ctx.sem("dv"), n_var)
        ctx.inc(sc.activation(var_sb[:], var_sb[:], AF.Ln), "ac")
        sc.wait_ge(ctx.sem("ac"), ctx.n("ac"))
        ctx.inc(sc.activation(tmprow[:], var_sb[:], AF.Exp, scale=-0.5), "ac")
    n_std = ctx.n("ac")


def emit_bcast2(ctx, block, mean_sb, inv_sb, mb_sb, invb_sb, psB1, psB2, ones1):
    n_inv = ctx.n("dv")
    n_inv_ac = ctx.n("ac")

    @block.tensor
    def _(t):
        t.wait_ge(ctx.sem("dv"), n_inv)
        t.wait_ge(ctx.sem("ac"), n_inv_ac)
        t.matmul(psB1[:], ones1[:], mean_sb[:], start=True, stop=True)
        ins = t.matmul(psB2[:], ones1[:], inv_sb[:], start=True, stop=True)
        ctx.inc(ins, "pe")
    n_bc = ctx.n("pe")

    @block.vector
    def _(v):
        v.wait_ge(ctx.sem("pe"), n_bc)
        ctx.inc(v.tensor_copy(mb_sb[:], psB1[:]), "dv")
        ctx.inc(v.tensor_copy(invb_sb[:], psB2[:]), "dv")



def blockA_host_inputs(p):
    def fm(vec, ntiles):
        return np.ascontiguousarray(vec.reshape(ntiles, 128).T)

    def fm2(mat, ntiles, k):
        return np.ascontiguousarray(mat.reshape(ntiles, 128, k).transpose(1, 0, 2))

    out = {
        "a_w_inT": np.ascontiguousarray(p["W_in"].T),
        "a_w_cT": np.ascontiguousarray(p["W_c"].T),
        "a_w_dT": np.ascontiguousarray(p["W_d"].T),
        "a_w_outT": np.ascontiguousarray(p["W_out"].T),
        "a_lng_fm": fm(p["ln_g"], 8),
        "a_lnb_fm": fm(p["ln_b"], 8),
        "a_binxp_fm": fm(p["b_in"][:2048], 16),
        "a_bing_fm": fm(p["b_in"][2048:], 16),
        "a_convw_fm": fm2(p["conv_w"][:, 0, :], 16, 4),
        "a_convb_fm": fm(p["conv_b"], 16),
        "a_A_fm": fm2(p["A"], 16, 16),
        "a_Bm_fm": fm2(p["Bm"], 16, 16),
        "a_ld_fm": fm(p["log_delta"], 16),
        "a_bc_fm": fm(p["b_c"], 16),
        "a_bd_fm": fm(p["b_d"], 16),
        "a_bout_fm": fm(p["b_out"], 8),
    }
    return out


def declare_blockA_drams(nc, T):
    F = mybir.dt.float32
    shapes = {
        "a_w_inT": [1024, 4096], "a_w_cT": [2048, 2048], "a_w_dT": [2048, 2048],
        "a_w_outT": [2048, 1024],
        "a_lng_fm": [128, 8], "a_lnb_fm": [128, 8],
        "a_binxp_fm": [128, 16], "a_bing_fm": [128, 16],
        "a_convw_fm": [128, 16, 4], "a_convb_fm": [128, 16],
        "a_A_fm": [128, 16, 16], "a_Bm_fm": [128, 16, 16],
        "a_ld_fm": [128, 16], "a_bc_fm": [128, 16], "a_bd_fm": [128, 16],
        "a_bout_fm": [128, 8],
    }
    return {k: nc.dram_tensor(k, v, F, kind="ExternalInput").ap() for k, v in shapes.items()}




def emit_barrier(ctx, block, dep_sem, dep_val):
    s = ctx.sem(dep_sem)

    @block.sync
    def _(e):
        e.wait_ge(s, dep_val)

    @block.gpsimd
    def _(e):
        e.wait_ge(s, dep_val)

    @block.tensor
    def _(e):
        e.wait_ge(s, dep_val)

    @block.vector
    def _(e):
        e.wait_ge(s, dep_val)

    @block.scalar
    def _(e):
        e.wait_ge(s, dep_val)




def emit_postln_pipelined(ctx, block, T, xin_r, xout_r, acc_r, n_acc_done, pref,
                          xb, tb, sq2, tmp2, mb2, ivb2, mrow2, vrow2, trow2,
                          psS, ones128, ones1):
    NCH = T // CH
    DT = 8

    for c in range(NCH):
        s0 = c * CH
        pp = c % 2
        psSum, psSq = psS[2 * pp], psS[2 * pp + 1]

        @block.sync
        def _(s):
            s.wait_ge(ctx.sem("ld"), ctx.n("ld"))
            s.wait_ge(ctx.sem("st"), n_acc_done)
            if c >= 2:
                s.wait_ge(ctx.sem("st2"), ctx.marks[(pref, "st2", c - 2)])
            ctx.inc(s.dma_start(out=xb[pp][:], in_=xin_r[:, :, s0:s0 + CH]), "ld", 16)
        n_x1 = ctx.n("ld")

        @block.gpsimd
        def _(g):
            g.wait_ge(ctx.sem("wq"), ctx.n("wq"))
            g.wait_ge(ctx.sem("st"), n_acc_done)
            if c >= 2:
                g.wait_ge(ctx.sem("dv"), ctx.marks[(pref, "resid", c - 2)])
            ctx.inc(g.dma_start(out=tb[pp][:], in_=acc_r[:, :, s0:s0 + CH]), "wq", 16)
        n_accld = ctx.n("wq")

        @block.vector
        def _(v):
            v.wait_ge(ctx.sem("ld"), n_x1)
            v.wait_ge(ctx.sem("wq"), n_accld)
            if c >= 2:
                v.wait_ge(ctx.sem("pe"), ctx.marks[(pref, "stmm", c - 2)])
            for dt in range(DT):
                ctx.inc(v.tensor_add(xb[pp][:, dt, :], xb[pp][:, dt, :], tb[pp][:, dt, :]), "dv")
        n_resid = ctx.n("dv")
        ctx.marks[(pref, "resid", c)] = n_resid

        pl_b = ctx.n("pl")

        @block.gpsimd
        def _(g):
            g.wait_ge(ctx.sem("dv"), n_resid)
            for dt in range(2):
                ctx.inc(g.tensor_mul(sq2[pp][:, dt, :], xb[pp][:, dt, :], xb[pp][:, dt, :]), "pl")
        for dt in range(DT):
            n_sqrdy = pl_b + 2 if dt < 2 else ctx.n("pl")

            @block.tensor
            def _(t):
                t.wait_ge(ctx.sem("pl"), n_sqrdy)
                if dt == 0:
                    t.wait_ge(ctx.sem("dv"), ctx.marks.get((pref, "statrd", c - 2), 0) if c >= 2 else 0)
                t.matmul(psSum[0:1, :], ones128[:], xb[pp][:, dt, :], start=(dt == 0), stop=(dt == DT - 1))
                ins = t.matmul(psSq[0:1, :], ones128[:], sq2[pp][:, dt % 2, :], start=(dt == 0), stop=(dt == DT - 1))
                ctx.inc(ins, "pe")
            n_pe_dt = ctx.n("pe")
            if dt + 2 < DT:
                @block.gpsimd
                def _(g):
                    g.wait_ge(ctx.sem("pe"), n_pe_dt)
                    ctx.inc(g.tensor_mul(sq2[pp][:, dt % 2, :], xb[pp][:, dt + 2, :], xb[pp][:, dt + 2, :]), "pl")
        n_stmm = ctx.n("pe")
        ctx.marks[(pref, "stmm", c)] = n_stmm

        @block.vector
        def _(v):
            v.wait_ge(ctx.sem("pe"), n_stmm)
            ctx.inc(v.tensor_scalar(mrow2[pp][:], psSum[0:1, :], 1.0 / 1024.0, None, ALU.mult), "dv")
            v.wait_ge(ctx.sem("dv"), ctx.n("dv"))
            ctx.inc(v.tensor_scalar(vrow2[pp][:], psSq[0:1, :], 1.0 / 1024.0, 1e-5, ALU.mult, ALU.add), "dv")
            v.wait_ge(ctx.sem("dv"), ctx.n("dv"))
            ctx.inc(v.tensor_mul(trow2[pp][:], mrow2[pp][:], mrow2[pp][:]), "dv")
            v.wait_ge(ctx.sem("dv"), ctx.n("dv"))
            ctx.inc(v.tensor_sub(vrow2[pp][:], vrow2[pp][:], trow2[pp][:]), "dv")
        n_var = ctx.n("dv")
        ctx.marks[(pref, "statrd", c)] = n_var

        @block.scalar
        def _(sc):
            sc.wait_ge(ctx.sem("dv"), n_var)
            ctx.inc(sc.activation(vrow2[pp][:], vrow2[pp][:], AF.Ln), "ac")
            sc.wait_ge(ctx.sem("ac"), ctx.n("ac"))
            ctx.inc(sc.activation(vrow2[pp][:], vrow2[pp][:], AF.Exp, scale=-0.5), "ac")
        n_inv = ctx.n("ac")

        @block.tensor
        def _(t):
            t.wait_ge(ctx.sem("ac"), n_inv)
            if c >= 2:
                t.wait_ge(ctx.sem("dv"), ctx.marks[(pref, "bcrd", c - 2)])
            t.matmul(psSum[:], ones1[:], mrow2[pp][:], start=True, stop=True)
            ins = t.matmul(psSq[:], ones1[:], vrow2[pp][:], start=True, stop=True)
            ctx.inc(ins, "pe")
        n_bc = ctx.n("pe")

        @block.vector
        def _(v):
            v.wait_ge(ctx.sem("pe"), n_bc)
            ctx.inc(v.tensor_copy(mb2[pp][:], psSum[:]), "dv")
            ctx.inc(v.tensor_copy(ivb2[pp][:], psSq[:]), "dv")
        n_bcc = ctx.n("dv")
        ctx.marks[(pref, "bcrd", c)] = n_bcc

        dv_b = ctx.n("dv")

        @block.gpsimd
        def _(g):
            g.wait_ge(ctx.sem("dv"), n_bcc)
            for dt in range(DT):
                if dt >= 2:
                    g.wait_ge(ctx.sem("dv"), dv_b + (dt - 2) + 1)
                ctx.inc(g.tensor_sub(tmp2[pp][:, dt % 2, :], xb[pp][:, dt, :], mb2[pp][:]), "pl")
                ctx.marks[(pref, "plsub", dt)] = ctx.n("pl")

        @block.vector
        def _(v):
            for dt in range(DT):
                v.wait_ge(ctx.sem("pl"), ctx.marks[(pref, "plsub", dt)])
                ctx.inc(v.tensor_mul(xb[pp][:, dt, :], tmp2[pp][:, dt % 2, :], ivb2[pp][:]), "dv")
        n_post = ctx.n("dv")

        @block.sync
        def _(s):
            s.wait_ge(ctx.sem("dv"), n_post)
            s.wait_ge(ctx.sem("st2"), ctx.n("st2"))
            ctx.inc(s.dma_start(out=xout_r[:, :, s0:s0 + CH], in_=xb[pp][:]), "st2", 16)
        ctx.marks[(pref, "st2", c)] = ctx.n("st2")

    return ctx.n("st2")



def emit_blockB(ctx, block, T, xin, xout, W, acc_hbm):
    nc = ctx.nc
    NCH = T // CH
    DT = 8
    FT = 16
    NE = 8

    xin_r = xin.rearrange("(t p) s -> p t s", p=128)
    xout_r = xout.rearrange("(t p) s -> p t s", p=128)
    acc_r = acc_hbm.rearrange("(t p) s -> p t s", p=128)
    gw_r = W["b_gwT"].rearrange("(t p) e -> p t e", p=128)
    w1_r = W["b_w1T"].rearrange("E (t p) f -> E p t f", p=128)
    w3_r = W["b_w3T"].rearrange("E (t p) f -> E p t f", p=128)
    w2_r = W["b_w2T"].rearrange("E (t p) f -> E p t f", p=128)

    xf_bf = ctx.sbuf("b_xf", [128, DT, T], BF16)
    gwt = ctx.sbuf("b_gwtsb", [128, DT, 8], F32)
    ident = ctx.sbuf("b_identsb", [128, 128], F32)
    sel = ctx.sbuf("b_selsb", [8, NE * 128], BF16)
    e_sb = ctx.sbuf("b_e", [128, 8], F32)
    nm1 = ctx.sbuf("b_nm1", [128, 1], F32)
    ssum = ctx.sbuf("b_ssum", [128, 1], F32)
    rsum = ctx.sbuf("b_rsum", [128, 1], F32)
    probs = ctx.sbuf("b_probs", [128, 8], F32)
    m1p = ctx.sbuf("b_m1p", [128, 1], F32)
    oh0 = ctx.sbuf("b_oh0", [128, 8], F32)
    tnot = ctx.sbuf("b_tnot", [128, 8], F32)
    probs2 = ctx.sbuf("b_probs2", [128, 8], F32)
    m2p = ctx.sbuf("b_m2p", [128, 1], F32)
    oh1 = ctx.sbuf("b_oh1", [128, 8], F32)
    den = ctx.sbuf("b_den", [128, 1], F32)
    rden = ctx.sbuf("b_rden", [128, 1], F32)
    w0c = ctx.sbuf("b_w0c", [128, 1], F32)
    w1c = ctx.sbuf("b_w1c", [128, 1], F32)
    comb = ctx.sbuf("b_comb", [128, 8], F32)
    combT = ctx.sbuf("b_combT", [8, T], BF16)
    w1s = ctx.sbuf("b_w1s", [128, DT, 2048], BF16)
    w3s = ctx.sbuf("b_w3s", [128, DT, 2048], BF16)
    w2s = ctx.sbuf("b_w2s", [128, FT, 1024], BF16)
    s1_sb = ctx.sbuf("b_s1", [128, 2, CH], BF16)
    h_bf = ctx.sbuf("b_h", [128, FT, CH], BF16)
    x1_sb = h_bf[:].bitcast(F32).rearrange("p a b -> p (a b)").rearrange("p (t s) -> p t s", t=DT)
    cb_sb = ctx.sbuf("b_cb", [128, CH], F32)
    t_sb = ctx.sbuf("b_t", [128, DT, CH], F32)
    sq = w3s[:, 0, :].bitcast(F32).rearrange("p (a b) -> p a b", b=CH)
    tmp1 = w3s[:, 1, :].bitcast(F32).rearrange("p (a b) -> p a b", b=CH)
    mb_sb = w1s[:, 0, 0:1024].bitcast(F32)
    invb_sb = w1s[:, 1, 0:1024].bitcast(F32)
    mean_sb = w2s[0:1, 0, :].bitcast(F32)[:, 0:CH]
    var_sb = w2s[0:1, 1, :].bitcast(F32)[:, 0:CH]
    tmprow = w2s[0:1, 2, :].bitcast(F32)[:, 0:CH]
    ones128 = ctx.sbuf("b_ones128", [128, 1], F32)
    ones1 = ctx.sbuf("b_ones1", [1, 128], F32)

    psA = [ctx.psum("b_psA0", [128, CH]), ctx.psum("b_psA1", [128, CH])]
    psC = [ctx.psum("b_psC0", [128, CH]), ctx.psum("b_psC1", [128, CH])]
    psO = [ctx.psum("b_psO0", [128, CH]), ctx.psum("b_psO1", [128, CH])]
    psB = ctx.psum("b_psB", [128, CH])
    psG = ctx.psum("b_psG", [128, 512])
    psT = psG

    @block.sync
    def _(s):
        s.wait_ge(ctx.sem("ld"), ctx.n("ld"))
        ctx.inc(s.dma_start(out=gwt[:], in_=gw_r[:]), "ld", 16)
        ctx.inc(s.dma_start(out=ident[:], in_=W["ident"]), "ld", 16)
    n_param_ld = ctx.n("ld")

    @block.gpsimd
    def _(g):
        g.wait_ge(ctx.sem("wq"), ctx.n("wq"))
        ctx.inc(g.dma_start(out=sel[:], in_=W["b_sel"]), "wq", 16)
        ctx.inc(g.memset(ones128[:], 1.0), "pl")
        ctx.inc(g.memset(ones1[:], 1.0), "pl")
    n_init_pl = ctx.n("pl")
    n_init_wq = ctx.n("wq")

    for c in range(NCH):
        s0 = c * CH

        @block.sync
        def _(s):
            s.wait_ge(ctx.sem("ld"), ctx.n("ld"))
            if c >= 1:
                ctx.wait(s, "pe")
            ctx.inc(s.dma_start(out=x1_sb[:], in_=xin_r[:, :, s0:s0 + CH]), "ld", 16)
        n_xld = ctx.n("ld")

        @block.gpsimd
        def _(g):
            g.wait_ge(ctx.sem("wq"), ctx.n("wq"))
            ctx.inc(g.dma_start(out=xf_bf[:, :, s0:s0 + CH], in_=xin_r[:, :, s0:s0 + CH]), "wq", 16)
        n_xfld = ctx.n("wq")

        for st in range(4):
            @block.tensor
            def _(t):
                t.wait_ge(ctx.sem("ld"), n_xld)
                if c == 0 and st == 0:
                    t.wait_ge(ctx.sem("pl"), n_init_pl)
                    t.wait_ge(ctx.sem("ld"), n_param_ld)
                if c >= 1 or st >= 1:
                    ctx.wait(t, "dv")
                for dt in range(DT):
                    ins = t.matmul(psG[:, st * 8:(st + 1) * 8], x1_sb[:, dt, st * 128:(st + 1) * 128], gwt[:, dt, :], start=(dt == 0), stop=(dt == DT - 1))
                ctx.inc(ins, "pe")
            n_gmm = ctx.n("pe")

            @block.vector
            def _(v):
                v.wait_ge(ctx.sem("pe"), n_gmm)
                ctx.inc(v.tensor_reduce(nm1[:].rearrange("p (a b) -> p a b", b=1), psG[:, st * 8:(st + 1) * 8].rearrange("p (a b) -> p a b", b=8), AX.X, ALU.max, negate=True), "dv")
            n_nm1 = ctx.n("dv")

            @block.scalar
            def _(sc):
                sc.wait_ge(ctx.sem("dv"), n_nm1)
                ctx.inc(sc.activation(e_sb[:], psG[:, st * 8:(st + 1) * 8], AF.Exp, bias=nm1[:, 0:1]), "ac")
            n_exp = ctx.n("ac")

            @block.vector
            def _(v):
                v.wait_ge(ctx.sem("ac"), n_exp)
                ctx.inc(v.tensor_reduce(ssum[:].rearrange("p (a b) -> p a b", b=1), e_sb[:].rearrange("p (a b) -> p a b", b=8), AX.X, ALU.add), "dv")
                v.wait_ge(ctx.sem("dv"), ctx.n("dv"))
                ctx.inc(v.reciprocal(rsum[:], ssum[:]), "dv")
                v.wait_ge(ctx.sem("dv"), ctx.n("dv"))
                ctx.inc(v.tensor_scalar(probs[:], e_sb[:], rsum[:, 0:1], None, ALU.mult), "dv")
                v.wait_ge(ctx.sem("dv"), ctx.n("dv"))
                ctx.inc(v.tensor_reduce(m1p[:].rearrange("p (a b) -> p a b", b=1), probs[:].rearrange("p (a b) -> p a b", b=8), AX.X, ALU.max), "dv")
                v.wait_ge(ctx.sem("dv"), ctx.n("dv"))
                ctx.inc(v.tensor_scalar(oh0[:], probs[:], m1p[:, 0:1], None, ALU.is_ge), "dv")
                v.wait_ge(ctx.sem("dv"), ctx.n("dv"))
                ctx.inc(v.tensor_scalar(tnot[:], oh0[:], -1.0, 1.0, ALU.mult, ALU.add), "dv")
                v.wait_ge(ctx.sem("dv"), ctx.n("dv"))
                ctx.inc(v.tensor_mul(probs2[:], probs[:], tnot[:]), "dv")
                v.wait_ge(ctx.sem("dv"), ctx.n("dv"))
                ctx.inc(v.tensor_reduce(m2p[:].rearrange("p (a b) -> p a b", b=1), probs2[:].rearrange("p (a b) -> p a b", b=8), AX.X, ALU.max), "dv")
                v.wait_ge(ctx.sem("dv"), ctx.n("dv"))
                ctx.inc(v.tensor_scalar(oh1[:], probs2[:], m2p[:, 0:1], None, ALU.is_ge), "dv")
                v.wait_ge(ctx.sem("dv"), ctx.n("dv"))
                ctx.inc(v.tensor_add(den[:], m1p[:], m2p[:]), "dv")
                v.wait_ge(ctx.sem("dv"), ctx.n("dv"))
                ctx.inc(v.reciprocal(rden[:], den[:]), "dv")
                v.wait_ge(ctx.sem("dv"), ctx.n("dv"))
                ctx.inc(v.tensor_mul(w0c[:], m1p[:], rden[:]), "dv")
                ctx.inc(v.tensor_mul(w1c[:], m2p[:], rden[:]), "dv")
                v.wait_ge(ctx.sem("dv"), ctx.n("dv"))
                ctx.inc(v.tensor_scalar(comb[:], oh0[:], w0c[:, 0:1], None, ALU.mult), "dv")
                v.wait_ge(ctx.sem("dv"), ctx.n("dv"))
                ctx.inc(v.scalar_tensor_tensor(comb[:], oh1[:], w1c[:, 0:1], comb[:], ALU.mult, ALU.add), "dv")
            n_comb = ctx.n("dv")

            @block.tensor
            def _(t):
                t.wait_ge(ctx.sem("dv"), n_comb)
                ctx.inc(t.transpose(psT[0:8, 128:256], comb[:], ident[:]), "pe")
            n_tp = ctx.n("pe")

            @block.vector
            def _(v):
                v.wait_ge(ctx.sem("pe"), n_tp)
                ctx.inc(v.tensor_copy(combT[:, s0 + st * 128:s0 + (st + 1) * 128], psT[0:8, 128:256]), "dv")
    n_gate_dv = ctx.n("dv")
    n_gate_wq = ctx.n("wq")

    w13_free = 0
    w2_free = 0
    evacA_b = [("pe", 0), ("pe", 0)]
    evacC_b = [("pe", 0), ("pe", 0)]
    w2rd = 0
    tsb_rd = 0

    wqb1_0 = ctx.n("wqb1")
    wqb3_0 = ctx.n("wqb3")
    wqb2_0 = ctx.n("wqb2")

    @block.gpsimd
    def _(g):
        ctx.inc(g.dma_start(out=w1s[:], in_=w1_r[0]), "wqb1", 16)
        ctx.inc(g.dma_start(out=w3s[:], in_=w3_r[0]), "wqb3", 16)
        ctx.inc(g.dma_start(out=w2s[:], in_=w2_r[0]), "wqb2", 16)

    for e in range(NE):
        n_w13 = wqb1_0 + 16 * (e + 1)
        n_w2l = wqb2_0 + 16 * (e + 1)

        for c in range(NCH):
            s0 = c * CH
            ac_base = ctx.n("ac")
            dv_base = ctx.n("dv")
            for f in range(FT):
                par = f % 2

                @block.tensor
                def _(t):
                    if f == 0 and c == 0:
                        t.wait_ge(ctx.sem("wqb1"), n_w13)
                        t.wait_ge(ctx.sem("wqb3"), n_w13)
                        if e == 0:
                            t.wait_ge(ctx.sem("wq"), n_gate_wq)
                    if f >= 2:
                        t.wait_ge(ctx.sem("ac"), ac_base + (f - 2) + 1)
                        t.wait_ge(ctx.sem("dv"), dv_base + (f - 2) + 1)
                    elif c > 0 or e > 0:
                        sem_name, val = evacA_b[par]
                        t.wait_ge(ctx.sem(sem_name), val)
                        sem_name, val = evacC_b[par]
                        t.wait_ge(ctx.sem(sem_name), val)
                    for dt in range(DT):
                        t.matmul(psA[par][:], w1s[:, dt, f * 128:(f + 1) * 128], xf_bf[:, dt, s0:s0 + CH], start=(dt == 0), stop=(dt == DT - 1))
                    for dt in range(DT):
                        ins = t.matmul(psC[par][:], w3s[:, dt, f * 128:(f + 1) * 128], xf_bf[:, dt, s0:s0 + CH], start=(dt == 0), stop=(dt == DT - 1))
                    ctx.inc(ins, "pe")
                ctx.marks[("b_hmm", f)] = ctx.n("pe")
            n_h_pe = ctx.n("pe")

            @block.scalar
            def _(sc):
                for f in range(FT):
                    sc.wait_ge(ctx.sem("pe"), ctx.marks[("b_hmm", f)])
                    if c > 0 or e > 0:
                        sc.wait_ge(ctx.sem("pe"), w2rd)
                    ctx.inc(sc.activation(s1_sb[:, f % 2, :], psA[f % 2][:], AF.Silu), "ac")
                    ctx.marks[("b_silu", f)] = ctx.n("ac")

            @block.vector
            def _(v):
                for f in range(FT):
                    v.wait_ge(ctx.sem("ac"), ctx.marks[("b_silu", f)])
                    ctx.inc(v.tensor_mul(h_bf[:, f, :], s1_sb[:, f % 2, :], psC[f % 2][:]), "dv")
                    if f >= FT - 2:
                        which = f % 2
                        evacC_b[which] = ("dv", ctx.n("dv"))
                        evacA_b[which] = ("ac", ctx.marks[("b_silu", f)])
            n_h = ctx.n("dv")

            if c == NCH - 1 and e < NE - 1:
                @block.gpsimd
                def _(g):
                    g.wait_ge(ctx.sem("pe"), n_h_pe)
                    ctx.inc(g.dma_start(out=w1s[:], in_=w1_r[e + 1]), "wqb1", 16)
                    ctx.inc(g.dma_start(out=w3s[:], in_=w3_r[e + 1]), "wqb3", 16)

            @block.tensor
            def _(t):
                t.wait_ge(ctx.sem("dv"), n_gate_dv)
                t.wait_ge(ctx.sem("dv"), ctx.marks.get("b_cb_rd", 0))
                ctx.inc(t.matmul(psB[:], sel[:, e * 128:(e + 1) * 128], combT[:, s0:s0 + CH], start=True, stop=True), "pe")
            n_cb = ctx.n("pe")

            @block.vector
            def _(v):
                v.wait_ge(ctx.sem("pe"), n_cb)
                ctx.inc(v.tensor_copy(cb_sb[:], psB[:]), "dv")
            n_cbc = ctx.n("dv")
            ctx.marks["b_cb_rd"] = None

            dv_scale_base = None
            for dt in range(DT):
                par = dt % 2

                @block.tensor
                def _(t):
                    if dt == 0 and c == 0:
                        t.wait_ge(ctx.sem("wqb2"), n_w2l)
                    if dt == 0:
                        t.wait_ge(ctx.sem("dv"), n_h)
                    if dt >= 2:
                        t.wait_ge(ctx.sem("dv"), ctx.marks[("b_scale", dt - 2)])
                    elif c > 0 or e > 0:
                        t.wait_ge(ctx.sem("dv"), ctx.marks[("b_scale_prev", dt)])
                    for f in range(FT):
                        ins = t.matmul(psO[par][:], w2s[:, f, dt * 128:(dt + 1) * 128], h_bf[:, f, :], start=(f == 0), stop=(f == FT - 1))
                    ctx.inc(ins, "pe")
                n_mm2 = ctx.n("pe")

                @block.vector
                def _(v):
                    v.wait_ge(ctx.sem("pe"), n_mm2)
                    v.wait_ge(ctx.sem("dv"), n_cbc)
                    if e > 0 or c > 0:
                        v.wait_ge(ctx.sem("st"), tsb_rd)
                    ctx.inc(v.tensor_mul(t_sb[:, dt, :], psO[par][:], cb_sb[:]), "dv")
                ctx.marks[("b_scale", dt)] = ctx.n("dv")
            for dt in range(DT):
                ctx.marks[("b_scale_prev", dt)] = ctx.marks[("b_scale", dt)]
            w2rd = ctx.n("pe")
            ctx.marks["b_cb_rd"] = ctx.n("dv")
            n_scale = ctx.n("dv")

            @block.gpsimd
            def _(g):
                g.wait_ge(ctx.sem("dv"), n_scale)
                g.wait_ge(ctx.sem("st"), ctx.n("st"))
                op = ALU.bypass if e == 0 else ALU.add
                ctx.inc(g.dma_start(out=acc_r[:, :, s0:s0 + CH], in_=t_sb[:], accum_op=op), "st", 16)
            tsb_rd = ctx.n("st")

        if e < NE - 1:
            @block.gpsimd
            def _(g):
                g.wait_ge(ctx.sem("pe"), w2rd)
                ctx.inc(g.dma_start(out=w2s[:], in_=w2_r[e + 1]), "wqb2", 16)
    n_acc_done = ctx.n("st")
    xb = [x1_sb, w1s[:, 0:4, :].bitcast(F32).rearrange("p a (x b) -> p (a x) b", b=CH)]
    tb = [t_sb, w1s[:, 4:8, :].bitcast(F32).rearrange("p a (x b) -> p (a x) b", b=CH)]
    sq2 = [w3s[:, 0, :].bitcast(F32).rearrange("p (a b) -> p a b", b=CH),
           w3s[:, 1, :].bitcast(F32).rearrange("p (a b) -> p a b", b=CH)]
    tmp2 = [w3s[:, 2, :].bitcast(F32).rearrange("p (a b) -> p a b", b=CH),
            w3s[:, 3, :].bitcast(F32).rearrange("p (a b) -> p a b", b=CH)]
    mb2 = [w3s[:, 4, 0:1024].bitcast(F32), w3s[:, 4, 1024:2048].bitcast(F32)]
    ivb2 = [w3s[:, 5, 0:1024].bitcast(F32), w3s[:, 5, 1024:2048].bitcast(F32)]
    mrow2 = [w3s[0:1, 6, 0:1024].bitcast(F32), w3s[0:1, 6, 1024:2048].bitcast(F32)]
    vrow2 = [w3s[0:1, 7, 0:1024].bitcast(F32), w3s[0:1, 7, 1024:2048].bitcast(F32)]
    trow2 = [w2s[0:1, 0, :].bitcast(F32), w2s[0:1, 1, :].bitcast(F32)]
    emit_postln_pipelined(ctx, block, T, xin_r, xout_r, acc_r, n_acc_done, "Bp",
                          xb, tb, sq2, tmp2, mb2, ivb2, mrow2, vrow2, trow2,
                          [psA[0], psA[1], psO[0], psO[1]], ones128, ones1)
    return ctx.n("st2")


def blockB_host_inputs(p):
    out = {
        "b_gwT": np.ascontiguousarray(p["gate_w"].T),
        "b_w1T": np.ascontiguousarray(p["w1"].transpose(0, 2, 1)),
        "b_w3T": np.ascontiguousarray(p["w3"].transpose(0, 2, 1)),
        "b_w2T": np.ascontiguousarray(p["w2"].transpose(0, 2, 1)),
        "ident": np.eye(128, dtype=np.float32),
    }
    sel = np.zeros((8, 8 * 128), np.float32)
    for e in range(8):
        sel[e, e * 128:(e + 1) * 128] = 1.0
    out["b_sel"] = sel
    return out


def declare_blockB_drams(nc, T):
    F = mybir.dt.float32
    shapes = {
        "b_gwT": [1024, 8], "b_w1T": [8, 1024, 2048], "b_w3T": [8, 1024, 2048],
        "b_w2T": [8, 2048, 1024], "ident": [128, 128], "b_sel": [8, 8 * 128],
    }
    return {k: nc.dram_tensor(k, v, F, kind="ExternalInput").ap() for k, v in shapes.items()}



def emit_blockC(ctx, block, T, xin, xout, W, kvscr, ksscr):
    nc = ctx.nc
    NCH = T // CH
    DT = 8
    NH = 16

    xin_r = xin.rearrange("(t p) s -> p t s", p=128)
    xout_r = xout.rearrange("(t p) s -> p t s", p=128)
    wq_r = W["c_wqT"].rearrange("(t p) f -> p t f", p=128)
    wk_r = W["c_wkT"].rearrange("(t p) f -> p t f", p=128)
    wv_r = W["c_wvT"].rearrange("(t p) f -> p t f", p=128)

    x_sb = ctx.sbuf("c_x", [128, DT, CH], F32)
    xbf = ctx.sbuf("c_xbf", [128, DT, CH], BF16)
    wqs = ctx.sbuf("c_wqs", [128, DT, 1024], BF16)
    wks = ctx.sbuf("c_wks", [128, DT, 1024], BF16)
    wvs = ctx.sbuf("c_wvs", [128, DT, 1024], BF16)
    q_bf = ctx.sbuf("c_q", [128, DT, T], BF16)
    sq_sb = ctx.sbuf("c_sq", [128, 2, CH], F32)
    rb_sb = ctx.sbuf("c_rb", [128, 2, CH], F32)
    t_sb = ctx.sbuf("c_t", [128, 2, CH], F32)
    nrm2 = ctx.sbuf("c_nrm2", [2, CH], F32)
    nrm_all = ctx.sbuf("c_nrmall", [2, DT, CH], F32)
    k_bf = ctx.sbuf("c_k", [128, 2, 1024], BF16)
    v_bf = ctx.sbuf("c_v", [128, 2, 1024], BF16)
    sk_sb = ctx.sbuf("c_sk", [128, 1024], F32)
    ssk = ctx.sbuf("c_ssk", [128, 16], F32)
    rk = ctx.sbuf("c_rk", [128, 16], F32)
    kv_sb = ctx.sbuf("c_kv", [64, 1024], BF16)
    kvacc = ctx.sbuf("c_kvacc", [64, 1024], F32)
    kv2_sb = ctx.sbuf("c_kv2", [128, 8, 64], BF16)
    ks_row = ctx.sbuf("c_ksrow", [1, 1024], F32)
    ksfm = ctx.sbuf("c_ksfm", [128, 8], BF16)
    ks2 = ctx.sbuf("c_ks2", [128, 8, 2], BF16)
    rn_bf_all = ctx.sbuf("c_rnbfall", [2, DT, CH], BF16)
    qn_all = ctx.sbuf("c_qnall", [128, DT, CH], BF16)
    indA = ctx.sbuf("c_indA_sb", [128, 2], F32)
    ident_c = ctx.sbuf("c_ident_sb", [128, 128], F32)
    indT2 = ctx.sbuf("c_indT2_sb", [2, 128], F32)
    indT2b = ctx.sbuf("c_indT2b_sb", [2, 128], BF16)
    maskA = ctx.sbuf("c_maskA_sb", [128, 1], F32)
    maskB = ctx.sbuf("c_maskB_sb", [128, 1], F32)
    ones128 = ctx.sbuf("c_ones128", [128, 1], F32)
    ones128b = ctx.sbuf("c_ones128b", [128, 1], BF16)
    ones1 = ctx.sbuf("c_ones1", [1, 128], F32)
    sq = wks[:, 0:2, :].bitcast(F32).rearrange("p a (x b) -> p (a x) b", b=CH)[:, 0:2, :]
    tmp1 = wks[:, 2:4, :].bitcast(F32).rearrange("p a (x b) -> p (a x) b", b=CH)[:, 0:2, :]
    mb_sb = wks[:, 4, :].bitcast(F32)[:, 0:CH]
    invb_sb = wks[:, 5, :].bitcast(F32)[:, 0:CH]
    mean_sb = wvs[0:1, 0, :].bitcast(F32)[:, 0:CH]
    var_sb = wvs[0:1, 1, :].bitcast(F32)[:, 0:CH]
    tmprow = wvs[0:1, 2, :].bitcast(F32)[:, 0:CH]

    P = [ctx.psum("c_P0", [128, CH]), ctx.psum("c_P1", [128, CH]),
         ctx.psum("c_P2", [128, CH]), ctx.psum("c_P3", [128, CH])]
    psKV = ctx.psum("c_psKV", [64, 512])
    psKS = ctx.psum("c_psKS", [1, 1024])
    psN = ctx.psum("c_psN", [2, CH])

    @block.sync
    def _(s):
        s.wait_ge(ctx.sem("ld"), ctx.n("ld"))
        for name, dst in [("c_indA", indA), ("c_indT2", indT2), ("c_ident", ident_c)]:
            ctx.inc(s.dma_start(out=dst[:], in_=W[name]), "ld", 16)
    n_param_ld = ctx.n("ld")

    @block.gpsimd
    def _(g):
        g.wait_ge(ctx.sem("wq"), ctx.n("wq"))
        ctx.inc(g.dma_start(out=wqs[:], in_=wq_r[:]), "wq", 16)
        g.wait_ge(ctx.sem("wq"), ctx.n("wq"))
        ctx.inc(g.dma_start(out=wks[:], in_=wk_r[:]), "wq", 16)
        g.wait_ge(ctx.sem("wq"), ctx.n("wq"))
        ctx.inc(g.dma_start(out=wvs[:], in_=wv_r[:]), "wq", 16)
        g.wait_ge(ctx.sem("wq"), ctx.n("wq"))
        ctx.inc(g.dma_start(out=indT2b[:], in_=W["c_indT2"]), "wq", 16)
        g.wait_ge(ctx.sem("wq"), ctx.n("wq"))
        ctx.inc(g.dma_start(out=maskA[:], in_=W["c_maskA"]), "wq", 16)
        g.wait_ge(ctx.sem("wq"), ctx.n("wq"))
        ctx.inc(g.dma_start(out=maskB[:], in_=W["c_maskB"]), "wq", 16)
        ctx.inc(g.memset(ones128[:], 1.0), "pl")
        ctx.inc(g.memset(ones128b[:], 1.0), "pl")
        ctx.inc(g.memset(ones1[:], 1.0), "pl")
    n_init_pl = ctx.n("pl")
    n_wload = ctx.n("wq")

    for c in range(NCH):
        s0 = c * CH

        @block.gpsimd
        def _(g):
            g.wait_ge(ctx.sem("wq"), ctx.n("wq"))
            if c >= 1:
                ctx.wait(g, "pe")
            ctx.inc(g.dma_start(out=xbf[:], in_=xin_r[:, :, s0:s0 + CH]), "wq", 16)
        n_xbfld = ctx.n("wq")

        dv_base_q = ctx.n("dv")
        ac_base_q = ctx.n("ac")
        def emit_qgroup(qf):
            par3 = qf % 3

            @block.tensor
            def _(t):
                t.wait_ge(ctx.sem("wq"), n_xbfld)
                if c == 0 and qf == 0:
                    t.wait_ge(ctx.sem("wq"), n_wload)
                    t.wait_ge(ctx.sem("pl"), n_init_pl)
                    t.wait_ge(ctx.sem("ld"), n_param_ld)
                if qf >= 3:
                    t.wait_ge(ctx.sem("dv"), dv_base_q + 4 * (qf - 3) + 4)
                else:
                    sem_name, val = ctx.marks.get(("c_P", par3), ("pe", 0))
                    t.wait_ge(ctx.sem(sem_name), val)
                for dt in range(DT):
                    ins = t.matmul(P[par3][:], wqs[:, dt, qf * 128:(qf + 1) * 128], xbf[:, dt, :], start=(dt == 0), stop=(dt == DT - 1))
                ctx.inc(ins, "pe")
            ctx.marks[("c_qmm", qf)] = ctx.n("pe")


        def emit_qchain(qf):
            par3 = qf % 3

            @block.scalar
            def _(sc):
                sc.wait_ge(ctx.sem("pe"), ctx.marks[("c_qmm", qf)])
                if qf >= 2:
                    sc.wait_ge(ctx.sem("pe"), ctx.marks[("c_ind", qf - 2)])
                ctx.inc(sc.square(sq_sb[:, qf % 2, :], P[par3][:]), "ac")
            n_sq = ctx.n("ac")

            @block.tensor
            def _(t):
                t.wait_ge(ctx.sem("ac"), n_sq)
                if qf >= 1:
                    t.wait_ge(ctx.sem("dv"), dv_base_q + 4 * (qf - 1) + 1)
                ctx.inc(t.matmul(psN[0:2, :], indA[:], sq_sb[:, qf % 2, :], start=True, stop=True), "pe")
            n_ind = ctx.n("pe")
            ctx.marks[("c_ind", qf)] = n_ind

            @block.vector
            def _(v):
                v.wait_ge(ctx.sem("pe"), n_ind)
                ctx.inc(v.tensor_copy(nrm_all[:, qf, :], psN[0:2, :]), "dv")
                v.wait_ge(ctx.sem("dv"), ctx.n("dv"))
                ctx.inc(v.tensor_scalar_max(nrm_all[:, qf, :], nrm_all[:, qf, :], 1e-24), "dv")
            n_mx = ctx.n("dv")

            @block.scalar
            def _(sc):
                sc.wait_ge(ctx.sem("dv"), n_mx)
                ctx.inc(sc.activation(nrm_all[:, qf, :], nrm_all[:, qf, :], AF.Ln), "ac")
                sc.wait_ge(ctx.sem("ac"), ctx.n("ac"))
                ctx.inc(sc.activation(nrm_all[:, qf, :], nrm_all[:, qf, :], AF.Exp, scale=-0.5), "ac")
            n_rsq = ctx.n("ac")

            @block.tensor
            def _(t):
                t.wait_ge(ctx.sem("ac"), n_rsq)
                if qf >= 1:
                    t.wait_ge(ctx.sem("dv"), dv_base_q + 4 * (qf - 1) + 3)
                ctx.inc(t.matmul(P[3][:], indT2[:], nrm_all[:, qf, :], start=True, stop=True), "pe")
            n_bc = ctx.n("pe")

            @block.vector
            def _(v):
                v.wait_ge(ctx.sem("pe"), n_bc)
                ctx.inc(v.tensor_copy(rb_sb[:, qf % 2, :], P[3][:]), "dv")
                v.wait_ge(ctx.sem("dv"), ctx.n("dv"))
                if qf >= 2:
                    v.wait_ge(ctx.sem("ac"), ac_base_q + 5 * (qf - 2) + 5)
                ctx.inc(v.tensor_mul(t_sb[:, qf % 2, :], P[par3][:], rb_sb[:, qf % 2, :]), "dv")
            n_ql2 = ctx.n("dv")
            ctx.marks[("c_P", par3)] = ("dv", n_ql2)

            @block.scalar
            def _(sc):
                sc.wait_ge(ctx.sem("dv"), n_ql2)
                ctx.inc(sc.activation(t_sb[:, qf % 2, :], t_sb[:, qf % 2, :], AF.Exp), "ac")
                sc.wait_ge(ctx.sem("ac"), ctx.n("ac"))
                ctx.inc(sc.activation(q_bf[:, qf, s0:s0 + CH], t_sb[:, qf % 2, :], AF.Ln, bias=1.0), "ac")

        for qf in range(DT):
            emit_qgroup(qf)
            if qf >= 2:
                emit_qchain(qf - 2)
        emit_qchain(6)
        emit_qchain(7)
        n_qdone = ctx.n("ac")

        for st in range(4):
            par = st % 2
            for half in range(2):
                @block.tensor
                def _(t):
                    sem_name, val = ctx.marks.get(("c_P", par), ("pe", 0))
                    t.wait_ge(ctx.sem(sem_name), val)
                    sem_name, val = ctx.marks.get(("c_P", 2 + par), ("pe", 0))
                    t.wait_ge(ctx.sem(sem_name), val)
                    if st == 0 and half == 0:
                        t.wait_ge(ctx.sem("ac"), n_qdone)
                    for dt in range(DT):
                        t.matmul(P[par][:], xbf[:, dt, st * 128:(st + 1) * 128], wks[:, dt, half * 512:(half + 1) * 512], start=(dt == 0), stop=(dt == DT - 1))
                    for dt in range(DT):
                        ins = t.matmul(P[2 + par][:], xbf[:, dt, st * 128:(st + 1) * 128], wvs[:, dt, half * 512:(half + 1) * 512], start=(dt == 0), stop=(dt == DT - 1))
                    ctx.inc(ins, "pe")
                n_kvmm = ctx.n("pe")

                @block.scalar
                def _(sc):
                    sc.wait_ge(ctx.sem("pe"), n_kvmm)
                    ctx.inc(sc.square(sk_sb[:, half * 512:(half + 1) * 512], P[par][:]), "ac")
                n_sqk = ctx.n("ac")

                @block.vector
                def _(v):
                    v.wait_ge(ctx.sem("ac"), n_sqk)
                    ctx.inc(v.tensor_reduce(ssk[:, half * 8:(half + 1) * 8].rearrange("p (a b) -> p a b", b=1), sk_sb[:, half * 512:(half + 1) * 512].rearrange("p (a b) -> p a b", b=64), AX.X, ALU.add), "dv")
                n_ssk = ctx.n("dv")

                @block.vector
                def _(v):
                    v.wait_ge(ctx.sem("dv"), n_ssk)
                    ctx.inc(v.tensor_scalar_max(rk[:, half * 8:(half + 1) * 8], ssk[:, half * 8:(half + 1) * 8], 1e-24), "dv")
                n_rkmx = ctx.n("dv")

                @block.scalar
                def _(sc):
                    sc.wait_ge(ctx.sem("dv"), n_rkmx)
                    ctx.inc(sc.activation(rk[:, half * 8:(half + 1) * 8], rk[:, half * 8:(half + 1) * 8], AF.Ln), "ac")
                    sc.wait_ge(ctx.sem("ac"), ctx.n("ac"))
                    ctx.inc(sc.activation(rk[:, half * 8:(half + 1) * 8], rk[:, half * 8:(half + 1) * 8], AF.Exp, scale=-0.5), "ac")
                n_rk2 = ctx.n("ac")

                @block.vector
                def _(v):
                    v.wait_ge(ctx.sem("ac"), n_rk2)
                    ctx.inc(v.tensor_tensor(sk_sb[:, half * 512:(half + 1) * 512].rearrange("p (a b) -> p a b", b=64), P[par][:].rearrange("p (a b) -> p a b", b=64), rk[:, half * 8:(half + 1) * 8].rearrange("p (a b) -> p a b", b=1).broadcast_to([128, 8, 64]), ALU.mult), "dv")
                n_kn = ctx.n("dv")

                @block.scalar
                def _(sc):
                    sc.wait_ge(ctx.sem("dv"), n_kn)
                    ctx.inc(sc.activation(sk_sb[:, half * 512:(half + 1) * 512], sk_sb[:, half * 512:(half + 1) * 512], AF.Exp), "ac")
                    sc.wait_ge(ctx.sem("ac"), ctx.n("ac"))
                    ctx.inc(sc.activation(k_bf[:, par, half * 512:(half + 1) * 512], sk_sb[:, half * 512:(half + 1) * 512], AF.Ln, bias=1.0), "ac")

                @block.vector
                def _(v):
                    v.wait_ge(ctx.sem("pe"), n_kvmm)
                    ctx.inc(v.tensor_copy(v_bf[:, par, half * 512:(half + 1) * 512], P[2 + par][:]), "dv")
                ctx.marks[("c_P", 2 + par)] = ("dv", ctx.n("dv"))

            gst = c * 4 + st
            n_kv_ac = ctx.n("ac")
            n_kv_dv = ctx.n("dv")

            for hh in range(2):
                @block.tensor
                def _(t):
                    t.wait_ge(ctx.sem("ac"), n_kv_ac)
                    t.wait_ge(ctx.sem("dv"), n_kv_dv)
                    if gst > 0 or hh > 0:
                        t.wait_ge(ctx.sem("dv"), ctx.marks.get("c_kvacc", 0))
                    for h in range(hh * 8, hh * 8 + 8):
                        ins = t.matmul(psKV[0:64, (h - hh * 8) * 64:(h - hh * 8 + 1) * 64], k_bf[:, par, h * 64:(h + 1) * 64], v_bf[:, par, h * 64:(h + 1) * 64], start=True, stop=True)
                    if hh == 1:
                        for half in range(2):
                            ins = t.matmul(psKS[0:1, half * 512:(half + 1) * 512], ones128b[:], k_bf[:, par, half * 512:(half + 1) * 512], start=(gst == 0), stop=(gst == 4 * NCH - 1))
                    ctx.inc(ins, "pe")
                n_kvpe = ctx.n("pe")

                @block.vector
                def _(v):
                    v.wait_ge(ctx.sem("pe"), n_kvpe)
                    if gst == 0:
                        ctx.inc(v.tensor_copy(kvacc[:, hh * 512:(hh + 1) * 512], psKV[:]), "dv")
                    else:
                        ctx.inc(v.tensor_add(kvacc[:, hh * 512:(hh + 1) * 512], kvacc[:, hh * 512:(hh + 1) * 512], psKV[:]), "dv")
                ctx.marks["c_kvacc"] = ctx.n("dv")
    n_pass1_pe = ctx.n("pe")

    n_pass1_dv = ctx.n("dv")

    @block.vector
    def _(v):
        v.wait_ge(ctx.sem("pe"), n_pass1_pe)
        v.wait_ge(ctx.sem("dv"), n_pass1_dv)
        ctx.inc(v.tensor_scalar(kv_sb[:], kvacc[:], 0.125, None, ALU.mult), "dv")
        ctx.inc(v.tensor_scalar(ks_row[:], psKS[0:1, :], 1.0, 1e-6, ALU.mult, ALU.add), "dv")
    n_fin = ctx.n("dv")

    @block.sync
    def _(s):
        s.wait_ge(ctx.sem("dv"), n_fin)
        s.wait_ge(ctx.sem("ld"), ctx.n("ld"))
        ctx.inc(s.dma_start(out=kvscr.transpose([1, 2, 0, 3]), in_=kv_sb[:].rearrange("dh (j two c) -> dh j two c", two=2, c=64)), "ld", 16)
        s.wait_ge(ctx.sem("ld"), ctx.n("ld"))
        ctx.inc(s.dma_start(out=kv2_sb[:], in_=kvscr.rearrange("two dh j c -> (two dh) j c")), "ld", 16)
    n_kv2 = ctx.n("ld")

    @block.tensor
    def _(t):
        t.wait_ge(ctx.sem("dv"), n_fin)
        for t_i in range(8):
            ins = t.transpose(P[0][:, t_i:t_i + 1], ks_row[0:1, t_i * 128:(t_i + 1) * 128], ident_c[0:1, 0:1])
        ctx.inc(ins, "pe")
    n_ksfm_pe = ctx.n("pe")

    @block.vector
    def _(v):
        v.wait_ge(ctx.sem("pe"), n_ksfm_pe)
        ctx.inc(v.tensor_copy(ksfm[:], P[0][:, 0:8]), "dv")
    n_ksfm = ctx.n("dv")

    @block.vector
    def _(v):
        v.wait_ge(ctx.sem("dv"), n_ksfm)
        for t_i in range(8):
            ctx.inc(v.tensor_scalar(ks2[:, t_i, 0:1], ksfm[:, t_i:t_i + 1], maskA[:, 0:1], None, ALU.mult), "dv")
            ctx.inc(v.tensor_scalar(ks2[:, t_i, 1:2], ksfm[:, t_i:t_i + 1], maskB[:, 0:1], None, ALU.mult), "dv")
    n_ks2 = ctx.n("dv")

    st2_0 = ctx.n("st2")
    for c in range(NCH):
        s0 = c * CH

        @block.sync
        def _(s):
            s.wait_ge(ctx.sem("ld"), ctx.n("ld"))
            if c >= 1:
                s.wait_ge(ctx.sem("st2"), st2_0 + 16 * c)
            ctx.inc(s.dma_start(out=x_sb[:], in_=xin_r[:, :, s0:s0 + CH]), "ld", 16)
        n_xld = ctx.n("ld")

        dv_base_p2 = ctx.n("dv")
        for t_i in range(DT):
            @block.tensor
            def _(t):
                if c == 0 and t_i == 0:
                    t.wait_ge(ctx.sem("dv"), n_ks2)
                    t.wait_ge(ctx.sem("ld"), n_kv2)
                if t_i >= 1:
                    t.wait_ge(ctx.sem("dv"), dv_base_p2 + 3 * (t_i - 1) + 1)
                else:
                    ctx.wait(t, "dv")
                ctx.inc(t.matmul(psN[0:2, :], ks2[:, t_i, :], q_bf[:, t_i, s0:s0 + CH], start=True, stop=True), "pe")
            n_nmm = ctx.n("pe")

            @block.vector
            def _(v):
                v.wait_ge(ctx.sem("pe"), n_nmm)
                if t_i >= 1:
                    v.wait_ge(ctx.sem("dv"), dv_base_p2 + 3 * (t_i - 1) + 3)
                ctx.inc(v.tensor_copy(nrm2[:], psN[0:2, :]), "dv")
                v.wait_ge(ctx.sem("dv"), ctx.n("dv"))
                ctx.inc(v.reciprocal(nrm2[:], nrm2[:]), "dv")
                v.wait_ge(ctx.sem("dv"), ctx.n("dv"))
                ctx.inc(v.tensor_copy(rn_bf_all[:, t_i, :], nrm2[:]), "dv")

        for t_i in range(DT):
            par = t_i % 2

            @block.tensor
            def _(t):
                t.wait_ge(ctx.sem("dv"), dv_base_p2 + 3 * t_i + 3)
                if t_i >= 2:
                    t.wait_ge(ctx.sem("dv"), ctx.marks[("c2_qn", t_i - 2)])
                ctx.inc(t.matmul(P[2 + par][:], indT2b[:], rn_bf_all[:, t_i, :], start=True, stop=True), "pe")
            n_rb = ctx.n("pe")

            @block.vector
            def _(v):
                v.wait_ge(ctx.sem("pe"), n_rb)
                ctx.inc(v.tensor_mul(qn_all[:, t_i, :], q_bf[:, t_i, s0:s0 + CH], P[2 + par][:]), "dv")
            ctx.marks[("c2_qn", t_i)] = ctx.n("dv")

        for t_i in range(DT):
            par = t_i % 2

            @block.tensor
            def _(t):
                t.wait_ge(ctx.sem("dv"), ctx.marks[("c2_qn", t_i)])
                if t_i >= 2:
                    t.wait_ge(ctx.sem("dv"), ctx.marks[("c2_res", t_i - 2)])
                elif c > 0:
                    t.wait_ge(ctx.sem("dv"), ctx.marks.get(("c2_res_prev", t_i), 0))
                for h2 in range(2):
                    b0 = h2 * 64
                    ins = t.matmul(P[par][b0:b0 + 64, :], kv2_sb[b0:b0 + 64, t_i, :], qn_all[b0:b0 + 64, t_i, :], start=True, stop=True)
                ctx.inc(ins, "pe")
            n_omm = ctx.n("pe")

            @block.vector
            def _(v):
                v.wait_ge(ctx.sem("pe"), n_omm)
                v.wait_ge(ctx.sem("ld"), n_xld)
                ctx.inc(v.tensor_add(x_sb[:, t_i, :], x_sb[:, t_i, :], P[par][:]), "dv")
            ctx.marks[("c2_res", t_i)] = ctx.n("dv")
        for t_i in range(DT):
            ctx.marks[("c2_res_prev", t_i)] = ctx.marks[("c2_res", t_i)]
        n_resid = ctx.n("dv")

        emit_stats(ctx, block, x_sb, sq, P[0], P[1], ones128, None, DT,
                   extra_pe_wait=("dv", n_resid))
        emit_stats_final(ctx, block, P[0], P[1], mean_sb, var_sb, tmprow, 1024.0, 1e-5)
        emit_bcast2(ctx, block, mean_sb, tmprow, mb_sb, invb_sb, P[2], P[3], ones1)
        n_bc = ctx.n("dv")
        for dt in range(DT):
            @block.gpsimd
            def _(g):
                g.wait_ge(ctx.sem("dv"), n_bc)
                if dt >= 2:
                    ctx.wait(g, "dv")
                ctx.inc(g.tensor_sub(tmp1[:, dt % 2, :], x_sb[:, dt, :], mb_sb[:]), "pl")
            n_pl_sub = ctx.n("pl")

            @block.vector
            def _(v):
                v.wait_ge(ctx.sem("pl"), n_pl_sub)
                ctx.inc(v.tensor_mul(x_sb[:, dt, :], tmp1[:, dt % 2, :], invb_sb[:]), "dv")
        n_post = ctx.n("dv")

        @block.sync
        def _(s):
            s.wait_ge(ctx.sem("dv"), n_post)
            s.wait_ge(ctx.sem("st2"), ctx.n("st2"))
            ctx.inc(s.dma_start(out=xout_r[:, :, s0:s0 + CH], in_=x_sb[:]), "st2", 16)

    return ctx.n("st2")


def blockC_host_inputs(p):
    W_qkv = p["W_qkv"]
    indA = np.zeros((128, 2), np.float32)
    indA[0:64, 0] = 1.0
    indA[64:128, 1] = 1.0
    indT2 = np.ascontiguousarray(indA.T)
    return {
        "c_ident": np.eye(128, dtype=np.float32),
        "c_wqT": np.ascontiguousarray(W_qkv[0:1024].T),
        "c_wkT": np.ascontiguousarray(W_qkv[1024:2048].T),
        "c_wvT": np.ascontiguousarray(W_qkv[2048:3072].T),
        "c_indA": indA,
        "c_indT2": indT2,
        "c_maskA": indA[:, 0:1].copy(),
        "c_maskB": indA[:, 1:2].copy(),
    }


def declare_blockC_drams(nc, T):
    F = mybir.dt.float32
    shapes = {
        "c_ident": [128, 128],
        "c_wqT": [1024, 1024], "c_wkT": [1024, 1024], "c_wvT": [1024, 1024],
        "c_indA": [128, 2], "c_indT2": [2, 128], "c_maskA": [128, 1], "c_maskB": [128, 1],
    }
    return {k: nc.dram_tensor(k, v, F, kind="ExternalInput").ap() for k, v in shapes.items()}



def emit_blockD(ctx, block, T, xin, xout, W, acc_hbm, ar_in, ar_out, use_cc=True):
    nc = ctx.nc
    NCH = T // CH
    DT = 8
    FT = 16
    NE = 8
    CAP = 2048.0

    xin_r = xin.rearrange("(t p) s -> p t s", p=128)
    xout_r = xout.rearrange("(t p) s -> p t s", p=128)
    acc_r = acc_hbm.rearrange("(t p) s -> p t s", p=128)
    gw_r = W["d_Wg"].rearrange("(t p) e -> p t e", p=128)
    w1_r = W["d_W1"].rearrange("E (t p) f -> E p t f", p=128)
    w2_r = W["d_W2"].rearrange("E (t p) f -> E p t f", p=128)
    b1_r = W["d_b1fm"]
    b2_r = W["d_b2r"]

    xf_bf = ctx.sbuf("d_xf", [128, DT, T], BF16)
    gwt = ctx.sbuf("d_gwt_sb", [128, DT, 8], F32)
    tri = ctx.sbuf("d_tri_sb", [128, 128], F32)
    ident = ctx.sbuf("d_ident_sb", [128, 128], F32)
    sel = ctx.sbuf("d_sel_sb", [8, NE * 128], BF16)
    rowmask = ctx.sbuf("d_rowmask_sb", [1, 128], F32)
    thmask = ctx.sbuf("d_thmask_sb", [128, 24], F32)
    e_sb = ctx.sbuf("d_e", [128, 8], F32)
    nm1 = ctx.sbuf("d_nm1", [128, 1], F32)
    ssum = ctx.sbuf("d_ssum", [128, 1], F32)
    rsum = ctx.sbuf("d_rsum", [128, 1], F32)
    probs = ctx.sbuf("d_probs", [128, 8], F32)
    m1p = ctx.sbuf("d_m1p", [128, 1], F32)
    tnot = ctx.sbuf("d_tnot", [128, 8], F32)
    probs2 = ctx.sbuf("d_probs2", [128, 8], F32)
    m2p = ctx.sbuf("d_m2p", [128, 1], F32)
    den = ctx.sbuf("d_den", [128, 1], F32)
    rden = ctx.sbuf("d_rden", [128, 1], F32)
    oh0_sb = ctx.sbuf("d_oh0", [128, 32, 8], F32)
    oh1_sb = ctx.sbuf("d_oh1", [128, 32, 8], F32)
    cum0_sb = ctx.sbuf("d_cum0", [128, 32, 8], F32)
    cum1_sb = ctx.sbuf("d_cum1", [128, 32, 8], F32)
    w0_sb = ctx.sbuf("d_w0", [128, 32], F32)
    w1c_sb = ctx.sbuf("d_w1c", [128, 32], F32)
    off0 = ctx.sbuf("d_off0", [1, 8], F32)
    off1 = ctx.sbuf("d_off1", [1, 8], F32)
    n01 = ctx.sbuf("d_n01", [1, 16], F32)
    contrib = ctx.sbuf("d_contrib", [1, 128], F32)
    allcnt = ctx.sbuf("d_allcnt", [1, 128], F32)
    accol = ctx.sbuf("d_accol", [128, 1], F32)
    one11 = ctx.sbuf("d_one11", [1, 1], F32)
    th_row = ctx.sbuf("d_throw", [1, 24], F32)
    th0 = ctx.sbuf("d_th0", [1, 8], F32)
    th1 = ctx.sbuf("d_th1", [1, 8], F32)
    t0k = ctx.sbuf("d_t0k", [1, 8], F32)
    th0b = ctx.sbuf("d_th0b", [128, 8], F32)
    th1b = ctx.sbuf("d_th1b", [128, 8], F32)
    keep = ctx.sbuf("d_keep", [128, 8], F32)
    keep1 = ctx.sbuf("d_keep1", [128, 8], F32)
    comb = ctx.sbuf("d_comb", [128, 8], F32)
    combT = ctx.sbuf("d_combT", [8, T], BF16)
    w1hs = ctx.sbuf("d_w1hs", [128, DT, 2048], BF16)
    w2hs = ctx.sbuf("d_w2hs", [128, FT, 1024], BF16)
    b1s = ctx.sbuf("d_b1s", [128, 2, 32], F32)
    b2s = ctx.sbuf("d_b2s", [1, 2, 1024], BF16)
    onesrow_bf = ctx.sbuf("d_onesrow", [1, CH], BF16)
    h_bf = ctx.sbuf("d_h", [128, FT, CH], BF16)
    x3_sb = h_bf[:].bitcast(F32).rearrange("p a b -> p (a b)").rearrange("p (t s) -> p t s", t=DT)
    cb_sb = ctx.sbuf("d_cb", [128, CH], F32)
    t_sb = ctx.sbuf("d_t", [128, DT, CH], F32)
    ones128 = ctx.sbuf("d_ones128", [128, 1], F32)
    ones1 = ctx.sbuf("d_ones1", [1, 128], F32)
    sq = w1hs[:, 0:2, 0:1024].bitcast(F32).rearrange("p a (x b) -> p (a x) b", b=CH)[:, 0:2, :]
    tmp1 = w1hs[:, 2:4, 0:1024].bitcast(F32).rearrange("p a (x b) -> p (a x) b", b=CH)[:, 0:2, :]
    mb_sb = w1hs[:, 4, 0:1024].bitcast(F32)[:, 0:CH]
    invb_sb = w1hs[:, 5, 0:1024].bitcast(F32)[:, 0:CH]
    mean_sb = w2hs[0:1, 0, :].bitcast(F32)[:, 0:CH]
    var_sb = w2hs[0:1, 1, :].bitcast(F32)[:, 0:CH]
    tmprow = w2hs[0:1, 2, :].bitcast(F32)[:, 0:CH]

    psA = [ctx.psum("d_psA0", [128, CH]), ctx.psum("d_psA1", [128, CH]),
           ctx.psum("d_psA2", [128, CH]), ctx.psum("d_psA3", [128, CH])]
    psO = [ctx.psum("d_psO0", [128, CH]), ctx.psum("d_psO1", [128, CH])]
    psB = ctx.psum("d_psB", [128, CH])
    psG = ctx.psum("d_psG", [128, 512])

    @block.sync
    def _(s):
        s.wait_ge(ctx.sem("ld"), ctx.n("ld"))
        ctx.inc(s.dma_start(out=gwt[:], in_=gw_r[:]), "ld", 16)
        s.wait_ge(ctx.sem("ld"), ctx.n("ld"))
        ctx.inc(s.dma_start(out=tri[:], in_=W["d_tri"]), "ld", 16)
        s.wait_ge(ctx.sem("ld"), ctx.n("ld"))
        ctx.inc(s.dma_start(out=ident[:], in_=W["d_ident"]), "ld", 16)
        s.wait_ge(ctx.sem("ld"), ctx.n("ld"))
        ctx.inc(s.dma_start(out=rowmask[:], in_=W["d_rowmask"]), "ld", 16)
        s.wait_ge(ctx.sem("ld"), ctx.n("ld"))
        ctx.inc(s.dma_start(out=thmask[:], in_=W["d_thmask"]), "ld", 16)
    n_param_ld = ctx.n("ld")

    @block.gpsimd
    def _(g):
        g.wait_ge(ctx.sem("wq"), ctx.n("wq"))
        ctx.inc(g.dma_start(out=sel[:], in_=W["d_sel"]), "wq", 16)
        ctx.inc(g.memset(ones128[:], 1.0), "pl")
        ctx.inc(g.memset(ones1[:], 1.0), "pl")
        ctx.inc(g.memset(one11[:], 1.0), "pl")
        ctx.inc(g.memset(onesrow_bf[:], 1.0), "pl")
        ctx.inc(g.memset(off0[:], 0.0), "pl")
        ctx.inc(g.memset(off1[:], 0.0), "pl")
    n_init_pl = ctx.n("pl")
    n_init_wq = ctx.n("wq")

    xg = [x3_sb, w2hs[:, 0:8, :].bitcast(F32)]
    gmm_mark = {}
    exp_mark = {}
    probs_mark = {}
    top2_mark = {}
    ccopy_mark = {}

    def emit_cum_d(j):
        @block.tensor
        def _(t):
            t.wait_ge(ctx.sem("dv"), top2_mark[j])
            t.matmul(psB[:, 8:16], tri[:], oh0_sb[:, j, :], start=True, stop=False)
            ctx.inc(t.matmul(psB[:, 8:16], ones1[:], off0[:], start=False, stop=True), "pe")
            t.matmul(psB[:, 16:24], tri[:], oh1_sb[:, j, :], start=True, stop=False)
            ctx.inc(t.matmul(psB[:, 16:24], ones1[:], off1[:], start=False, stop=True), "pe")
            ctx.inc(t.matmul(psB[0:1, 24:32], ones128[:], oh0_sb[:, j, :], start=True, stop=True), "pe")
            ctx.inc(t.matmul(psB[0:1, 32:40], ones128[:], oh1_sb[:, j, :], start=True, stop=True), "pe")
        n_cumj = ctx.n("pe")

        @block.vector
        def _(v):
            v.wait_ge(ctx.sem("pe"), n_cumj)
            ctx.inc(v.tensor_copy(cum0_sb[:, j, :], psB[:, 8:16]), "dv")
            ctx.inc(v.tensor_copy(cum1_sb[:, j, :], psB[:, 16:24]), "dv")
            v.wait_ge(ctx.sem("dv"), ctx.n("dv"))
            ctx.inc(v.tensor_add(off0[:], off0[:], psB[0:1, 24:32]), "dv")
            ctx.inc(v.tensor_add(off1[:], off1[:], psB[0:1, 32:40]), "dv")
        ccopy_mark[j] = ctx.n("dv")

    for c in range(NCH):
        s0 = c * CH
        xcur = xg[c % 2]

        @block.sync
        def _(s):
            s.wait_ge(ctx.sem("ld"), ctx.n("ld"))
            if c >= 2:
                s.wait_ge(ctx.sem("pe"), gmm_mark[(c - 2) * 4 + 3])
            ctx.inc(s.dma_start(out=xcur[:], in_=xin_r[:, :, s0:s0 + CH]), "ld", 16)
        n_xld = ctx.n("ld")

        @block.gpsimd
        def _(g):
            g.wait_ge(ctx.sem("wq"), ctx.n("wq"))
            ctx.inc(g.dma_start(out=xf_bf[:, :, s0:s0 + CH], in_=xin_r[:, :, s0:s0 + CH]), "wq", 16)
        n_xfld = ctx.n("wq")

        for st in range(4):
            sti = c * 4 + st

            @block.tensor
            def _(t):
                t.wait_ge(ctx.sem("ld"), n_xld)
                if sti == 0:
                    t.wait_ge(ctx.sem("pl"), n_init_pl)
                    t.wait_ge(ctx.sem("ld"), n_param_ld)
                else:
                    t.wait_ge(ctx.sem("ac"), exp_mark[sti - 1])
                for dt in range(DT):
                    ins = t.matmul(psG[:, 0:8], xcur[:, dt, st * 128:(st + 1) * 128], gwt[:, dt, :], start=(dt == 0), stop=(dt == DT - 1))
                ctx.inc(ins, "pe")
            gmm_mark[sti] = ctx.n("pe")
            n_gmm = gmm_mark[sti]

            @block.vector
            def _(v):
                v.wait_ge(ctx.sem("pe"), n_gmm)
                ctx.inc(v.tensor_reduce(nm1[:].rearrange("p (a b) -> p a b", b=1), psG[:, 0:8].rearrange("p (a b) -> p a b", b=8), AX.X, ALU.max, negate=True), "dv")
            n_nm1 = ctx.n("dv")

            @block.scalar
            def _(sc):
                sc.wait_ge(ctx.sem("dv"), n_nm1)
                if sti >= 1:
                    sc.wait_ge(ctx.sem("dv"), probs_mark[sti - 1])
                ctx.inc(sc.activation(e_sb[:], psG[:, 0:8], AF.Exp, bias=nm1[:, 0:1]), "ac")
            exp_mark[sti] = ctx.n("ac")
            n_exp = exp_mark[sti]

            if sti >= 1:
                emit_cum_d(sti - 1)

            @block.vector
            def _(v):
                v.wait_ge(ctx.sem("ac"), n_exp)
                ctx.inc(v.tensor_reduce(ssum[:].rearrange("p (a b) -> p a b", b=1), e_sb[:].rearrange("p (a b) -> p a b", b=8), AX.X, ALU.add), "dv")
                v.wait_ge(ctx.sem("dv"), ctx.n("dv"))
                ctx.inc(v.reciprocal(rsum[:], ssum[:]), "dv")
                v.wait_ge(ctx.sem("dv"), ctx.n("dv"))
                ctx.inc(v.tensor_scalar(probs[:], e_sb[:], rsum[:, 0:1], None, ALU.mult), "dv")
                probs_mark[sti] = ctx.n("dv")
                v.wait_ge(ctx.sem("dv"), ctx.n("dv"))
                ctx.inc(v.tensor_reduce(m1p[:].rearrange("p (a b) -> p a b", b=1), probs[:].rearrange("p (a b) -> p a b", b=8), AX.X, ALU.max), "dv")
                v.wait_ge(ctx.sem("dv"), ctx.n("dv"))
                ctx.inc(v.tensor_scalar(oh0_sb[:, sti, :], probs[:], m1p[:, 0:1], None, ALU.is_ge), "dv")
                v.wait_ge(ctx.sem("dv"), ctx.n("dv"))
                ctx.inc(v.tensor_scalar(tnot[:], oh0_sb[:, sti, :], -1.0, 1.0, ALU.mult, ALU.add), "dv")
                v.wait_ge(ctx.sem("dv"), ctx.n("dv"))
                ctx.inc(v.tensor_mul(probs2[:], probs[:], tnot[:]), "dv")
                v.wait_ge(ctx.sem("dv"), ctx.n("dv"))
                ctx.inc(v.tensor_reduce(m2p[:].rearrange("p (a b) -> p a b", b=1), probs2[:].rearrange("p (a b) -> p a b", b=8), AX.X, ALU.max), "dv")
                v.wait_ge(ctx.sem("dv"), ctx.n("dv"))
                ctx.inc(v.tensor_scalar(oh1_sb[:, sti, :], probs2[:], m2p[:, 0:1], None, ALU.is_ge), "dv")
                v.wait_ge(ctx.sem("dv"), ctx.n("dv"))
                ctx.inc(v.tensor_scalar(den[:], m1p[:], 1.0, 1e-9, ALU.mult, ALU.add), "dv")
                v.wait_ge(ctx.sem("dv"), ctx.n("dv"))
                ctx.inc(v.tensor_add(den[:], den[:], m2p[:]), "dv")
                v.wait_ge(ctx.sem("dv"), ctx.n("dv"))
                ctx.inc(v.reciprocal(rden[:], den[:]), "dv")
                v.wait_ge(ctx.sem("dv"), ctx.n("dv"))
                ctx.inc(v.tensor_mul(w0_sb[:, sti:sti + 1], m1p[:], rden[:]), "dv")
                ctx.inc(v.tensor_mul(w1c_sb[:, sti:sti + 1], m2p[:], rden[:]), "dv")
            top2_mark[sti] = ctx.n("dv")
    emit_cum_d(4 * NCH - 1)
    n_gate_dv = ctx.n("dv")
    n_gate_wq = ctx.n("wq")

    @block.vector
    def _(v):
        v.wait_ge(ctx.sem("dv"), n_gate_dv)
        ctx.inc(v.tensor_copy(n01[:, 0:8], off0[:]), "dv")
        ctx.inc(v.tensor_copy(n01[:, 8:16], off1[:]), "dv")
        v.wait_ge(ctx.sem("dv"), ctx.n("dv"))
        ctx.inc(v.tensor_tensor(contrib[:].rearrange("o (a b) -> o a b", b=16), rowmask[:].rearrange("o (a b) -> o a b", b=16), n01[:].unsqueeze(1).broadcast_to([1, 8, 16]), ALU.mult), "dv")
    n_contrib = ctx.n("dv")

    @block.sync
    def _(s):
        s.wait_ge(ctx.sem("dv"), n_contrib)
        s.wait_ge(ctx.sem("ld"), ctx.n("ld"))
        ctx.inc(s.dma_start(out=ar_in[:], in_=contrib[:]), "ld", 16)
    n_arin = ctx.n("ld")

    @block.gpsimd
    def _(g):
        g.wait_ge(ctx.sem("ld"), n_arin)
        g.wait_ge(ctx.sem("wq"), ctx.n("wq"))
        if use_cc:
            ctx.inc(g.collective_compute("AllReduce", ALU.add, [[0, 1, 2, 3, 4, 5, 6, 7]], [ar_in[:]], [ar_out[:]]), "cc", 1)
            g.wait_ge(ctx.sem("cc"), ctx.n("cc"))
        else:
            ctx.inc(g.dma_start(out=ar_out[:], in_=ar_in[:]), "wq", 16)
            g.wait_ge(ctx.sem("wq"), ctx.n("wq"))
        ctx.inc(g.dma_start(out=allcnt[:], in_=ar_out[:]), "wq", 16)
    n_allcnt = ctx.n("wq")

    @block.tensor
    def _(t):
        t.wait_ge(ctx.sem("wq"), n_allcnt)
        ctx.wait(t, "dv")
        ctx.inc(t.matmul(psG[:, 40:41], allcnt[:], one11[:], start=True, stop=True), "pe")
    n_acol = ctx.n("pe")

    @block.vector
    def _(v):
        v.wait_ge(ctx.sem("pe"), n_acol)
        ctx.inc(v.tensor_copy(accol[:], psG[:, 40:41]), "dv")
    n_accol = ctx.n("dv")

    @block.tensor
    def _(t):
        t.wait_ge(ctx.sem("dv"), n_accol)
        ctx.inc(t.matmul(psG[0:1, 48:72], accol[:], thmask[:], start=True, stop=True), "pe")
    n_th = ctx.n("pe")

    @block.vector
    def _(v):
        v.wait_ge(ctx.sem("pe"), n_th)
        ctx.inc(v.tensor_copy(th_row[:], psG[0:1, 48:72]), "dv")
        v.wait_ge(ctx.sem("dv"), ctx.n("dv"))
        ctx.inc(v.tensor_scalar(th0[:], th_row[:, 0:8], -1.0, CAP, ALU.mult, ALU.add), "dv")
        ctx.inc(v.tensor_scalar_min(t0k[:], th_row[:, 8:16], CAP), "dv")
        v.wait_ge(ctx.sem("dv"), ctx.n("dv"))
        ctx.inc(v.tensor_scalar(th1[:], th_row[:, 16:24], -1.0, CAP, ALU.mult, ALU.add), "dv")
        v.wait_ge(ctx.sem("dv"), ctx.n("dv"))
        ctx.inc(v.tensor_sub(th1[:], th1[:], t0k[:]), "dv")
    n_thf = ctx.n("dv")

    @block.tensor
    def _(t):
        t.wait_ge(ctx.sem("dv"), n_thf)
        t.matmul(psG[:, 72:80], ones1[:], th0[:], start=True, stop=True)
        ctx.inc(t.matmul(psG[:, 80:88], ones1[:], th1[:], start=True, stop=True), "pe")
    n_thb = ctx.n("pe")

    @block.vector
    def _(v):
        v.wait_ge(ctx.sem("pe"), n_thb)
        ctx.inc(v.tensor_copy(th0b[:], psG[:, 72:80]), "dv")
        ctx.inc(v.tensor_copy(th1b[:], psG[:, 80:88]), "dv")
    n_thbc = ctx.n("dv")

    for sti in range(4 * NCH):
        @block.vector
        def _(v):
            if sti == 0:
                v.wait_ge(ctx.sem("dv"), n_thbc)
            ctx.inc(v.tensor_tensor(keep[:], cum0_sb[:, sti, :], th0b[:], ALU.is_le), "dv")
            ctx.inc(v.tensor_tensor(keep1[:], cum1_sb[:, sti, :], th1b[:], ALU.is_le), "dv")
            v.wait_ge(ctx.sem("dv"), ctx.n("dv"))
            ctx.inc(v.tensor_mul(keep[:], keep[:], oh0_sb[:, sti, :]), "dv")
            ctx.inc(v.tensor_mul(keep1[:], keep1[:], oh1_sb[:, sti, :]), "dv")
            v.wait_ge(ctx.sem("dv"), ctx.n("dv"))
            ctx.inc(v.tensor_scalar(comb[:], keep[:], w0_sb[:, sti:sti + 1], None, ALU.mult), "dv")
            v.wait_ge(ctx.sem("dv"), ctx.n("dv"))
            ctx.inc(v.scalar_tensor_tensor(comb[:], keep1[:], w1c_sb[:, sti:sti + 1], comb[:], ALU.mult, ALU.add), "dv")
        n_comb = ctx.n("dv")

        @block.tensor
        def _(t):
            t.wait_ge(ctx.sem("dv"), n_comb)
            ctx.inc(t.transpose(psG[0:8, 128:256], comb[:], ident[:]), "pe")
        n_tp = ctx.n("pe")

        @block.vector
        def _(v):
            v.wait_ge(ctx.sem("pe"), n_tp)
            ctx.inc(v.tensor_copy(combT[:, sti * 128:(sti + 1) * 128], psG[0:8, 128:256]), "dv")
    n_combT = ctx.n("dv")

    w1h_free = 0
    w2h_free = 0
    evacA_d = [("pe", 0), ("pe", 0), ("pe", 0), ("pe", 0)]
    w2rd_d = 0
    tsb_rd_d = 0

    wqD1_0 = ctx.n("wqD1")
    wqD2_0 = ctx.n("wqD2")
    wqD3_0 = ctx.n("wqD3")

    @block.gpsimd
    def _(g):
        ctx.inc(g.dma_start(out=w1hs[:], in_=w1_r[0, :, :, 0:2048]), "wqD1", 16)
        ctx.inc(g.dma_start(out=w2hs[:], in_=w2_r[0, :, 0:16, :]), "wqD2", 16)
        ctx.inc(g.dma_start(out=b1s[:, 0, :], in_=b1_r[0]), "wqD3", 16)
        ctx.inc(g.dma_start(out=b2s[0:1, 0, :], in_=b2_r[0]), "wqD3", 16)

    for e in range(NE):
        for fh in range(2):
            q = e * 2 + fh
            n_w1l = wqD1_0 + 16 * (q + 1)
            n_w2l = wqD2_0 + 16 * (q + 1)
            n_b12 = wqD3_0 + 32 * (e + 1)

            for c in range(NCH):
                s0 = c * CH
                ac_base = ctx.n("ac")
                for f in range(FT):
                    par = f % 4

                    @block.tensor
                    def _(t):
                        if f == 0 and c == 0:
                            t.wait_ge(ctx.sem("wqD1"), n_w1l)
                            if fh == 0:
                                t.wait_ge(ctx.sem("wqD3"), n_b12)
                        if f >= 4:
                            t.wait_ge(ctx.sem("ac"), ac_base + (f - 4) + 1)
                        elif c > 0 or q > 0:
                            sem_name, val = evacA_d[par]
                            t.wait_ge(ctx.sem(sem_name), val)
                        for dt in range(DT):
                            ins = t.matmul(psA[par][:], w1hs[:, dt, f * 128:(f + 1) * 128], xf_bf[:, dt, s0:s0 + CH], start=(dt == 0), stop=(dt == DT - 1))
                        ctx.inc(ins, "pe")
                    ctx.marks[("d_hmm", f)] = ctx.n("pe")
                n_h_pe = ctx.n("pe")

                @block.scalar
                def _(sc):
                    for f in range(FT):
                        sc.wait_ge(ctx.sem("pe"), ctx.marks[("d_hmm", f)])
                        if c > 0 or q > 0:
                            sc.wait_ge(ctx.sem("pe"), w2rd_d)
                        ctx.inc(sc.activation(h_bf[:, f, :], psA[f % 4][:], AF.Gelu, bias=b1s[:, e % 2, fh * 16 + f:fh * 16 + f + 1]), "ac")
                        if f >= FT - 4:
                            evacA_d[f % 4] = ("ac", ctx.n("ac"))
                n_h = ctx.n("ac")

                if c == NCH - 1 and q < 2 * NE - 1:
                    e2, fh2 = divmod(q + 1, 2)

                    @block.gpsimd
                    def _(g):
                        g.wait_ge(ctx.sem("pe"), n_h_pe)
                        ctx.inc(g.dma_start(out=w1hs[:], in_=w1_r[e2, :, :, fh2 * 2048:(fh2 + 1) * 2048]), "wqD1", 16)
                        if fh2 == 0:
                            ctx.inc(g.dma_start(out=b1s[:, e2 % 2, :], in_=b1_r[e2]), "wqD3", 16)
                            ctx.inc(g.dma_start(out=b2s[0:1, e2 % 2, :], in_=b2_r[e2]), "wqD3", 16)

                @block.tensor
                def _(t):
                    t.wait_ge(ctx.sem("dv"), n_combT)
                    t.wait_ge(ctx.sem("dv"), ctx.marks.get("d_cb_rd", 0))
                    ctx.inc(t.matmul(psB[:], sel[:, e * 128:(e + 1) * 128], combT[:, s0:s0 + CH], start=True, stop=True), "pe")
                n_cb = ctx.n("pe")

                @block.vector
                def _(v):
                    v.wait_ge(ctx.sem("pe"), n_cb)
                    ctx.inc(v.tensor_copy(cb_sb[:], psB[:]), "dv")
                n_cbc = ctx.n("dv")

                for dt in range(DT):
                    par = dt % 2

                    @block.tensor
                    def _(t):
                        if dt == 0 and c == 0:
                            t.wait_ge(ctx.sem("wqD2"), n_w2l)
                        if dt == 0:
                            t.wait_ge(ctx.sem("ac"), n_h)
                        if dt >= 2:
                            t.wait_ge(ctx.sem("dv"), ctx.marks[("d_scale", dt - 2)])
                        elif c > 0 or q > 0:
                            t.wait_ge(ctx.sem("dv"), ctx.marks[("d_scale_prev", dt)])
                        for f in range(FT):
                            last = (f == FT - 1) and (fh == 0)
                            ins = t.matmul(psO[par][:], w2hs[:, f, dt * 128:(dt + 1) * 128], h_bf[:, f, :], start=(f == 0), stop=last)
                        if fh == 1:
                            ins = t.matmul(psO[par][:], b2s[0:1, e % 2, dt * 128:(dt + 1) * 128], onesrow_bf[:], start=False, stop=True)
                        ctx.inc(ins, "pe")
                    n_mm2 = ctx.n("pe")

                    @block.vector
                    def _(v):
                        v.wait_ge(ctx.sem("pe"), n_mm2)
                        v.wait_ge(ctx.sem("dv"), n_cbc)
                        if q > 0 or c > 0:
                            v.wait_ge(ctx.sem("st"), tsb_rd_d)
                        ctx.inc(v.tensor_mul(t_sb[:, dt, :], psO[par][:], cb_sb[:]), "dv")
                    ctx.marks[("d_scale", dt)] = ctx.n("dv")
                for dt in range(DT):
                    ctx.marks[("d_scale_prev", dt)] = ctx.marks[("d_scale", dt)]
                w2rd_d = ctx.n("pe")
                ctx.marks["d_cb_rd"] = ctx.n("dv")
                n_scale = ctx.n("dv")

                @block.gpsimd
                def _(g):
                    g.wait_ge(ctx.sem("dv"), n_scale)
                    g.wait_ge(ctx.sem("st"), ctx.n("st"))
                    op = ALU.bypass if (e == 0 and fh == 0) else ALU.add
                    ctx.inc(g.dma_start(out=acc_r[:, :, s0:s0 + CH], in_=t_sb[:], accum_op=op), "st", 16)
                tsb_rd_d = ctx.n("st")

            if q < 2 * NE - 1:
                e2, fh2 = divmod(q + 1, 2)

                @block.gpsimd
                def _(g):
                    g.wait_ge(ctx.sem("pe"), w2rd_d)
                    ctx.inc(g.dma_start(out=w2hs[:], in_=w2_r[e2, :, fh2 * 16:(fh2 + 1) * 16, :]), "wqD2", 16)
    n_acc_done = ctx.n("st")


    xb = [x3_sb, w1hs[:, 0:4, :].bitcast(F32).rearrange("p a (x b) -> p (a x) b", b=CH)]
    tb = [t_sb, w1hs[:, 4:8, :].bitcast(F32).rearrange("p a (x b) -> p (a x) b", b=CH)]
    sq2 = [w2hs[:, 0:2, :].bitcast(F32).rearrange("p a (x b) -> p (a x) b", b=CH)[:, 0:2, :],
           w2hs[:, 2:4, :].bitcast(F32).rearrange("p a (x b) -> p (a x) b", b=CH)[:, 0:2, :]]
    tmp2 = [w2hs[:, 4:6, :].bitcast(F32).rearrange("p a (x b) -> p (a x) b", b=CH)[:, 0:2, :],
            w2hs[:, 6:8, :].bitcast(F32).rearrange("p a (x b) -> p (a x) b", b=CH)[:, 0:2, :]]
    mb2 = [w2hs[:, 8, :].bitcast(F32), w2hs[:, 9, :].bitcast(F32)]
    ivb2 = [w2hs[:, 10, :].bitcast(F32), w2hs[:, 11, :].bitcast(F32)]
    mrow2 = [w2hs[0:1, 12, :].bitcast(F32), w2hs[0:1, 13, :].bitcast(F32)]
    vrow2 = [w2hs[0:1, 14, :].bitcast(F32), w2hs[0:1, 15, :].bitcast(F32)]
    trow2 = [xf_bf[0:1, 0, 0:1024].bitcast(F32), xf_bf[0:1, 1, 0:1024].bitcast(F32)]
    emit_postln_pipelined(ctx, block, T, xin_r, xout_r, acc_r, n_acc_done, "Dp",
                          xb, tb, sq2, tmp2, mb2, ivb2, mrow2, vrow2, trow2,
                          [psA[0], psA[1], psO[0], psO[1]], ones128, ones1)
    return ctx.n("st2")


def blockD_host_inputs(p, core_id):
    sel = np.zeros((8, 8 * 128), np.float32)
    for e in range(8):
        sel[e, e * 128:(e + 1) * 128] = 1.0
    rowmask = np.zeros((1, 128), np.float32)
    rowmask[0, core_id * 16:(core_id + 1) * 16] = 1.0
    thmask = np.zeros((128, 24), np.float32)
    for cp in range(8):
        for kp in range(2):
            for ep in range(8):
                pidx = cp * 16 + kp * 8 + ep
                if kp == 0 and cp < core_id:
                    thmask[pidx, ep] = 1.0
                if kp == 0:
                    thmask[pidx, 8 + ep] = 1.0
                if kp == 1 and cp < core_id:
                    thmask[pidx, 16 + ep] = 1.0
    b1fm = np.stack([p["b1"][e].reshape(32, 128).T for e in range(8)])
    return {
        "d_Wg": np.ascontiguousarray(p["Wg"]),
        "d_W1": np.ascontiguousarray(p["W1"]),
        "d_W2": np.ascontiguousarray(p["W2"]),
        "d_b1fm": np.ascontiguousarray(b1fm),
        "d_b2r": np.ascontiguousarray(p["b2"][:, None, :]),
        "d_tri": np.tril(np.ones((128, 128), np.float32)).T.copy(),
        "d_ident": np.eye(128, dtype=np.float32),
        "d_sel": sel,
        "d_rowmask": rowmask,
        "d_thmask": thmask,
    }


def declare_blockD_drams(nc, T):
    F = mybir.dt.float32
    shapes = {
        "d_Wg": [1024, 8], "d_W1": [8, 1024, 4096], "d_W2": [8, 4096, 1024],
        "d_b1fm": [8, 128, 32], "d_b2r": [8, 1, 1024],
        "d_tri": [128, 128], "d_ident": [128, 128], "d_sel": [8, 8 * 128],
        "d_rowmask": [1, 128], "d_thmask": [128, 24],
    }
    return {k: nc.dram_tensor(k, v, F, kind="ExternalInput").ap() for k, v in shapes.items()}



def build_fused(T=4096, use_cc=True):
    nc = bass.Bass("TRN2", target_bir_lowering=False, debug=False)
    F = F32
    xin = nc.dram_tensor("xin", [1024, T], F, kind="ExternalInput").ap()
    xout = nc.dram_tensor("xout", [1024, T], F, kind="ExternalOutput").ap()
    x1buf = nc.dram_tensor("x1buf", [1024, T], F, kind="Internal").ap()
    x2buf = nc.dram_tensor("x2buf", [1024, T], F, kind="Internal").ap()
    x3buf = nc.dram_tensor("x3buf", [1024, T], F, kind="Internal").ap()
    accbuf = nc.dram_tensor("accbuf", [1024, T], F, kind="Internal").ap()
    kvscr = nc.dram_tensor("kvscr", [2, 64, 8, 64], BF16, kind="Internal").ap()
    ksscr = nc.dram_tensor("ksscr", [8, 2, 64], F, kind="Internal").ap()
    ar_in = nc.dram_tensor("ar_in", [1, 128], F, kind="Internal").ap()
    ar_out = nc.dram_tensor("ar_out", [1, 128], F, kind="Internal").ap()
    WA = declare_blockA_drams(nc, T)
    WB = declare_blockB_drams(nc, T)
    WC = declare_blockC_drams(nc, T)
    WD = declare_blockD_drams(nc, T)

    outer = contextlib.ExitStack()
    with outer:
        ctx = Ctx(nc, outer)
        with nc.Block() as block:
            with contextlib.ExitStack() as es_a:
                ctx.es = es_a
                nA = emit_blockA(ctx, block, T, xin, x1buf, WA)
            emit_barrier(ctx, block, "st2", nA)
            with contextlib.ExitStack() as es_b:
                ctx.es = es_b
                nB = emit_blockB(ctx, block, T, x1buf, x2buf, WB, accbuf)
            emit_barrier(ctx, block, "st2", nB)
            with contextlib.ExitStack() as es_c:
                ctx.es = es_c
                nC = emit_blockC(ctx, block, T, x2buf, x3buf, WC, kvscr, ksscr)
            emit_barrier(ctx, block, "st2", nC)
            with contextlib.ExitStack() as es_d:
                ctx.es = es_d
                nD = emit_blockD(ctx, block, T, x3buf, xout, WD, accbuf, ar_in, ar_out, use_cc=use_cc)

            @block.sync
            def _(s):
                s.wait_ge(ctx.sem("st2"), nD)
    return nc


import concourse.bass_utils as _bu


def kernel(**inputs):
    T = 4096
    NCORES = 8

    x_full = np.asarray(inputs["L0_attn_x"], dtype=np.float32)

    pA = {k: np.asarray(inputs["L0_attn_" + k], dtype=np.float32) for k in
          ["ln_g", "ln_b", "W_in", "b_in", "conv_w", "conv_b", "A", "Bm",
           "W_c", "b_c", "W_d", "b_d", "log_delta", "W_out", "b_out"]}
    pB = {"gate_w": np.asarray(inputs["L0_ffn_gate_w"], dtype=np.float32),
          "w1": np.asarray(inputs["L0_ffn_w1"], dtype=np.float32),
          "w2": np.asarray(inputs["L0_ffn_w2"], dtype=np.float32),
          "w3": np.asarray(inputs["L0_ffn_w3"], dtype=np.float32)}
    pC = {"W_qkv": np.asarray(inputs["L1_attn_W_qkv"], dtype=np.float32)}
    pD = {"Wg": np.asarray(inputs["L1_ffn_Wg"], dtype=np.float32),
          "W1": np.asarray(inputs["L1_ffn_W1"], dtype=np.float32),
          "b1": np.asarray(inputs["L1_ffn_b1"], dtype=np.float32),
          "W2": np.asarray(inputs["L1_ffn_W2"], dtype=np.float32),
          "b2": np.asarray(inputs["L1_ffn_b2"], dtype=np.float32)}

    WA = blockA_host_inputs(pA)
    WB = blockB_host_inputs(pB)
    WC = blockC_host_inputs(pC)

    nc = build_fused(T=T, use_cc=True)

    maps = []
    for c in range(NCORES):
        m = {"xin": np.ascontiguousarray(x_full[c].T)}
        m.update(WA)
        m.update(WB)
        m.update(WC)
        m.update(blockD_host_inputs(pD, c))
        maps.append(m)

    res = _bu.run_bass_kernel_spmd(nc, maps, core_ids=list(range(NCORES)))
    out = np.stack([np.asarray(res.results[c]["xout"]).T for c in range(NCORES)], axis=0).astype(np.float32)
    return out
